# version 03
# baseline (speedup 1.0000x reference)
import contextlib
import math

import numpy as np
import ml_dtypes

import concourse.bass as bass
import concourse.mybir as mybir
from concourse.bass_utils import run_bass_kernel_spmd

F32 = mybir.dt.float32
BF16 = mybir.dt.bfloat16
AF = mybir.ActivationFunctionType
ALU = mybir.AluOpType
AX = mybir.AxisListType

H = 4096
S = 2048
B = 4
NT = 12
NU = 8
HALO = 512
NQH = 32
NKV = 8
NK = 32
SCALE = 1.0 / math.sqrt(128.0)
ESHIFT = -24.0
N_CORES = 8

_NC_CACHE = {}


def _clip(r):
    if r < 4:
        return 0, 128 * (r + 1)
    return 128 * (r - 4), 512


def build_program():
    nc = bass.Bass("TRN2", target_bir_lowering=False)

    xs = nc.dram_tensor("xs", [NT * 128, H], F32, kind="ExternalInput").ap()
    wq = nc.dram_tensor("wq", [8, 128, NK, 512], F32, kind="ExternalInput").ap()
    wk = nc.dram_tensor("wk", [2, 128, NK, 512], F32, kind="ExternalInput").ap()
    wv = nc.dram_tensor("wv", [2, 128, NK, 512], F32, kind="ExternalInput").ap()
    wo = nc.dram_tensor("wo", [8, 128, NK, 512], F32, kind="ExternalInput").ap()
    wb = nc.dram_tensor("wb", [8, 128, NK, 512], F32, kind="ExternalInput").ap()
    wr = nc.dram_tensor("wr", [128, NK, 8], F32, kind="ExternalInput").ap()
    la = nc.dram_tensor("la", [128, NK, 128], F32, kind="ExternalInput").ap()
    lb = nc.dram_tensor("lb", [128, H], F32, kind="ExternalInput").ap()
    bb = nc.dram_tensor("bb", [128, H], F32, kind="ExternalInput").ap()
    br = nc.dram_tensor("br", [128, 8], F32, kind="ExternalInput").ap()
    chi = nc.dram_tensor("chi", [128, NT * 128], F32, kind="ExternalInput").ap()
    clo = nc.dram_tensor("clo", [128, NT * 128], F32, kind="ExternalInput").ap()
    shi = nc.dram_tensor("shi", [128, NT * 128], F32, kind="ExternalInput").ap()
    slo = nc.dram_tensor("slo", [128, NT * 128], F32, kind="ExternalInput").ap()
    mlo = nc.dram_tensor("mlo", [128, 8, 512], BF16, kind="ExternalInput").ap()
    tri = nc.dram_tensor("tri", [128, 128], BF16, kind="ExternalInput").ap()
    one = nc.dram_tensor("one", [128, 128], BF16, kind="ExternalInput").ap()
    os_ = nc.dram_tensor("os_", [NU * 128, H], F32, kind="ExternalOutput").ap()
    kts = nc.dram_tensor("kts", [NKV, NT, 128, 128], BF16, kind="Internal").ap()
    vts = nc.dram_tensor("vts", [NKV, NT, 128, 128], BF16, kind="Internal").ap()
    qts = nc.dram_tensor("qts", [NQH, NU, 128, 128], BF16, kind="Internal").ap()
    avs = nc.dram_tensor("avs", [NQH, 128, 1024], BF16, kind="Internal").ap()
    xns = nc.dram_tensor("xns", [128, NK, 1024], BF16, kind="Internal").ap()
    ss = nc.dram_tensor("ss", [NU, 128, H], F32, kind="Internal").ap()

    st = contextlib.ExitStack()
    with st:
        sb = lambda *a, **k: st.enter_context(nc.sbuf_tensor(*a, **k))
        xnT = sb("xnT", [128, NK, NT * 128], BF16, is_transpose_dest=True)
        wbuf = sb("wbuf", [128, NK, 512], BF16)
        xbuf = sb("xbuf", [128, H // 2], F32)
        adum = sb("adum", [128, H // 2], F32)
        xtm0 = sb("xtm0", [128, H], BF16)
        xtm1 = sb("xtm1", [128, H], BF16)
        sqsc = sb("sqsc", [128, NT * 128], BF16)
        cosv = sb("cosv", [128, NT * 128], F32)
        sinv = sb("sinv", [128, NT * 128], F32)
        stats = sb("stats", [128, NT, 8], F32)
        spx = sb("spx", [128, 128], F32)
        spd = sb("spd", [128, 16], F32)
        hwT = sb("hwT", [128, NU * 128], BF16, is_transpose_dest=True)
        hwtm = sb("hwtm", [128, 128], BF16)
        wrsb = sb("wrsb", [128, NK, 8], BF16)
        brsb = sb("brsb", [128, 8], F32)
        trisb = sb("trisb", [128, 128], BF16)
        onesb = sb("onesb", [128, 128], BF16)
        nb24 = sb("nb24", [128, 1], F32)
        kvstg0 = sb("kvstg0", [128, 512], BF16)
        kvstg1 = sb("kvstg1", [128, 512], BF16)
        trstg0 = sb("trstg0", [128, 512], BF16, is_transpose_dest=True)
        trstg1 = sb("trstg1", [128, 512], BF16, is_transpose_dest=True)
        sstg0 = sb("sstg0", [128, 512], F32)
        sstg1 = sb("sstg1", [128, 512], F32)
        gstg0 = sb("gstg0", [128, 512], F32)
        gstg1 = sb("gstg1", [128, 512], F32)
        ostg0 = sb("ostg0", [128, 512], F32)
        ostg1 = sb("ostg1", [128, 512], F32)
        xck0 = sb("xck0", [128, 512], F32)
        xck1 = sb("xck1", [128, 512], F32)
        bbs = sb("bbs", [128, 512], F32)
        lbs0 = sb("lbs0", [128, 512], BF16)

        exp_sb = xtm0.rearrange("p (r q) -> p r q", r=8)
        qh = [xtm1[:, 0:1024], xtm1[:, 1024:2048]]
        kh = xtm1[:, 2048:2048 + NT * 128]
        vh = sqsc
        avh = [kvstg0, kvstg1]
        den_r = gstg0
        rtA, rtB = sstg0, sstg1

        pb = [st.enter_context(nc.psum_tensor(f"pb{i}", [128, 512], F32))
              for i in range(6)]

        semnames = [
            "s_ph", "s_xld", "s_stat", "s_rec", "s_rsq", "s_apl",
            "s_trA0", "s_trA1", "s_sml", "s_sm2", "s_tbl", "s_pmm", "s_spx",
            "s_ksum", "s_hw", "s_hwT", "s_wld", "s_kvmm", "s_kstg",
            "s_kvtr0", "s_kvtr1", "s_kvsp", "s_xnsp", "s_qld0", "s_qld1",
            "s_kvld", "s_sc", "s_exp", "s_msk", "s_av", "s_avn", "s_avsp",
            "s_avld", "s_smm", "s_scp", "s_ssp", "s_sld0", "s_sld1",
            "s_xcl0", "s_xcl1", "s_fmm", "s_fb", "s_fo", "s_fg", "s_ost",
            "s_lbl", "s_bbl", "s_msl", "s_dvf",
        ]
        sm = {n: st.enter_context(nc.semaphore(n)) for n in semnames}
        block = st.enter_context(nc.Block())

        xtm = [xtm0, xtm1]
        strA = [sm["s_trA0"], sm["s_trA1"]]
        kvstg = [kvstg0, kvstg1]
        trstg = [trstg0, trstg1]
        skvtr = [sm["s_kvtr0"], sm["s_kvtr1"]]
        sqld = [sm["s_qld0"], sm["s_qld1"]]
        sstg = [sstg0, sstg1]
        gstg = [gstg0, gstg1]
        ostg = [ostg0, ostg1]
        xck = [xck0, xck1]
        lbs = [lbs0, lbs0]
        ssld = [sm["s_sld0"], sm["s_sld1"]]
        sxcl = [sm["s_xcl0"], sm["s_xcl1"]]

        dvf = [0]

        def fence(v, instop):
            dvf[0] += 1
            instop.then_inc(sm["s_dvf"], 1)
            v.wait_ge(sm["s_dvf"], dvf[0])

        def ropeviews(ps, tt, outbuf):
            psv = ps.rearrange("p (h u j) -> p h u j", h=4, u=2)
            pss = bass.AP(ps, 64, [[512, 128], [128, 4], [-64, 2], [1, 64]])
            cv = bass.AP(cosv, tt * 128,
                         [[NT * 128, 128], [0, 4], [64, 2], [1, 64]])
            sv = bass.AP(sinv, tt * 128,
                         [[NT * 128, 128], [0, 4], [64, 2], [1, 64]])
            ov = outbuf.rearrange("p (h u j) -> p h u j", h=4, u=2)
            rA = rtA.rearrange("p (h u j) -> p h u j", h=4, u=2)
            rB = rtB.rearrange("p (h u j) -> p h u j", h=4, u=2)
            return psv, pss, cv, sv, ov, rA, rB

        @block.sync
        def _(sy):
            sy.dma_start(out=brsb[:, :], in_=br).then_inc(sm["s_sml"], 16)
            sy.dma_start(out=trisb[:, :], in_=tri).then_inc(sm["s_sml"], 16)
            sy.dma_start(out=onesb[:, :], in_=one).then_inc(sm["s_sml"], 16)
            sy.dma_start(out=cosv[:, :], in_=chi).then_inc(sm["s_tbl"], 16)
            sy.dma_start(out=sinv[:, :], in_=shi).then_inc(sm["s_tbl"], 16)
            sy.wait_ge(sm["s_sml"], 48)
            sy.nop().then_inc(sm["s_ph"], 1)

        @block.gpsimd
        def _(g):
            g.dma_start(out=wrsb[:, :, :], in_=wr).then_inc(sm["s_sm2"], 16)
            g.wait_ge(sm["s_tbl"], 32)
            g.dma_start(out=cosv[:, :], in_=clo, accum_op=ALU.add
                        ).then_inc(sm["s_sm2"], 16)
            g.dma_start(out=sinv[:, :], in_=slo, accum_op=ALU.add
                        ).then_inc(sm["s_sm2"], 16)
            g.wait_ge(sm["s_sm2"], 48)
            g.nop().then_inc(sm["s_ph"], 1)

        @block.vector
        def _(v):
            v.memset(nb24[:, :], ESHIFT)
            v.nop().then_inc(sm["s_ph"], 1)

        @block.scalar
        def _(s):
            s.nop().then_inc(sm["s_ph"], 1)

        @block.tensor
        def _(t):
            t.nop().then_inc(sm["s_ph"], 1)

        @block.sync
        def _(sy):
            sy.wait_ge(sm["s_ph"], 5)
            HH = H // 2
            for t in range(NT):
                for j, gate in enumerate([
                        ("apl", 2 * t), ("stat", 2 * t + 1),
                        ("stat", 2 * t + 2), ("apl", 2 * t + 1)]):
                    nm, val = gate
                    if val > 0:
                        sy.wait_ge(sm["s_apl" if nm == "apl" else "s_stat"], val)
                    half = j % 2
                    sy.dma_start(out=xbuf[:, :],
                                 in_=xs[t * 128:(t + 1) * 128,
                                        half * HH:(half + 1) * HH]
                                 ).then_inc(sm["s_xld"], 16)
            sy.wait_ge(strA[0], 512 * 6)
            sy.wait_ge(strA[1], 512 * 6)
            sy.nop().then_inc(sm["s_ph"], 1)

        @block.scalar
        def _(s):
            s.wait_ge(sm["s_ph"], 5)
            for t in range(NT):
                pa = t % 2
                for half in range(2):
                    s.wait_ge(sm["s_xld"], 16 * (4 * t + half + 1))
                    s.activation(adum[:, :], xbuf[:, :], AF.Copy,
                                 accum_out=stats[:, t, 0 + half:1 + half])
                    s.activation(adum[:, :], xbuf[:, :], AF.Square,
                                 accum_out=stats[:, t, 2 + half:3 + half]
                                 ).then_inc(sm["s_stat"], 1)
                s.wait_ge(sm["s_rec"], t + 1)
                s.activation(stats[:, t, 6:7], stats[:, t, 5:6], AF.Sqrt
                             ).then_inc(sm["s_rsq"], 1)
                s.wait_ge(sm["s_apl"], 2 * t + 2)
                for k in range(NK):
                    s.dma_start_transpose(xnT[:, k, t * 128:(t + 1) * 128],
                                          xtm[pa][:, k * 128:(k + 1) * 128]
                                          ).then_inc(strA[pa], 16)
            s.wait_ge(strA[0], 512 * 6)
            s.wait_ge(strA[1], 512 * 6)
            s.nop().then_inc(sm["s_ph"], 1)

        @block.vector
        def _(v):
            v.wait_ge(sm["s_ph"], 5)
            HH = H // 2
            for t in range(NT):
                pa = t % 2
                v.wait_ge(sm["s_stat"], 2 * t + 2)
                fence(v, v.tensor_tensor(stats[:, t, 4:5], stats[:, t, 0:1],
                                          stats[:, t, 1:2], op=ALU.add))
                fence(v, v.tensor_tensor(stats[:, t, 5:6], stats[:, t, 2:3],
                                         stats[:, t, 3:4], op=ALU.add))
                fence(v, v.tensor_scalar(stats[:, t, 4:5], stats[:, t, 4:5],
                                         1.0 / H, None, op0=ALU.mult))
                fence(v, v.tensor_scalar(stats[:, t, 5:6], stats[:, t, 5:6],
                                         1.0 / H, None, op0=ALU.mult))
                fence(v, v.tensor_tensor(stats[:, t, 7:8], stats[:, t, 4:5],
                                         stats[:, t, 4:5], op=ALU.mult))
                fence(v, v.tensor_tensor(stats[:, t, 5:6], stats[:, t, 5:6],
                                         stats[:, t, 7:8], op=ALU.subtract))
                fence(v, v.tensor_scalar(stats[:, t, 5:6], stats[:, t, 5:6],
                                         1e-5, None, op0=ALU.add))
                v.reciprocal(stats[:, t, 5:6], stats[:, t, 5:6]
                             ).then_inc(sm["s_rec"], 1)
                v.wait_ge(sm["s_rsq"], t + 1)
                if t >= 2:
                    v.wait_ge(strA[pa], 512 * (t // 2))
                for half in range(2):
                    v.wait_ge(sm["s_xld"], 16 * (4 * t + 2 + half + 1))
                    v.tensor_scalar(xtm[pa][:, half * HH:(half + 1) * HH],
                                    xbuf[:, :], stats[:, t, 4:5],
                                    stats[:, t, 6:7], op0=ALU.subtract,
                                    op1=ALU.mult).then_inc(sm["s_apl"], 1)
            v.nop().then_inc(sm["s_ph"], 1)

        @block.tensor
        def _(t):
            t.wait_ge(sm["s_ph"], 5)
            t.nop().then_inc(sm["s_ph"], 1)

        @block.gpsimd
        def _(g):
            g.wait_ge(sm["s_ph"], 5)
            g.dma_start(out=wbuf[:, :, 0:128], in_=la).then_inc(sm["s_sm2"], 16)
            g.nop().then_inc(sm["s_ph"], 1)

        CMP = spx[:, 0:64].rearrange("p (i j) -> p i j", i=8)
        LG = spx[:, 64:72]
        KI = spx[:, 72:80]
        SI = spx[:, 80:88]
        KZ = spx[:, 88:96]
        CO = spx[:, 96:104]
        ZC = spx[:, 104:112]
        KS = spx[:, 112:113]
        TS = spx[:, 113:114]
        TM = spx[:, 114:115]
        RK = spx[:, 115:116]
        TAU = spx[:, 116:117]
        WW = spx[:, 120:128]

        @block.tensor
        def _(t):
            t.wait_ge(sm["s_ph"], 10)
            t.wait_ge(sm["s_sm2"], 64)
            for u in range(NU):
                tt = u + 4
                if u >= 1:
                    t.wait_ge(sm["s_spx"], u)
                    t.wait_ge(sm["s_hw"], u)
                for k in range(NK):
                    t.matmul(pb[0][:, 0:8], xnT[:, k, tt * 128:(tt + 1) * 128],
                             wrsb[:, k, :], start=(k == 0), stop=(k == NK - 1)
                             ).then_inc(sm["s_pmm"], 1) if k == NK - 1 else \
                        t.matmul(pb[0][:, 0:8], xnT[:, k, tt * 128:(tt + 1) * 128],
                                 wrsb[:, k, :], start=(k == 0), stop=False)
                for k in range(NK):
                    t.matmul(pb[1][:, 0:128], xnT[:, k, tt * 128:(tt + 1) * 128],
                             wbuf[:, k, 0:128], start=(k == 0), stop=(k == NK - 1)
                             ).then_inc(sm["s_pmm"], 1) if k == NK - 1 else \
                        t.matmul(pb[1][:, 0:128], xnT[:, k, tt * 128:(tt + 1) * 128],
                                 wbuf[:, k, 0:128], start=(k == 0), stop=False)
            t.nop().then_inc(sm["s_ph"], 1)

        @block.vector
        def _(v):
            v.wait_ge(sm["s_ph"], 10)
            for u in range(NU):
                v.wait_ge(sm["s_pmm"], 2 * u + 1)
                fence(v, v.tensor_tensor(LG, pb[0][:, 0:8], brsb[:, :], op=ALU.add))
                for i in range(8):
                    v.tensor_scalar(CMP[:, i, :], LG, LG[:, i:i + 1], None,
                                    op0=ALU.is_ge)
                fence(v, v.tensor_scalar(CMP[:, 7, :], LG, LG[:, 7:8], None,
                                         op0=ALU.is_ge))
                fence(v, v.tensor_reduce(KI, CMP, axis=AX.X, op=ALU.add))
                fence(v, v.tensor_tensor(CMP, CMP,
                                         bass.AP(spx, 64, [[128, 128], [0, 8], [1, 8]]),
                                         op=ALU.mult))
                fence(v, v.tensor_reduce(SI, CMP, axis=AX.X, op=ALU.add))
                fence(v, v.tensor_tensor(KZ, KI, LG, op=ALU.mult))
                fence(v, v.tensor_scalar(KZ, KZ, 1.0, None, op0=ALU.add))
                fence(v, v.tensor_tensor(CO, KZ, SI, op=ALU.is_gt))
                fence(v, v.tensor_tensor(ZC, LG, CO, op=ALU.mult))
                v.nop().then_inc(sm["s_spx"], 1)
                v.wait_ge(sm["s_ksum"], u + 1)
                fence(v, v.tensor_scalar(TM, TS, 1.0, None, op0=ALU.subtract))
                fence(v, v.reciprocal(RK, KS))
                fence(v, v.tensor_tensor(TAU, TM, RK, op=ALU.mult))
                fence(v, v.tensor_scalar(WW, LG, TAU, 0.0, op0=ALU.subtract,
                                         op1=ALU.max))
                v.wait_ge(sm["s_pmm"], 2 * u + 2)
                if u >= 1:
                    v.wait_ge(sm["s_hwT"], 16 * u)
                for e in range(8):
                    op = v.tensor_scalar(hwtm[:, e * 16:(e + 1) * 16],
                                         pb[1][:, e * 16:(e + 1) * 16],
                                         WW[:, e:e + 1], None, op0=ALU.mult)
                    if e == 7:
                        op.then_inc(sm["s_hw"], 1)
            v.nop().then_inc(sm["s_ph"], 1)

        @block.scalar
        def _(s):
            s.wait_ge(sm["s_ph"], 10)
            for u in range(NU):
                s.wait_ge(sm["s_spx"], u + 1)
                s.activation(spd[:, 0:8], CO, AF.Copy, accum_out=KS)
                s.activation(spd[:, 8:16], ZC, AF.Copy, accum_out=TS
                             ).then_inc(sm["s_ksum"], 1)
            s.nop().then_inc(sm["s_ph"], 1)

        @block.sync
        def _(sy):
            sy.wait_ge(sm["s_ph"], 10)
            for u in range(NU):
                sy.wait_ge(sm["s_hw"], u + 1)
                sy.dma_start_transpose(hwT[:, u * 128:(u + 1) * 128],
                                       hwtm[:, 0:128]).then_inc(sm["s_hwT"], 16)
            sy.wait_ge(sm["s_hwT"], 16 * NU)
            sy.nop().then_inc(sm["s_ph"], 1)

        @block.gpsimd
        def _(g):
            g.wait_ge(sm["s_ph"], 10)
            g.nop().then_inc(sm["s_ph"], 1)

        @block.gpsimd
        def _(g):
            g.wait_ge(sm["s_ph"], 15)
            for c in range(4):
                src = [wk, wk, wv, wv][c]
                oc = c % 2
                if c >= 1:
                    g.wait_ge(sm["s_kvmm"], c * NT)
                g.dma_start(out=wbuf[:, :, :], in_=src[oc]).then_inc(sm["s_wld"], 16)
            g.nop().then_inc(sm["s_ph"], 1)

        @block.tensor
        def _(t):
            t.wait_ge(sm["s_ph"], 15)
            for c in range(4):
                t.wait_ge(sm["s_wld"], 16 * (c + 1))
                for tt in range(NT):
                    it = c * NT + tt
                    if it >= 2:
                        t.wait_ge(sm["s_kstg"], it - 1)
                    for k in range(NK):
                        mmop = t.matmul(pb[it % 2][:, :],
                                        xnT[:, k, tt * 128:(tt + 1) * 128],
                                        wbuf[:, k, :], start=(k == 0),
                                        stop=(k == NK - 1))
                        if k == NK - 1:
                            mmop.then_inc(sm["s_kvmm"], 1)
            t.nop().then_inc(sm["s_ph"], 1)

        @block.vector
        def _(v):
            v.wait_ge(sm["s_ph"], 15)
            for c in range(4):
                isk = c < 2
                for tt in range(NT):
                    it = c * NT + tt
                    v.wait_ge(sm["s_kvmm"], it + 1)
                    if it >= 2:
                        if it - 2 < 24:
                            v.wait_ge(skvtr[it % 2], 64 * ((it - 2) // 2 + 1))
                        else:
                            v.wait_ge(sm["s_kvsp"], 16 * (it - 1))
                    if isk:
                        psv, pss, cv, sv, ov, rA, rB = ropeviews(
                            pb[it % 2], tt, kvstg[it % 2])
                        v.tensor_tensor(rA, psv, cv, op=ALU.mult)
                        fence(v, v.tensor_tensor(rB, pss, sv, op=ALU.mult))
                        v.tensor_tensor(ov, rA, rB, op=ALU.add
                                        ).then_inc(sm["s_kstg"], 1)
                    else:
                        v.tensor_copy(kvstg[it % 2][:, :], pb[it % 2][:, :]
                                      ).then_inc(sm["s_kstg"], 1)
            v.nop().then_inc(sm["s_ph"], 1)

        @block.scalar
        def _(s):
            s.wait_ge(sm["s_ph"], 15)
            for c in range(2):
                for tt in range(NT):
                    it = c * NT + tt
                    s.wait_ge(sm["s_kstg"], it + 1)
                    if it >= 2:
                        s.wait_ge(sm["s_kvsp"], 16 * (it - 1))
                    for j in range(4):
                        s.dma_start_transpose(trstg[it % 2][:, j * 128:(j + 1) * 128],
                                              kvstg[it % 2][:, j * 128:(j + 1) * 128]
                                              ).then_inc(skvtr[it % 2], 16)
            s.nop().then_inc(sm["s_ph"], 1)

        @block.sync
        def _(sy):
            sy.wait_ge(sm["s_ph"], 15)
            for c in range(4):
                isk = c < 2
                oc = c % 2
                for tt in range(NT):
                    it = c * NT + tt
                    if isk:
                        sy.wait_ge(skvtr[it % 2], 64 * (it // 2 + 1))
                        sy.dma_start(
                            out=kts[4 * oc:4 * oc + 4, tt, :, :].transpose([1, 0, 2]),
                            in_=trstg[it % 2].rearrange("p (j q) -> p j q", j=4)
                        ).then_inc(sm["s_kvsp"], 16)
                    else:
                        sy.wait_ge(sm["s_kstg"], it + 1)
                        sy.dma_start(
                            out=vts[4 * oc:4 * oc + 4, tt, :, :].transpose([1, 0, 2]),
                            in_=kvstg[it % 2].rearrange("p (j q) -> p j q", j=4)
                        ).then_inc(sm["s_kvsp"], 16)
            sy.wait_ge(sm["s_kvsp"], 16 * 48)
            sy.nop().then_inc(sm["s_ph"], 1)

        @block.gpsimd
        def _(g):
            g.wait_ge(sm["s_ph"], 20)
            for oc in range(8):
                if oc >= 1:
                    g.wait_ge(sm["s_kvmm"], 48 + oc * NU)
                g.dma_start(out=wbuf[:, :, :], in_=wq[oc]).then_inc(sm["s_wld"], 16)
            g.nop().then_inc(sm["s_ph"], 1)

        @block.tensor
        def _(t):
            t.wait_ge(sm["s_ph"], 20)
            for oc in range(8):
                t.wait_ge(sm["s_wld"], 16 * (4 + oc + 1))
                for u in range(NU):
                    tt = u + 4
                    it = oc * NU + u
                    if it >= 2:
                        t.wait_ge(sm["s_kstg"], 48 + it - 1)
                    for k in range(NK):
                        mmop = t.matmul(pb[it % 2][:, :],
                                        xnT[:, k, tt * 128:(tt + 1) * 128],
                                        wbuf[:, k, :], start=(k == 0),
                                        stop=(k == NK - 1))
                        if k == NK - 1:
                            mmop.then_inc(sm["s_kvmm"], 1)
            t.nop().then_inc(sm["s_ph"], 1)

        @block.vector
        def _(v):
            v.wait_ge(sm["s_ph"], 20)
            for oc in range(8):
                for u in range(NU):
                    tt = u + 4
                    it = oc * NU + u
                    v.wait_ge(sm["s_kvmm"], 48 + it + 1)
                    if it >= 2:
                        v.wait_ge(skvtr[it % 2], 768 + 64 * ((it - 2) // 2 + 1))
                    psv, pss, cv, sv, ov, rA, rB = ropeviews(
                        pb[it % 2], tt, kvstg[it % 2])
                    v.tensor_tensor(rA, psv, cv, op=ALU.mult)
                    fence(v, v.tensor_tensor(rB, pss, sv, op=ALU.mult))
                    v.tensor_tensor(ov, rA, rB, op=ALU.add).then_inc(sm["s_kstg"], 1)
            v.nop().then_inc(sm["s_ph"], 1)

        @block.scalar
        def _(s):
            s.wait_ge(sm["s_ph"], 20)
            for oc in range(8):
                for u in range(NU):
                    it = oc * NU + u
                    s.wait_ge(sm["s_kstg"], 48 + it + 1)
                    if it >= 2:
                        s.wait_ge(sm["s_kvsp"], 16 * (48 + it - 1))
                    for j in range(4):
                        s.dma_start_transpose(trstg[it % 2][:, j * 128:(j + 1) * 128],
                                              kvstg[it % 2][:, j * 128:(j + 1) * 128]
                                              ).then_inc(skvtr[it % 2], 16)
            s.nop().then_inc(sm["s_ph"], 1)

        @block.sync
        def _(sy):
            sy.wait_ge(sm["s_ph"], 20)
            for oc in range(8):
                for u in range(NU):
                    it = oc * NU + u
                    sy.wait_ge(skvtr[it % 2], 768 + 64 * (it // 2 + 1))
                    sy.dma_start(
                        out=qts[4 * oc:4 * oc + 4, u, :, :].transpose([1, 0, 2]),
                        in_=trstg[it % 2].rearrange("p (j q) -> p j q", j=4)
                    ).then_inc(sm["s_kvsp"], 16)
            sy.wait_ge(sm["s_kvsp"], 16 * (48 + 64))
            for g4 in range(4):
                sy.dma_start(out=xns[:, g4 * 8:(g4 + 1) * 8, :],
                             in_=xnT[:, g4 * 8:(g4 + 1) * 8, 512:NT * 128]
                             ).then_inc(sm["s_xnsp"], 16)
            sy.wait_ge(sm["s_xnsp"], 64)
            sy.nop().then_inc(sm["s_ph"], 1)

        @block.gpsimd
        def _(g):
            g.wait_ge(sm["s_ph"], 25)
            g.dma_start(out=wbuf[:, 0:8, :], in_=mlo).then_inc(sm["s_msl"], 16)
            g.nop().then_inc(sm["s_ph"], 1)

        @block.sync
        def _(sy):
            sy.wait_ge(sm["s_ph"], 25)
            for h in range(NQH):
                if h >= 2:
                    sy.wait_ge(sm["s_sc"], ((h - 2) * 2 + 2) * 8)
                sy.dma_start(out=qh[h % 2].rearrange("p (u q) -> p u q", u=8),
                             in_=qts[h].transpose([1, 0, 2])
                             ).then_inc(sqld[h % 2], 16)
                if h % 4 == 0:
                    if h >= 4:
                        sy.wait_ge(sm["s_av"], 2 * h)
                    sy.dma_start(out=kh.rearrange("p (tt q) -> p tt q", tt=NT),
                                 in_=kts[h // 4].transpose([1, 0, 2])
                                 ).then_inc(sm["s_kvld"], 16)
                    sy.dma_start(out=vh.rearrange("p (tt q) -> p tt q", tt=NT),
                                 in_=vts[h // 4].transpose([1, 0, 2])
                                 ).then_inc(sm["s_kvld"], 16)
                sy.wait_ge(sm["s_avn"], 2 * h + 2)
                sy.dma_start(out=avs[h][:, 0:512], in_=kvstg0[:, :]
                             ).then_inc(sm["s_avsp"], 16)
                sy.dma_start(out=avs[h][:, 512:1024], in_=kvstg1[:, :]
                             ).then_inc(sm["s_avsp"], 16)
            sy.wait_ge(sm["s_avsp"], 32 * NQH)
            sy.nop().then_inc(sm["s_ph"], 1)

        @block.tensor
        def _(t):
            t.wait_ge(sm["s_ph"], 25)
            for h in range(NQH):
                t.wait_ge(sqld[h % 2], 16 * (h // 2 + 1))
                t.wait_ge(sm["s_kvld"], 32 * (h // 4 + 1))
                for qc in range(2):
                    it = h * 2 + qc
                    for r in range(8):
                        kb = 4 * qc + r
                        lo, hi = _clip(r)
                        if it * 8 + r >= 2:
                            t.wait_ge(sm["s_exp"], it * 8 + r - 1)
                        t.matmul(pb[r % 2][:, lo:hi],
                                 kh[:, kb * 128:(kb + 1) * 128],
                                 qh[h % 2][:, qc * 512 + lo:qc * 512 + hi],
                                 start=True, stop=True).then_inc(sm["s_sc"], 1)
                    t.wait_ge(sm["s_msk"], (it + 1) * 8)
                    if it >= 1:
                        t.wait_ge(sm["s_avn"], it)
                    order = [4, 0, 1, 2, 3, 5, 6, 7]
                    for idx, r in enumerate(order):
                        kb = 4 * qc + r
                        lo, hi = _clip(r)
                        t.matmul(pb[2][:, lo:hi], vh[:, kb * 128:(kb + 1) * 128],
                                 exp_sb[:, r, lo:hi], start=(idx == 0),
                                 stop=(idx == 7))
                    for idx, r in enumerate(order):
                        lo, hi = _clip(r)
                        mmop = t.matmul(pb[3][:, lo:hi], onesb[:, :],
                                        exp_sb[:, r, lo:hi], start=(idx == 0),
                                        stop=(idx == 7))
                        if idx == 7:
                            mmop.then_inc(sm["s_av"], 1)
            t.nop().then_inc(sm["s_ph"], 1)

        @block.scalar
        def _(s):
            s.wait_ge(sm["s_ph"], 25)
            for h in range(NQH):
                for qc in range(2):
                    it = h * 2 + qc
                    for r in range(8):
                        lo, hi = _clip(r)
                        s.wait_ge(sm["s_sc"], it * 8 + r + 1)
                        if r == 0 and it >= 1:
                            s.wait_ge(sm["s_av"], it)
                        s.activation(exp_sb[:, r, lo:hi], pb[r % 2][:, lo:hi],
                                     AF.Exp, bias=nb24[:, 0:1], scale=SCALE
                                     ).then_inc(sm["s_exp"], 1)
            s.nop().then_inc(sm["s_ph"], 1)

        @block.vector
        def _(v):
            v.wait_ge(sm["s_ph"], 25)
            v.wait_ge(sm["s_msl"], 16)
            for h in range(NQH):
                for qc in range(2):
                    it = h * 2 + qc
                    for r in range(8):
                        lo, hi = _clip(r)
                        v.wait_ge(sm["s_exp"], it * 8 + r + 1)
                        if r < 4:
                            v.tensor_tensor(exp_sb[:, r, lo:hi],
                                            exp_sb[:, r, lo:hi],
                                            wbuf[:, 4 * qc + r, lo:hi],
                                            op=ALU.mult).then_inc(sm["s_msk"], 1)
                        else:
                            d0 = 128 * (r - 4)
                            v.tensor_tensor(exp_sb[:, r, d0:d0 + 128],
                                            exp_sb[:, r, d0:d0 + 128],
                                            trisb[:, :], op=ALU.mult
                                            ).then_inc(sm["s_msk"], 1)
                    v.wait_ge(sm["s_av"], it + 1)
                    if qc == 0 and h >= 1:
                        v.wait_ge(sm["s_avsp"], 32 * h)
                    fence(v, v.reciprocal(den_r[:, :], pb[3][:, :]))
                    v.tensor_tensor(avh[qc][:, :], pb[2][:, :], den_r[:, :],
                                    op=ALU.mult).then_inc(sm["s_avn"], 1)
            v.nop().then_inc(sm["s_ph"], 1)

        @block.gpsimd
        def _(g):
            g.wait_ge(sm["s_ph"], 30)
            for oc in range(8):
                if oc >= 1:
                    g.wait_ge(sm["s_smm"], oc * NU)
                g.dma_start(out=wbuf[:, :, :], in_=wo[oc]).then_inc(sm["s_wld"], 16)
            g.nop().then_inc(sm["s_ph"], 1)

        @block.sync
        def _(sy):
            sy.wait_ge(sm["s_ph"], 30)
            for h in range(NQH):
                sy.dma_start(out=xnT[:, h, 0:1024], in_=avs[h]
                             ).then_inc(sm["s_avld"], 16)
            for oc in range(8):
                for u in range(NU):
                    it = oc * NU + u
                    sy.wait_ge(sm["s_scp"], it + 1)
                    sy.dma_start(out=ss[u][:, oc * 512:(oc + 1) * 512],
                                 in_=sstg[it % 2][:, :]).then_inc(sm["s_ssp"], 16)
            sy.wait_ge(sm["s_ssp"], 16 * 64)
            sy.nop().then_inc(sm["s_ph"], 1)

        @block.tensor
        def _(t):
            t.wait_ge(sm["s_ph"], 30)
            t.wait_ge(sm["s_avld"], 16 * NQH)
            for oc in range(8):
                t.wait_ge(sm["s_wld"], 16 * (12 + oc + 1))
                for u in range(NU):
                    it = oc * NU + u
                    if it >= 2:
                        t.wait_ge(sm["s_scp"], it - 1)
                    for k in range(NK):
                        mmop = t.matmul(pb[u % 2][:, :], xnT[:, k, u * 128:(u + 1) * 128],
                                        wbuf[:, k, :], start=(k == 0),
                                        stop=(k == NK - 1))
                        if k == NK - 1:
                            mmop.then_inc(sm["s_smm"], 1)
            t.nop().then_inc(sm["s_ph"], 1)

        @block.scalar
        def _(s):
            s.wait_ge(sm["s_ph"], 30)
            for oc in range(8):
                for u in range(NU):
                    it = oc * NU + u
                    s.wait_ge(sm["s_smm"], it + 1)
                    if it >= 2:
                        s.wait_ge(sm["s_ssp"], 16 * (it - 1))
                    s.activation(sstg[it % 2][:, :], pb[u % 2][:, :], AF.Copy
                                 ).then_inc(sm["s_scp"], 1)
            s.nop().then_inc(sm["s_ph"], 1)

        @block.vector
        def _(v):
            v.wait_ge(sm["s_ph"], 30)
            v.nop().then_inc(sm["s_ph"], 1)

        @block.gpsimd
        def _(g):
            g.wait_ge(sm["s_ph"], 35)
            for oc in range(8):
                if oc >= 1:
                    g.wait_ge(sm["s_fmm"], oc * NU)
                g.dma_start(out=wbuf[:, :, :], in_=wb[oc]).then_inc(sm["s_wld"], 16)
                if oc >= 1:
                    g.wait_ge(sm["s_fmm"], oc * NU)
                g.dma_start(out=lbs[oc % 2][:, :], in_=lb[:, oc * 512:(oc + 1) * 512]
                            ).then_inc(sm["s_lbl"], 16)
            g.nop().then_inc(sm["s_ph"], 1)

        @block.sync
        def _(sy):
            sy.wait_ge(sm["s_ph"], 35)
            for g4 in range(4):
                sy.dma_start(out=xnT[:, g4 * 8:(g4 + 1) * 8, 0:1024],
                             in_=xns[:, g4 * 8:(g4 + 1) * 8, :]
                             ).then_inc(sm["s_avld"], 16)
            for oc in range(8):
                if oc >= 1:
                    sy.wait_ge(sm["s_fb"], oc * NU)
                sy.dma_start(out=bbs[:, :], in_=bb[:, oc * 512:(oc + 1) * 512]
                             ).then_inc(sm["s_bbl"], 16)
                for u in range(NU):
                    it = oc * NU + u
                    if it >= 2:
                        sy.wait_ge(sm["s_fg"], it - 1)
                    sy.dma_start(out=sstg[it % 2][:, :],
                                 in_=ss[u][:, oc * 512:(oc + 1) * 512]
                                 ).then_inc(ssld[it % 2], 16)
                    sy.dma_start(out=xck[it % 2][:, :],
                                 in_=xs[(u + 4) * 128:(u + 5) * 128,
                                        oc * 512:(oc + 1) * 512]
                                 ).then_inc(sxcl[it % 2], 16)
                    sy.wait_ge(sm["s_fo"], it + 1)
                    sy.dma_start(out=os_[u * 128:(u + 1) * 128,
                                         oc * 512:(oc + 1) * 512],
                                 in_=ostg[it % 2][:, :]).then_inc(sm["s_ost"], 16)
            sy.wait_ge(sm["s_ost"], 16 * 64)

        @block.tensor
        def _(t):
            t.wait_ge(sm["s_ph"], 35)
            t.wait_ge(sm["s_avld"], 16 * (NQH + 4))
            for oc in range(8):
                t.wait_ge(sm["s_wld"], 16 * (20 + oc + 1))
                t.wait_ge(sm["s_lbl"], 16 * (oc + 1))
                for u in range(NU):
                    it = oc * NU + u
                    if it >= 2:
                        t.wait_ge(sm["s_scp"], 64 + it - 1)
                    for k in range(NK):
                        t.matmul(pb[u % 2][:, :], xnT[:, k, u * 128:(u + 1) * 128],
                                 wbuf[:, k, :], start=(k == 0), stop=False)
                    t.matmul(pb[u % 2][:, :], hwT[:, u * 128:(u + 1) * 128],
                             lbs[oc % 2][:, :], start=False, stop=True
                             ).then_inc(sm["s_fmm"], 1)
            t.nop().then_inc(sm["s_ph"], 1)

        @block.vector
        def _(v):
            v.wait_ge(sm["s_ph"], 35)
            for oc in range(8):
                for u in range(NU):
                    it = oc * NU + u
                    v.wait_ge(sm["s_fmm"], it + 1)
                    v.wait_ge(sm["s_bbl"], 16 * (oc + 1))
                    v.tensor_tensor(pb[u % 2][:, :], pb[u % 2][:, :], bbs[:, :],
                                    op=ALU.add).then_inc(sm["s_fb"], 1)
                    v.wait_ge(sm["s_scp"], 64 + it + 1)
                    v.wait_ge(ssld[it % 2], 16 * (it // 2 + 1))
                    v.wait_ge(sxcl[it % 2], 16 * (it // 2 + 1))
                    if it >= 2:
                        v.wait_ge(sm["s_ost"], 16 * (it - 1))
                    fence(v, v.tensor_tensor(ostg[it % 2][:, :], sstg[it % 2][:, :],
                                              gstg[it % 2][:, :], op=ALU.mult))
                    v.tensor_tensor(ostg[it % 2][:, :], ostg[it % 2][:, :],
                                    xck[it % 2][:, :], op=ALU.add
                                    ).then_inc(sm["s_fo"], 1)
                    v.nop().then_inc(sm["s_fg"], 1)

        @block.scalar
        def _(s):
            s.wait_ge(sm["s_ph"], 35)
            for oc in range(8):
                for u in range(NU):
                    it = oc * NU + u
                    s.wait_ge(sm["s_fb"], it + 1)
                    if it >= 2:
                        s.wait_ge(sm["s_fg"], it - 1)
                    s.activation(gstg[it % 2][:, :], pb[u % 2][:, :], AF.Silu
                                 ).then_inc(sm["s_scp"], 1)

    return nc


def _tables(h_half):
    theta = 10000.0
    inv = 1.0 / theta ** (np.arange(0, 128, 2, dtype=np.float64) / 128.0)
    loc = np.arange(NT * 128)
    if h_half == 0:
        pos = np.where(loc < HALO, 0, loc - HALO).astype(np.float64)
    else:
        pos = (loc + HALO).astype(np.float64)
    ang = pos[:, None] * inv[None, :]
    cos64 = np.cos(ang)
    sin64 = np.sin(ang)
    cosT = np.concatenate([cos64, cos64], axis=1)
    sinA = np.concatenate([-sin64, sin64], axis=1)
    out = []
    for tab in (cosT, sinA):
        hi = tab.astype(np.float32)
        lo = (tab - hi.astype(np.float64)).astype(np.float32)
        out.append(np.ascontiguousarray(
            hi.reshape(NT, 128, 128).transpose(1, 0, 2).reshape(128, NT * 128)))
        out.append(np.ascontiguousarray(
            lo.reshape(NT, 128, 128).transpose(1, 0, 2).reshape(128, NT * 128)))
    return out


def _masks(h_half):
    m = np.zeros((128, 8, 512), dtype=np.float32)
    kc = np.arange(128)[:, None]
    qc = np.arange(512)[None, :]
    for qch in range(2):
        for r in range(4):
            if h_half == 0 and qch == 0:
                continue
            lo, hi = _clip(r)
            allow = (qc <= r * 128 + kc) & (qc < hi)
            m[:, qch * 4 + r, :] = allow.astype(np.float32)
    return m.astype(ml_dtypes.bfloat16)


def _w_tiles(w, noc):
    return np.ascontiguousarray(
        w.reshape(NK, 128, noc, 512).transpose(2, 1, 0, 3))


def kernel(L0_attn_hidden_states, L0_attn_Wq, L0_attn_Wk, L0_attn_Wv, L0_attn_Wo,
           L0_ffn_W_base, L0_ffn_b_base, L0_ffn_W_router, L0_ffn_b_router,
           L0_ffn_lora_A, L0_ffn_lora_B):
    x = np.ascontiguousarray(L0_attn_hidden_states, dtype=np.float32)
    wq_h = _w_tiles(np.ascontiguousarray(L0_attn_Wq, dtype=np.float32), 8)
    wk_h = _w_tiles(np.ascontiguousarray(L0_attn_Wk, dtype=np.float32), 2)
    wv_h = _w_tiles(np.ascontiguousarray(L0_attn_Wv, dtype=np.float32), 2)
    wo_h = _w_tiles(np.ascontiguousarray(L0_attn_Wo, dtype=np.float32), 8)
    wb_h = _w_tiles(np.ascontiguousarray(L0_ffn_W_base.T, dtype=np.float32), 8)
    wr_h = np.ascontiguousarray(
        np.asarray(L0_ffn_W_router, dtype=np.float32).T.reshape(NK, 128, 8)
        .transpose(1, 0, 2))
    la_flat = np.asarray(L0_ffn_lora_A, dtype=np.float32).transpose(1, 0, 2).reshape(H, 128)
    la_h = np.ascontiguousarray(la_flat.reshape(NK, 128, 128).transpose(1, 0, 2))
    lb_h = np.ascontiguousarray(np.asarray(L0_ffn_lora_B, dtype=np.float32).reshape(128, H))
    bb_h = np.ascontiguousarray(
        np.broadcast_to(np.asarray(L0_ffn_b_base, dtype=np.float32), (128, H)))
    br_h = np.ascontiguousarray(
        np.broadcast_to(np.asarray(L0_ffn_b_router, dtype=np.float32), (128, 8)))
    tri_h = (np.arange(128)[:, None] <= np.arange(128)[None, :]
             ).astype(ml_dtypes.bfloat16)
    one_h = np.ones((128, 128), dtype=ml_dtypes.bfloat16)

    if "nc" not in _NC_CACHE:
        _NC_CACHE["nc"] = build_program()
    nc = _NC_CACHE["nc"]

    tabs = [_tables(0), _tables(1)]
    msks = [_masks(0), _masks(1)]
    in_maps = []
    for c in range(N_CORES):
        b, hh = c // 2, c % 2
        if hh == 0:
            xsl = np.concatenate([np.zeros((HALO, H), np.float32), x[b, 0:1024]],
                                 axis=0)
        else:
            xsl = np.ascontiguousarray(x[b, HALO:2048])
        chi_h, clo_h, shi_h, slo_h = tabs[hh]
        in_maps.append({
            "xs": xsl, "wq": wq_h, "wk": wk_h, "wv": wv_h, "wo": wo_h,
            "wb": wb_h, "wr": wr_h, "la": la_h, "lb": lb_h, "bb": bb_h,
            "br": br_h, "chi": chi_h, "clo": clo_h, "shi": shi_h, "slo": slo_h,
            "mlo": msks[hh], "tri": tri_h, "one": one_h,
        })
    res = run_bass_kernel_spmd(nc, in_maps, core_ids=list(range(N_CORES)))
    out = np.empty((B, S, H), dtype=np.float32)
    for c in range(N_CORES):
        b, hh = c // 2, c % 2
        out[b, hh * 1024:(hh + 1) * 1024] = res.results[c]["os_"]
    return out


# version 04
# speedup vs baseline: 1.0890x; 1.0890x over previous
import contextlib
import math

import numpy as np
import ml_dtypes

import concourse.bass as bass
import concourse.mybir as mybir
from concourse.bass_utils import run_bass_kernel_spmd

F32 = mybir.dt.float32
BF16 = mybir.dt.bfloat16
AF = mybir.ActivationFunctionType
ALU = mybir.AluOpType
AX = mybir.AxisListType

H = 4096
S = 2048
B = 4
NT = 12
NU = 8
HALO = 512
NQH = 32
NKV = 8
NK = 32
SCALE = 1.0 / math.sqrt(128.0)
ESHIFT = -24.0
N_CORES = 8

_NC_CACHE = {}


def _clip(r):
    if r < 4:
        return 0, 128 * (r + 1)
    return 128 * (r - 4), 512


def build_program():
    nc = bass.Bass("TRN2", target_bir_lowering=False)

    xs = nc.dram_tensor("xs", [NT * 128, H], F32, kind="ExternalInput").ap()
    wq = nc.dram_tensor("wq", [8, 128, NK, 512], F32, kind="ExternalInput").ap()
    wk = nc.dram_tensor("wk", [2, 128, NK, 512], F32, kind="ExternalInput").ap()
    wv = nc.dram_tensor("wv", [2, 128, NK, 512], F32, kind="ExternalInput").ap()
    wo = nc.dram_tensor("wo", [8, 128, NK, 512], F32, kind="ExternalInput").ap()
    wb = nc.dram_tensor("wb", [8, 128, NK, 512], F32, kind="ExternalInput").ap()
    wr = nc.dram_tensor("wr", [128, NK, 8], F32, kind="ExternalInput").ap()
    la = nc.dram_tensor("la", [128, NK, 128], F32, kind="ExternalInput").ap()
    lb = nc.dram_tensor("lb", [128, H], F32, kind="ExternalInput").ap()
    bb = nc.dram_tensor("bb", [128, H], F32, kind="ExternalInput").ap()
    br = nc.dram_tensor("br", [128, 8], F32, kind="ExternalInput").ap()
    chi = nc.dram_tensor("chi", [128, NT * 128], F32, kind="ExternalInput").ap()
    clo = nc.dram_tensor("clo", [128, NT * 128], F32, kind="ExternalInput").ap()
    shi = nc.dram_tensor("shi", [128, NT * 128], F32, kind="ExternalInput").ap()
    slo = nc.dram_tensor("slo", [128, NT * 128], F32, kind="ExternalInput").ap()
    mlo = nc.dram_tensor("mlo", [128, 8, 512], BF16, kind="ExternalInput").ap()
    tri = nc.dram_tensor("tri", [128, 128], BF16, kind="ExternalInput").ap()
    one = nc.dram_tensor("one", [128, 128], BF16, kind="ExternalInput").ap()
    os_ = nc.dram_tensor("os_", [NU * 128, H], F32, kind="ExternalOutput").ap()
    kts = nc.dram_tensor("kts", [NKV, NT, 128, 128], BF16, kind="Internal").ap()
    vts = nc.dram_tensor("vts", [NKV, NT, 128, 128], BF16, kind="Internal").ap()
    qts = nc.dram_tensor("qts", [NQH, NU, 128, 128], BF16, kind="Internal").ap()
    avs = nc.dram_tensor("avs", [NQH, 128, 1024], BF16, kind="Internal").ap()
    xns = nc.dram_tensor("xns", [128, NK, 1024], BF16, kind="Internal").ap()
    ss = nc.dram_tensor("ss", [NU, 128, H], F32, kind="Internal").ap()

    st = contextlib.ExitStack()
    with st:
        sb = lambda *a, **k: st.enter_context(nc.sbuf_tensor(*a, **k))
        xnT = sb("xnT", [128, NK, NT * 128], BF16, is_transpose_dest=True)
        wbuf = sb("wbuf", [128, NK, 512], BF16)
        xbuf = sb("xbuf", [128, H // 2], F32)
        adum = sb("adum", [128, H // 2], F32)
        xtm0 = sb("xtm0", [128, H], BF16)
        xtm1 = sb("xtm1", [128, H], BF16)
        sqsc = sb("sqsc", [128, NT * 128], BF16)
        cosv = sb("cosv", [128, NT * 128], F32)
        sinv = sb("sinv", [128, NT * 128], F32)
        stats = sb("stats", [128, NT, 8], F32)
        spx = sb("spx", [128, 128], F32)
        spd = sb("spd", [128, 16], F32)
        hwT = sb("hwT", [128, NU * 128], BF16, is_transpose_dest=True)
        hwtm = sb("hwtm", [128, 128], BF16)
        wrsb = sb("wrsb", [128, NK, 8], BF16)
        brsb = sb("brsb", [128, 8], F32)
        trisb = sb("trisb", [128, 128], BF16)
        onesb = sb("onesb", [128, 128], BF16)
        nb24 = sb("nb24", [128, 1], F32)
        kvstg0 = sb("kvstg0", [128, 512], BF16)
        kvstg1 = sb("kvstg1", [128, 512], BF16)
        trstg0 = sb("trstg0", [128, 512], BF16, is_transpose_dest=True)
        trstg1 = sb("trstg1", [128, 512], BF16, is_transpose_dest=True)
        sstg0 = sb("sstg0", [128, 512], F32)
        sstg1 = sb("sstg1", [128, 512], F32)
        gstg0 = sb("gstg0", [128, 512], F32)
        gstg1 = sb("gstg1", [128, 512], F32)
        ostg0 = sb("ostg0", [128, 512], F32)
        ostg1 = sb("ostg1", [128, 512], F32)
        xck0 = sb("xck0", [128, 512], F32)
        xck1 = sb("xck1", [128, 512], F32)
        bbs = sb("bbs", [128, 512], F32)
        lbs0 = sb("lbs0", [128, 512], BF16)

        exp_sb = xtm0.rearrange("p (r q) -> p r q", r=8)
        qh = [xtm1[:, 0:1024], xtm1[:, 1024:2048]]
        kh = xtm1[:, 2048:2048 + NT * 128]
        vh = sqsc
        avh = [kvstg0, kvstg1]
        den_r = gstg0
        rtA, rtB = sstg0, sstg1

        pb = [st.enter_context(nc.psum_tensor(f"pb{i}", [128, 512], F32))
              for i in range(6)]

        semnames = [
            "s_ph", "s_xld", "s_stat", "s_rec", "s_rsq", "s_apl",
            "s_trA0", "s_trA1", "s_sml", "s_sm2", "s_tbl", "s_pmm", "s_spx",
            "s_ksum", "s_hw", "s_hwT", "s_wld", "s_wld1", "s_kvmm", "s_kstg",
            "s_kvtr0", "s_kvtr1", "s_kvsp", "s_xnsp", "s_qld0", "s_qld1",
            "s_kvld", "s_sc", "s_exp", "s_msk", "s_av", "s_avn", "s_avsp",
            "s_avld", "s_smm", "s_scp", "s_ssp", "s_sld0", "s_sld1",
            "s_xcl0", "s_xcl1", "s_fmm", "s_fb", "s_fo", "s_fg", "s_ost",
            "s_lbl", "s_bbl", "s_msl", "s_dvf",
        ]
        sm = {n: st.enter_context(nc.semaphore(n)) for n in semnames}
        block = st.enter_context(nc.Block())

        xtm = [xtm0, xtm1]
        strA = [sm["s_trA0"], sm["s_trA1"]]
        kvstg = [kvstg0, kvstg1]
        trstg = [trstg0, trstg1]
        skvtr = [sm["s_kvtr0"], sm["s_kvtr1"]]
        sqld = [sm["s_qld0"], sm["s_qld1"]]
        sstg = [sstg0, sstg1]
        gstg = [gstg0, gstg1]
        ostg = [ostg0, ostg1]
        xck = [xck0, xck1]
        lbs = [lbs0, lbs0]
        ssld = [sm["s_sld0"], sm["s_sld1"]]
        sxcl = [sm["s_xcl0"], sm["s_xcl1"]]

        dvf = [0]
        wq2 = lambda k: xnT[:, k, 0:512]
        wz2 = lambda k: xnT[:, k, 1024:1536]
        wq2_full = xnT[:, :, 0:512]
        wz2_full = xnT[:, :, 1024:1536]

        def fence(v, instop):
            dvf[0] += 1
            instop.then_inc(sm["s_dvf"], 1)
            v.wait_ge(sm["s_dvf"], dvf[0])

        def ropeviews(ps, tt, outbuf):
            psv = ps.rearrange("p (h u j) -> p h u j", h=4, u=2)
            pss = bass.AP(ps, 64, [[512, 128], [128, 4], [-64, 2], [1, 64]])
            cv = bass.AP(cosv, tt * 128,
                         [[NT * 128, 128], [0, 4], [64, 2], [1, 64]])
            sv = bass.AP(sinv, tt * 128,
                         [[NT * 128, 128], [0, 4], [64, 2], [1, 64]])
            ov = outbuf.rearrange("p (h u j) -> p h u j", h=4, u=2)
            rA = rtA.rearrange("p (h u j) -> p h u j", h=4, u=2)
            rB = rtB.rearrange("p (h u j) -> p h u j", h=4, u=2)
            return psv, pss, cv, sv, ov, rA, rB

        @block.sync
        def _(sy):
            sy.dma_start(out=brsb[:, :], in_=br).then_inc(sm["s_sml"], 16)
            sy.dma_start(out=trisb[:, :], in_=tri).then_inc(sm["s_sml"], 16)
            sy.dma_start(out=onesb[:, :], in_=one).then_inc(sm["s_sml"], 16)
            sy.dma_start(out=cosv[:, :], in_=chi).then_inc(sm["s_tbl"], 16)
            sy.dma_start(out=sinv[:, :], in_=shi).then_inc(sm["s_tbl"], 16)
            sy.wait_ge(sm["s_sml"], 48)
            sy.nop().then_inc(sm["s_ph"], 1)

        @block.gpsimd
        def _(g):
            g.dma_start(out=wrsb[:, :, :], in_=wr).then_inc(sm["s_sm2"], 16)
            g.wait_ge(sm["s_tbl"], 32)
            g.dma_start(out=cosv[:, :], in_=clo, accum_op=ALU.add
                        ).then_inc(sm["s_sm2"], 16)
            g.dma_start(out=sinv[:, :], in_=slo, accum_op=ALU.add
                        ).then_inc(sm["s_sm2"], 16)
            g.wait_ge(sm["s_sm2"], 48)
            g.nop().then_inc(sm["s_ph"], 1)

        @block.vector
        def _(v):
            v.memset(nb24[:, :], ESHIFT)
            v.nop().then_inc(sm["s_ph"], 1)

        @block.scalar
        def _(s):
            s.nop().then_inc(sm["s_ph"], 1)

        @block.tensor
        def _(t):
            t.nop().then_inc(sm["s_ph"], 1)

        @block.sync
        def _(sy):
            sy.wait_ge(sm["s_ph"], 5)
            HH = H // 2
            for t in range(NT):
                for j, gate in enumerate([
                        ("apl", 2 * t), ("stat", 2 * t + 1),
                        ("stat", 2 * t + 2), ("apl", 2 * t + 1)]):
                    nm, val = gate
                    if val > 0:
                        sy.wait_ge(sm["s_apl" if nm == "apl" else "s_stat"], val)
                    half = j % 2
                    sy.dma_start(out=xbuf[:, :],
                                 in_=xs[t * 128:(t + 1) * 128,
                                        half * HH:(half + 1) * HH]
                                 ).then_inc(sm["s_xld"], 16)
            sy.wait_ge(strA[0], 512 * 6)
            sy.wait_ge(strA[1], 512 * 6)
            sy.nop().then_inc(sm["s_ph"], 1)

        @block.scalar
        def _(s):
            s.wait_ge(sm["s_ph"], 5)
            for t in range(NT):
                pa = t % 2
                for half in range(2):
                    s.wait_ge(sm["s_xld"], 16 * (4 * t + half + 1))
                    s.activation(adum[:, :], xbuf[:, :], AF.Copy,
                                 accum_out=stats[:, t, 0 + half:1 + half])
                    s.activation(adum[:, :], xbuf[:, :], AF.Square,
                                 accum_out=stats[:, t, 2 + half:3 + half]
                                 ).then_inc(sm["s_stat"], 1)
                s.wait_ge(sm["s_rec"], t + 1)
                s.activation(stats[:, t, 6:7], stats[:, t, 5:6], AF.Sqrt
                             ).then_inc(sm["s_rsq"], 1)
                s.wait_ge(sm["s_apl"], 2 * t + 2)
                for k in range(NK):
                    s.dma_start_transpose(xnT[:, k, t * 128:(t + 1) * 128],
                                          xtm[pa][:, k * 128:(k + 1) * 128]
                                          ).then_inc(strA[pa], 16)
            s.wait_ge(strA[0], 512 * 6)
            s.wait_ge(strA[1], 512 * 6)
            s.nop().then_inc(sm["s_ph"], 1)

        @block.vector
        def _(v):
            v.wait_ge(sm["s_ph"], 5)
            HH = H // 2
            for t in range(NT):
                pa = t % 2
                v.wait_ge(sm["s_stat"], 2 * t + 2)
                fence(v, v.tensor_tensor(stats[:, t, 4:5], stats[:, t, 0:1],
                                          stats[:, t, 1:2], op=ALU.add))
                fence(v, v.tensor_tensor(stats[:, t, 5:6], stats[:, t, 2:3],
                                         stats[:, t, 3:4], op=ALU.add))
                fence(v, v.tensor_scalar(stats[:, t, 4:5], stats[:, t, 4:5],
                                         1.0 / H, None, op0=ALU.mult))
                fence(v, v.tensor_scalar(stats[:, t, 5:6], stats[:, t, 5:6],
                                         1.0 / H, None, op0=ALU.mult))
                fence(v, v.tensor_tensor(stats[:, t, 7:8], stats[:, t, 4:5],
                                         stats[:, t, 4:5], op=ALU.mult))
                fence(v, v.tensor_tensor(stats[:, t, 5:6], stats[:, t, 5:6],
                                         stats[:, t, 7:8], op=ALU.subtract))
                fence(v, v.tensor_scalar(stats[:, t, 5:6], stats[:, t, 5:6],
                                         1e-5, None, op0=ALU.add))
                v.reciprocal(stats[:, t, 5:6], stats[:, t, 5:6]
                             ).then_inc(sm["s_rec"], 1)
                v.wait_ge(sm["s_rsq"], t + 1)
                if t >= 2:
                    v.wait_ge(strA[pa], 512 * (t // 2))
                for half in range(2):
                    v.wait_ge(sm["s_xld"], 16 * (4 * t + 2 + half + 1))
                    v.tensor_scalar(xtm[pa][:, half * HH:(half + 1) * HH],
                                    xbuf[:, :], stats[:, t, 4:5],
                                    stats[:, t, 6:7], op0=ALU.subtract,
                                    op1=ALU.mult).then_inc(sm["s_apl"], 1)
            v.nop().then_inc(sm["s_ph"], 1)

        @block.tensor
        def _(t):
            t.wait_ge(sm["s_ph"], 5)
            t.nop().then_inc(sm["s_ph"], 1)

        @block.gpsimd
        def _(g):
            g.wait_ge(sm["s_ph"], 5)
            g.dma_start(out=wbuf[:, :, 0:128], in_=la).then_inc(sm["s_sm2"], 16)
            g.nop().then_inc(sm["s_ph"], 1)

        CMP = spx[:, 0:64].rearrange("p (i j) -> p i j", i=8)
        LG = spx[:, 64:72]
        KI = spx[:, 72:80]
        SI = spx[:, 80:88]
        KZ = spx[:, 88:96]
        CO = spx[:, 96:104]
        ZC = spx[:, 104:112]
        KS = spx[:, 112:113]
        TS = spx[:, 113:114]
        TM = spx[:, 114:115]
        RK = spx[:, 115:116]
        TAU = spx[:, 116:117]
        WW = spx[:, 120:128]

        @block.tensor
        def _(t):
            t.wait_ge(sm["s_ph"], 10)
            t.wait_ge(sm["s_sm2"], 64)
            for u in range(NU):
                tt = u + 4
                if u >= 1:
                    t.wait_ge(sm["s_spx"], u)
                    t.wait_ge(sm["s_hw"], u)
                for k in range(NK):
                    t.matmul(pb[0][:, 0:8], xnT[:, k, tt * 128:(tt + 1) * 128],
                             wrsb[:, k, :], start=(k == 0), stop=(k == NK - 1)
                             ).then_inc(sm["s_pmm"], 1) if k == NK - 1 else \
                        t.matmul(pb[0][:, 0:8], xnT[:, k, tt * 128:(tt + 1) * 128],
                                 wrsb[:, k, :], start=(k == 0), stop=False)
                for k in range(NK):
                    t.matmul(pb[1][:, 0:128], xnT[:, k, tt * 128:(tt + 1) * 128],
                             wbuf[:, k, 0:128], start=(k == 0), stop=(k == NK - 1)
                             ).then_inc(sm["s_pmm"], 1) if k == NK - 1 else \
                        t.matmul(pb[1][:, 0:128], xnT[:, k, tt * 128:(tt + 1) * 128],
                                 wbuf[:, k, 0:128], start=(k == 0), stop=False)
            t.nop().then_inc(sm["s_ph"], 1)

        @block.vector
        def _(v):
            v.wait_ge(sm["s_ph"], 10)
            for u in range(NU):
                v.wait_ge(sm["s_pmm"], 2 * u + 1)
                fence(v, v.tensor_tensor(LG, pb[0][:, 0:8], brsb[:, :], op=ALU.add))
                for i in range(8):
                    v.tensor_scalar(CMP[:, i, :], LG, LG[:, i:i + 1], None,
                                    op0=ALU.is_ge)
                fence(v, v.tensor_scalar(CMP[:, 7, :], LG, LG[:, 7:8], None,
                                         op0=ALU.is_ge))
                fence(v, v.tensor_reduce(KI, CMP, axis=AX.X, op=ALU.add))
                fence(v, v.tensor_tensor(CMP, CMP,
                                         bass.AP(spx, 64, [[128, 128], [0, 8], [1, 8]]),
                                         op=ALU.mult))
                fence(v, v.tensor_reduce(SI, CMP, axis=AX.X, op=ALU.add))
                fence(v, v.tensor_tensor(KZ, KI, LG, op=ALU.mult))
                fence(v, v.tensor_scalar(KZ, KZ, 1.0, None, op0=ALU.add))
                fence(v, v.tensor_tensor(CO, KZ, SI, op=ALU.is_gt))
                fence(v, v.tensor_tensor(ZC, LG, CO, op=ALU.mult))
                v.nop().then_inc(sm["s_spx"], 1)
                v.wait_ge(sm["s_ksum"], u + 1)
                fence(v, v.tensor_scalar(TM, TS, 1.0, None, op0=ALU.subtract))
                fence(v, v.reciprocal(RK, KS))
                fence(v, v.tensor_tensor(TAU, TM, RK, op=ALU.mult))
                fence(v, v.tensor_scalar(WW, LG, TAU, 0.0, op0=ALU.subtract,
                                         op1=ALU.max))
                v.wait_ge(sm["s_pmm"], 2 * u + 2)
                if u >= 1:
                    v.wait_ge(sm["s_hwT"], 16 * u)
                for e in range(8):
                    op = v.tensor_scalar(hwtm[:, e * 16:(e + 1) * 16],
                                         pb[1][:, e * 16:(e + 1) * 16],
                                         WW[:, e:e + 1], None, op0=ALU.mult)
                    if e == 7:
                        op.then_inc(sm["s_hw"], 1)
            v.nop().then_inc(sm["s_ph"], 1)

        @block.scalar
        def _(s):
            s.wait_ge(sm["s_ph"], 10)
            for u in range(NU):
                s.wait_ge(sm["s_spx"], u + 1)
                s.activation(spd[:, 0:8], CO, AF.Copy, accum_out=KS)
                s.activation(spd[:, 8:16], ZC, AF.Copy, accum_out=TS
                             ).then_inc(sm["s_ksum"], 1)
            s.nop().then_inc(sm["s_ph"], 1)

        @block.sync
        def _(sy):
            sy.wait_ge(sm["s_ph"], 10)
            for u in range(NU):
                sy.wait_ge(sm["s_hw"], u + 1)
                sy.dma_start_transpose(hwT[:, u * 128:(u + 1) * 128],
                                       hwtm[:, 0:128]).then_inc(sm["s_hwT"], 16)
            sy.wait_ge(sm["s_hwT"], 16 * NU)
            sy.nop().then_inc(sm["s_ph"], 1)

        @block.gpsimd
        def _(g):
            g.wait_ge(sm["s_ph"], 10)
            g.nop().then_inc(sm["s_ph"], 1)

        @block.gpsimd
        def _(g):
            g.wait_ge(sm["s_ph"], 15)
            for c in range(4):
                src = [wk, wk, wv, wv][c]
                oc = c % 2
                if c >= 1:
                    g.wait_ge(sm["s_kvmm"], c * NT)
                g.dma_start(out=wbuf[:, :, :], in_=src[oc]).then_inc(sm["s_wld"], 16)
            g.nop().then_inc(sm["s_ph"], 1)

        @block.tensor
        def _(t):
            t.wait_ge(sm["s_ph"], 15)
            for c in range(4):
                t.wait_ge(sm["s_wld"], 16 * (c + 1))
                for tt in range(NT):
                    it = c * NT + tt
                    if it >= 2:
                        t.wait_ge(sm["s_kstg"], it - 1)
                    for k in range(NK):
                        mmop = t.matmul(pb[it % 2][:, :],
                                        xnT[:, k, tt * 128:(tt + 1) * 128],
                                        wbuf[:, k, :], start=(k == 0),
                                        stop=(k == NK - 1))
                        if k == NK - 1:
                            mmop.then_inc(sm["s_kvmm"], 1)
            t.nop().then_inc(sm["s_ph"], 1)

        @block.vector
        def _(v):
            v.wait_ge(sm["s_ph"], 15)
            for c in range(4):
                isk = c < 2
                for tt in range(NT):
                    it = c * NT + tt
                    v.wait_ge(sm["s_kvmm"], it + 1)
                    if it >= 2:
                        if it - 2 < 24:
                            v.wait_ge(skvtr[it % 2], 64 * ((it - 2) // 2 + 1))
                        else:
                            v.wait_ge(sm["s_kvsp"], 16 * (it - 1))
                    if isk:
                        psv, pss, cv, sv, ov, rA, rB = ropeviews(
                            pb[it % 2], tt, kvstg[it % 2])
                        v.tensor_tensor(rA, psv, cv, op=ALU.mult)
                        fence(v, v.tensor_tensor(rB, pss, sv, op=ALU.mult))
                        v.tensor_tensor(ov, rA, rB, op=ALU.add
                                        ).then_inc(sm["s_kstg"], 1)
                    else:
                        v.tensor_copy(kvstg[it % 2][:, :], pb[it % 2][:, :]
                                      ).then_inc(sm["s_kstg"], 1)
            v.nop().then_inc(sm["s_ph"], 1)

        @block.scalar
        def _(s):
            s.wait_ge(sm["s_ph"], 15)
            for c in range(2):
                for tt in range(NT):
                    it = c * NT + tt
                    s.wait_ge(sm["s_kstg"], it + 1)
                    if it >= 2:
                        s.wait_ge(sm["s_kvsp"], 16 * (it - 1))
                    for j in range(4):
                        s.dma_start_transpose(trstg[it % 2][:, j * 128:(j + 1) * 128],
                                              kvstg[it % 2][:, j * 128:(j + 1) * 128]
                                              ).then_inc(skvtr[it % 2], 16)
            s.nop().then_inc(sm["s_ph"], 1)

        @block.sync
        def _(sy):
            sy.wait_ge(sm["s_ph"], 15)
            for c in range(4):
                isk = c < 2
                oc = c % 2
                for tt in range(NT):
                    it = c * NT + tt
                    if isk:
                        sy.wait_ge(skvtr[it % 2], 64 * (it // 2 + 1))
                        sy.dma_start(
                            out=kts[4 * oc:4 * oc + 4, tt, :, :].transpose([1, 0, 2]),
                            in_=trstg[it % 2].rearrange("p (j q) -> p j q", j=4)
                        ).then_inc(sm["s_kvsp"], 16)
                    else:
                        sy.wait_ge(sm["s_kstg"], it + 1)
                        sy.dma_start(
                            out=vts[4 * oc:4 * oc + 4, tt, :, :].transpose([1, 0, 2]),
                            in_=kvstg[it % 2].rearrange("p (j q) -> p j q", j=4)
                        ).then_inc(sm["s_kvsp"], 16)
            sy.wait_ge(sm["s_kvsp"], 16 * 48)
            sy.nop().then_inc(sm["s_ph"], 1)

        @block.gpsimd
        def _(g):
            g.wait_ge(sm["s_ph"], 20)
            for oc in range(8):
                if oc >= 2:
                    g.wait_ge(sm["s_kvmm"], 48 + (oc - 1) * NU)
                dst = wbuf[:, :, :] if oc % 2 == 0 else wq2_full
                g.dma_start(out=dst, in_=wq[oc]).then_inc(
                    sm["s_wld"] if oc % 2 == 0 else sm["s_wld1"], 16)
            g.nop().then_inc(sm["s_ph"], 1)

        @block.tensor
        def _(t):
            t.wait_ge(sm["s_ph"], 20)
            for oc in range(8):
                if oc % 2 == 0:
                    t.wait_ge(sm["s_wld"], 16 * (4 + oc // 2 + 1))
                else:
                    t.wait_ge(sm["s_wld1"], 16 * (oc // 2 + 1))
                for u in range(NU):
                    tt = u + 4
                    it = oc * NU + u
                    if it >= 2:
                        t.wait_ge(sm["s_kstg"], 48 + it - 1)
                    for k in range(NK):
                        rhsw = wbuf[:, k, :] if oc % 2 == 0 else wq2(k)
                        mmop = t.matmul(pb[it % 2][:, :],
                                        xnT[:, k, tt * 128:(tt + 1) * 128],
                                        rhsw, start=(k == 0),
                                        stop=(k == NK - 1))
                        if k == NK - 1:
                            mmop.then_inc(sm["s_kvmm"], 1)
            t.nop().then_inc(sm["s_ph"], 1)

        @block.vector
        def _(v):
            v.wait_ge(sm["s_ph"], 20)
            for oc in range(8):
                for u in range(NU):
                    tt = u + 4
                    it = oc * NU + u
                    v.wait_ge(sm["s_kvmm"], 48 + it + 1)
                    if it >= 2:
                        v.wait_ge(skvtr[it % 2], 768 + 64 * ((it - 2) // 2 + 1))
                    psv, pss, cv, sv, ov, rA, rB = ropeviews(
                        pb[it % 2], tt, kvstg[it % 2])
                    v.tensor_tensor(rA, psv, cv, op=ALU.mult)
                    fence(v, v.tensor_tensor(rB, pss, sv, op=ALU.mult))
                    v.tensor_tensor(ov, rA, rB, op=ALU.add).then_inc(sm["s_kstg"], 1)
            v.nop().then_inc(sm["s_ph"], 1)

        @block.scalar
        def _(s):
            s.wait_ge(sm["s_ph"], 20)
            for oc in range(8):
                for u in range(NU):
                    it = oc * NU + u
                    s.wait_ge(sm["s_kstg"], 48 + it + 1)
                    if it >= 2:
                        s.wait_ge(sm["s_kvsp"], 16 * (48 + it - 1))
                    for j in range(4):
                        s.dma_start_transpose(trstg[it % 2][:, j * 128:(j + 1) * 128],
                                              kvstg[it % 2][:, j * 128:(j + 1) * 128]
                                              ).then_inc(skvtr[it % 2], 16)
            s.nop().then_inc(sm["s_ph"], 1)

        @block.sync
        def _(sy):
            sy.wait_ge(sm["s_ph"], 20)
            for oc in range(8):
                for u in range(NU):
                    it = oc * NU + u
                    sy.wait_ge(skvtr[it % 2], 768 + 64 * (it // 2 + 1))
                    sy.dma_start(
                        out=qts[4 * oc:4 * oc + 4, u, :, :].transpose([1, 0, 2]),
                        in_=trstg[it % 2].rearrange("p (j q) -> p j q", j=4)
                    ).then_inc(sm["s_kvsp"], 16)
            sy.wait_ge(sm["s_kvsp"], 16 * (48 + 64))
            for g4 in range(4):
                sy.dma_start(out=xns[:, g4 * 8:(g4 + 1) * 8, :],
                             in_=xnT[:, g4 * 8:(g4 + 1) * 8, 512:NT * 128]
                             ).then_inc(sm["s_xnsp"], 16)
            sy.wait_ge(sm["s_xnsp"], 64)
            sy.nop().then_inc(sm["s_ph"], 1)

        @block.gpsimd
        def _(g):
            g.wait_ge(sm["s_ph"], 25)
            g.dma_start(out=wbuf[:, 0:8, :], in_=mlo).then_inc(sm["s_msl"], 16)
            g.nop().then_inc(sm["s_ph"], 1)

        @block.sync
        def _(sy):
            sy.wait_ge(sm["s_ph"], 25)
            for h in range(NQH):
                if h >= 2:
                    sy.wait_ge(sm["s_sc"], ((h - 2) * 2 + 2) * 8)
                sy.dma_start(out=qh[h % 2].rearrange("p (u q) -> p u q", u=8),
                             in_=qts[h].transpose([1, 0, 2])
                             ).then_inc(sqld[h % 2], 16)
                if h % 4 == 0:
                    if h >= 4:
                        sy.wait_ge(sm["s_av"], 2 * h)
                    sy.dma_start(out=kh.rearrange("p (tt q) -> p tt q", tt=NT),
                                 in_=kts[h // 4].transpose([1, 0, 2])
                                 ).then_inc(sm["s_kvld"], 16)
                    sy.dma_start(out=vh.rearrange("p (tt q) -> p tt q", tt=NT),
                                 in_=vts[h // 4].transpose([1, 0, 2])
                                 ).then_inc(sm["s_kvld"], 16)
                sy.wait_ge(sm["s_avn"], 2 * h + 2)
                sy.dma_start(out=avs[h][:, 0:512], in_=kvstg0[:, :]
                             ).then_inc(sm["s_avsp"], 16)
                sy.dma_start(out=avs[h][:, 512:1024], in_=kvstg1[:, :]
                             ).then_inc(sm["s_avsp"], 16)
            sy.wait_ge(sm["s_avsp"], 32 * NQH)
            sy.nop().then_inc(sm["s_ph"], 1)

        @block.tensor
        def _(t):
            t.wait_ge(sm["s_ph"], 25)
            for h in range(NQH):
                t.wait_ge(sqld[h % 2], 16 * (h // 2 + 1))
                t.wait_ge(sm["s_kvld"], 32 * (h // 4 + 1))
                for qc in range(2):
                    it = h * 2 + qc
                    for r in range(8):
                        kb = 4 * qc + r
                        lo, hi = _clip(r)
                        if it * 8 + r >= 2:
                            t.wait_ge(sm["s_exp"], it * 8 + r - 1)
                        t.matmul(pb[r % 2][:, lo:hi],
                                 kh[:, kb * 128:(kb + 1) * 128],
                                 qh[h % 2][:, qc * 512 + lo:qc * 512 + hi],
                                 start=True, stop=True).then_inc(sm["s_sc"], 1)
                    t.wait_ge(sm["s_msk"], (it + 1) * 8)
                    if it >= 1:
                        t.wait_ge(sm["s_avn"], it)
                    order = [4, 0, 1, 2, 3, 5, 6, 7]
                    for idx, r in enumerate(order):
                        kb = 4 * qc + r
                        lo, hi = _clip(r)
                        t.matmul(pb[2][:, lo:hi], vh[:, kb * 128:(kb + 1) * 128],
                                 exp_sb[:, r, lo:hi], start=(idx == 0),
                                 stop=(idx == 7))
                    for idx, r in enumerate(order):
                        lo, hi = _clip(r)
                        mmop = t.matmul(pb[3][:, lo:hi], onesb[:, :],
                                        exp_sb[:, r, lo:hi], start=(idx == 0),
                                        stop=(idx == 7))
                        if idx == 7:
                            mmop.then_inc(sm["s_av"], 1)
            t.nop().then_inc(sm["s_ph"], 1)

        @block.scalar
        def _(s):
            s.wait_ge(sm["s_ph"], 25)
            for h in range(NQH):
                for qc in range(2):
                    it = h * 2 + qc
                    for r in range(8):
                        lo, hi = _clip(r)
                        s.wait_ge(sm["s_sc"], it * 8 + r + 1)
                        if r == 0 and it >= 1:
                            s.wait_ge(sm["s_av"], it)
                        s.activation(exp_sb[:, r, lo:hi], pb[r % 2][:, lo:hi],
                                     AF.Exp, bias=nb24[:, 0:1], scale=SCALE
                                     ).then_inc(sm["s_exp"], 1)
            s.nop().then_inc(sm["s_ph"], 1)

        @block.vector
        def _(v):
            v.wait_ge(sm["s_ph"], 25)
            v.wait_ge(sm["s_msl"], 16)
            for h in range(NQH):
                for qc in range(2):
                    it = h * 2 + qc
                    for r in range(8):
                        lo, hi = _clip(r)
                        v.wait_ge(sm["s_exp"], it * 8 + r + 1)
                        if r < 4:
                            v.tensor_tensor(exp_sb[:, r, lo:hi],
                                            exp_sb[:, r, lo:hi],
                                            wbuf[:, 4 * qc + r, lo:hi],
                                            op=ALU.mult).then_inc(sm["s_msk"], 1)
                        else:
                            d0 = 128 * (r - 4)
                            v.tensor_tensor(exp_sb[:, r, d0:d0 + 128],
                                            exp_sb[:, r, d0:d0 + 128],
                                            trisb[:, :], op=ALU.mult
                                            ).then_inc(sm["s_msk"], 1)
                    v.wait_ge(sm["s_av"], it + 1)
                    if qc == 0 and h >= 1:
                        v.wait_ge(sm["s_avsp"], 32 * h)
                    fence(v, v.reciprocal(den_r[:, :], pb[3][:, :]))
                    v.tensor_tensor(avh[qc][:, :], pb[2][:, :], den_r[:, :],
                                    op=ALU.mult).then_inc(sm["s_avn"], 1)
            v.nop().then_inc(sm["s_ph"], 1)

        @block.gpsimd
        def _(g):
            g.wait_ge(sm["s_ph"], 30)
            for oc in range(8):
                if oc >= 2:
                    g.wait_ge(sm["s_smm"], (oc - 1) * NU)
                dst = wbuf[:, :, :] if oc % 2 == 0 else wz2_full
                g.dma_start(out=dst, in_=wo[oc]).then_inc(
                    sm["s_wld"] if oc % 2 == 0 else sm["s_wld1"], 16)
            g.nop().then_inc(sm["s_ph"], 1)

        @block.sync
        def _(sy):
            sy.wait_ge(sm["s_ph"], 30)
            for h in range(NQH):
                sy.dma_start(out=xnT[:, h, 0:1024], in_=avs[h]
                             ).then_inc(sm["s_avld"], 16)
            for oc in range(8):
                for u in range(NU):
                    it = oc * NU + u
                    sy.wait_ge(sm["s_scp"], it + 1)
                    sy.dma_start(out=ss[u][:, oc * 512:(oc + 1) * 512],
                                 in_=sstg[it % 2][:, :]).then_inc(sm["s_ssp"], 16)
            sy.wait_ge(sm["s_ssp"], 16 * 64)
            sy.nop().then_inc(sm["s_ph"], 1)

        @block.tensor
        def _(t):
            t.wait_ge(sm["s_ph"], 30)
            t.wait_ge(sm["s_avld"], 16 * NQH)
            for oc in range(8):
                if oc % 2 == 0:
                    t.wait_ge(sm["s_wld"], 16 * (8 + oc // 2 + 1))
                else:
                    t.wait_ge(sm["s_wld1"], 16 * (4 + oc // 2 + 1))
                for u in range(NU):
                    it = oc * NU + u
                    if it >= 2:
                        t.wait_ge(sm["s_scp"], it - 1)
                    for k in range(NK):
                        rhsw = wbuf[:, k, :] if oc % 2 == 0 else wz2(k)
                        mmop = t.matmul(pb[u % 2][:, :], xnT[:, k, u * 128:(u + 1) * 128],
                                        rhsw, start=(k == 0),
                                        stop=(k == NK - 1))
                        if k == NK - 1:
                            mmop.then_inc(sm["s_smm"], 1)
            t.nop().then_inc(sm["s_ph"], 1)

        @block.scalar
        def _(s):
            s.wait_ge(sm["s_ph"], 30)
            for oc in range(8):
                for u in range(NU):
                    it = oc * NU + u
                    s.wait_ge(sm["s_smm"], it + 1)
                    if it >= 2:
                        s.wait_ge(sm["s_ssp"], 16 * (it - 1))
                    s.activation(sstg[it % 2][:, :], pb[u % 2][:, :], AF.Copy
                                 ).then_inc(sm["s_scp"], 1)
            s.nop().then_inc(sm["s_ph"], 1)

        @block.vector
        def _(v):
            v.wait_ge(sm["s_ph"], 30)
            v.nop().then_inc(sm["s_ph"], 1)

        @block.gpsimd
        def _(g):
            g.wait_ge(sm["s_ph"], 35)
            for oc in range(8):
                if oc >= 2:
                    g.wait_ge(sm["s_fmm"], (oc - 1) * NU)
                dst = wbuf[:, :, :] if oc % 2 == 0 else wz2_full
                g.dma_start(out=dst, in_=wb[oc]).then_inc(
                    sm["s_wld"] if oc % 2 == 0 else sm["s_wld1"], 16)
                if oc >= 1:
                    g.wait_ge(sm["s_fmm"], oc * NU)
                g.dma_start(out=lbs[oc % 2][:, :], in_=lb[:, oc * 512:(oc + 1) * 512]
                            ).then_inc(sm["s_lbl"], 16)
            g.nop().then_inc(sm["s_ph"], 1)

        @block.sync
        def _(sy):
            sy.wait_ge(sm["s_ph"], 35)
            for g4 in range(4):
                sy.dma_start(out=xnT[:, g4 * 8:(g4 + 1) * 8, 0:1024],
                             in_=xns[:, g4 * 8:(g4 + 1) * 8, :]
                             ).then_inc(sm["s_avld"], 16)
            for oc in range(8):
                if oc >= 1:
                    sy.wait_ge(sm["s_fb"], oc * NU)
                sy.dma_start(out=bbs[:, :], in_=bb[:, oc * 512:(oc + 1) * 512]
                             ).then_inc(sm["s_bbl"], 16)
                for u in range(NU):
                    it = oc * NU + u
                    if it >= 2:
                        sy.wait_ge(sm["s_fg"], it - 1)
                    sy.dma_start(out=sstg[it % 2][:, :],
                                 in_=ss[u][:, oc * 512:(oc + 1) * 512]
                                 ).then_inc(ssld[it % 2], 16)
                    sy.dma_start(out=xck[it % 2][:, :],
                                 in_=xs[(u + 4) * 128:(u + 5) * 128,
                                        oc * 512:(oc + 1) * 512]
                                 ).then_inc(sxcl[it % 2], 16)
                    sy.wait_ge(sm["s_fo"], it + 1)
                    sy.dma_start(out=os_[u * 128:(u + 1) * 128,
                                         oc * 512:(oc + 1) * 512],
                                 in_=ostg[it % 2][:, :]).then_inc(sm["s_ost"], 16)
            sy.wait_ge(sm["s_ost"], 16 * 64)

        @block.tensor
        def _(t):
            t.wait_ge(sm["s_ph"], 35)
            t.wait_ge(sm["s_avld"], 16 * (NQH + 4))
            for oc in range(8):
                if oc % 2 == 0:
                    t.wait_ge(sm["s_wld"], 16 * (12 + oc // 2 + 1))
                else:
                    t.wait_ge(sm["s_wld1"], 16 * (8 + oc // 2 + 1))
                t.wait_ge(sm["s_lbl"], 16 * (oc + 1))
                for u in range(NU):
                    it = oc * NU + u
                    if it >= 2:
                        t.wait_ge(sm["s_scp"], 64 + it - 1)
                    for k in range(NK):
                        rhsw = wbuf[:, k, :] if oc % 2 == 0 else wz2(k)
                        t.matmul(pb[u % 2][:, :], xnT[:, k, u * 128:(u + 1) * 128],
                                 rhsw, start=(k == 0), stop=False)
                    t.matmul(pb[u % 2][:, :], hwT[:, u * 128:(u + 1) * 128],
                             lbs[oc % 2][:, :], start=False, stop=True
                             ).then_inc(sm["s_fmm"], 1)
            t.nop().then_inc(sm["s_ph"], 1)

        @block.vector
        def _(v):
            v.wait_ge(sm["s_ph"], 35)
            for oc in range(8):
                for u in range(NU):
                    it = oc * NU + u
                    v.wait_ge(sm["s_fmm"], it + 1)
                    v.wait_ge(sm["s_bbl"], 16 * (oc + 1))
                    v.tensor_tensor(pb[u % 2][:, :], pb[u % 2][:, :], bbs[:, :],
                                    op=ALU.add).then_inc(sm["s_fb"], 1)
                    v.wait_ge(sm["s_scp"], 64 + it + 1)
                    v.wait_ge(ssld[it % 2], 16 * (it // 2 + 1))
                    v.wait_ge(sxcl[it % 2], 16 * (it // 2 + 1))
                    if it >= 2:
                        v.wait_ge(sm["s_ost"], 16 * (it - 1))
                    fence(v, v.tensor_tensor(ostg[it % 2][:, :], sstg[it % 2][:, :],
                                              gstg[it % 2][:, :], op=ALU.mult))
                    v.tensor_tensor(ostg[it % 2][:, :], ostg[it % 2][:, :],
                                    xck[it % 2][:, :], op=ALU.add
                                    ).then_inc(sm["s_fo"], 1)
                    v.nop().then_inc(sm["s_fg"], 1)

        @block.scalar
        def _(s):
            s.wait_ge(sm["s_ph"], 35)
            for oc in range(8):
                for u in range(NU):
                    it = oc * NU + u
                    s.wait_ge(sm["s_fb"], it + 1)
                    if it >= 2:
                        s.wait_ge(sm["s_fg"], it - 1)
                    s.activation(gstg[it % 2][:, :], pb[u % 2][:, :], AF.Silu
                                 ).then_inc(sm["s_scp"], 1)

    return nc


def _tables(h_half):
    theta = 10000.0
    inv = 1.0 / theta ** (np.arange(0, 128, 2, dtype=np.float64) / 128.0)
    loc = np.arange(NT * 128)
    if h_half == 0:
        pos = np.where(loc < HALO, 0, loc - HALO).astype(np.float64)
    else:
        pos = (loc + HALO).astype(np.float64)
    ang = pos[:, None] * inv[None, :]
    cos64 = np.cos(ang)
    sin64 = np.sin(ang)
    cosT = np.concatenate([cos64, cos64], axis=1)
    sinA = np.concatenate([-sin64, sin64], axis=1)
    out = []
    for tab in (cosT, sinA):
        hi = tab.astype(np.float32)
        lo = (tab - hi.astype(np.float64)).astype(np.float32)
        out.append(np.ascontiguousarray(
            hi.reshape(NT, 128, 128).transpose(1, 0, 2).reshape(128, NT * 128)))
        out.append(np.ascontiguousarray(
            lo.reshape(NT, 128, 128).transpose(1, 0, 2).reshape(128, NT * 128)))
    return out


def _masks(h_half):
    m = np.zeros((128, 8, 512), dtype=np.float32)
    kc = np.arange(128)[:, None]
    qc = np.arange(512)[None, :]
    for qch in range(2):
        for r in range(4):
            if h_half == 0 and qch == 0:
                continue
            lo, hi = _clip(r)
            allow = (qc <= r * 128 + kc) & (qc < hi)
            m[:, qch * 4 + r, :] = allow.astype(np.float32)
    return m.astype(ml_dtypes.bfloat16)


def _w_tiles(w, noc):
    return np.ascontiguousarray(
        w.reshape(NK, 128, noc, 512).transpose(2, 1, 0, 3))


def kernel(L0_attn_hidden_states, L0_attn_Wq, L0_attn_Wk, L0_attn_Wv, L0_attn_Wo,
           L0_ffn_W_base, L0_ffn_b_base, L0_ffn_W_router, L0_ffn_b_router,
           L0_ffn_lora_A, L0_ffn_lora_B):
    x = np.ascontiguousarray(L0_attn_hidden_states, dtype=np.float32)
    wq_h = _w_tiles(np.ascontiguousarray(L0_attn_Wq, dtype=np.float32), 8)
    wk_h = _w_tiles(np.ascontiguousarray(L0_attn_Wk, dtype=np.float32), 2)
    wv_h = _w_tiles(np.ascontiguousarray(L0_attn_Wv, dtype=np.float32), 2)
    wo_h = _w_tiles(np.ascontiguousarray(L0_attn_Wo, dtype=np.float32), 8)
    wb_h = _w_tiles(np.ascontiguousarray(L0_ffn_W_base.T, dtype=np.float32), 8)
    wr_h = np.ascontiguousarray(
        np.asarray(L0_ffn_W_router, dtype=np.float32).T.reshape(NK, 128, 8)
        .transpose(1, 0, 2))
    la_flat = np.asarray(L0_ffn_lora_A, dtype=np.float32).transpose(1, 0, 2).reshape(H, 128)
    la_h = np.ascontiguousarray(la_flat.reshape(NK, 128, 128).transpose(1, 0, 2))
    lb_h = np.ascontiguousarray(np.asarray(L0_ffn_lora_B, dtype=np.float32).reshape(128, H))
    bb_h = np.ascontiguousarray(
        np.broadcast_to(np.asarray(L0_ffn_b_base, dtype=np.float32), (128, H)))
    br_h = np.ascontiguousarray(
        np.broadcast_to(np.asarray(L0_ffn_b_router, dtype=np.float32), (128, 8)))
    tri_h = (np.arange(128)[:, None] <= np.arange(128)[None, :]
             ).astype(ml_dtypes.bfloat16)
    one_h = np.ones((128, 128), dtype=ml_dtypes.bfloat16)

    if "nc" not in _NC_CACHE:
        _NC_CACHE["nc"] = build_program()
    nc = _NC_CACHE["nc"]

    tabs = [_tables(0), _tables(1)]
    msks = [_masks(0), _masks(1)]
    in_maps = []
    for c in range(N_CORES):
        b, hh = c // 2, c % 2
        if hh == 0:
            xsl = np.concatenate([np.zeros((HALO, H), np.float32), x[b, 0:1024]],
                                 axis=0)
        else:
            xsl = np.ascontiguousarray(x[b, HALO:2048])
        chi_h, clo_h, shi_h, slo_h = tabs[hh]
        in_maps.append({
            "xs": xsl, "wq": wq_h, "wk": wk_h, "wv": wv_h, "wo": wo_h,
            "wb": wb_h, "wr": wr_h, "la": la_h, "lb": lb_h, "bb": bb_h,
            "br": br_h, "chi": chi_h, "clo": clo_h, "shi": shi_h, "slo": slo_h,
            "mlo": msks[hh], "tri": tri_h, "one": one_h,
        })
    res = run_bass_kernel_spmd(nc, in_maps, core_ids=list(range(N_CORES)))
    out = np.empty((B, S, H), dtype=np.float32)
    for c in range(N_CORES):
        b, hh = c // 2, c % 2
        out[b, hh * 1024:(hh + 1) * 1024] = res.results[c]["os_"]
    return out


# version 05
# speedup vs baseline: 1.3168x; 1.2092x over previous
import contextlib
import math

import numpy as np
import ml_dtypes

import concourse.bass as bass
import concourse.mybir as mybir
from concourse.bass_utils import run_bass_kernel_spmd

F32 = mybir.dt.float32
BF16 = mybir.dt.bfloat16
AF = mybir.ActivationFunctionType
ALU = mybir.AluOpType
AX = mybir.AxisListType

H = 4096
S = 2048
B = 4
NT = 12
NU = 8
HALO = 512
NQH = 32
NKV = 8
NK = 32
SCALE = 1.0 / math.sqrt(128.0)
ESHIFT = -24.0
N_CORES = 8

_NC_CACHE = {}


def _clip(r):
    if r < 4:
        return 0, 128 * (r + 1)
    return 128 * (r - 4), 512


def build_program():
    nc = bass.Bass("TRN2", target_bir_lowering=False)

    xs = nc.dram_tensor("xs", [NT * 128, H], F32, kind="ExternalInput").ap()
    wq = nc.dram_tensor("wq", [8, 128, NK, 512], F32, kind="ExternalInput").ap()
    wk = nc.dram_tensor("wk", [2, 128, NK, 512], F32, kind="ExternalInput").ap()
    wv = nc.dram_tensor("wv", [2, 128, NK, 512], F32, kind="ExternalInput").ap()
    wo = nc.dram_tensor("wo", [8, 128, NK, 512], F32, kind="ExternalInput").ap()
    wb = nc.dram_tensor("wb", [8, 128, NK, 512], F32, kind="ExternalInput").ap()
    wr = nc.dram_tensor("wr", [128, NK, 8], F32, kind="ExternalInput").ap()
    la = nc.dram_tensor("la", [128, NK, 128], F32, kind="ExternalInput").ap()
    lb = nc.dram_tensor("lb", [128, H], F32, kind="ExternalInput").ap()
    bb = nc.dram_tensor("bb", [128, H], F32, kind="ExternalInput").ap()
    br = nc.dram_tensor("br", [128, 8], F32, kind="ExternalInput").ap()
    chi = nc.dram_tensor("chi", [128, NT * 128], F32, kind="ExternalInput").ap()
    clo = nc.dram_tensor("clo", [128, NT * 128], F32, kind="ExternalInput").ap()
    shi = nc.dram_tensor("shi", [128, NT * 128], F32, kind="ExternalInput").ap()
    slo = nc.dram_tensor("slo", [128, NT * 128], F32, kind="ExternalInput").ap()
    mlo = nc.dram_tensor("mlo", [128, 8, 512], BF16, kind="ExternalInput").ap()
    tri = nc.dram_tensor("tri", [128, 128], BF16, kind="ExternalInput").ap()
    one = nc.dram_tensor("one", [128, 128], BF16, kind="ExternalInput").ap()
    idn = nc.dram_tensor("idn", [128, 128], BF16, kind="ExternalInput").ap()
    os_ = nc.dram_tensor("os_", [NU * 128, H], F32, kind="ExternalOutput").ap()
    kts = nc.dram_tensor("kts", [NKV, NT, 128, 128], BF16, kind="Internal").ap()
    vts = nc.dram_tensor("vts", [NKV, NT, 128, 128], BF16, kind="Internal").ap()
    qts = nc.dram_tensor("qts", [NQH, NU, 128, 128], BF16, kind="Internal").ap()
    avs = nc.dram_tensor("avs", [NQH, 128, 1024], BF16, kind="Internal").ap()
    xns = nc.dram_tensor("xns", [128, NK, 1024], BF16, kind="Internal").ap()
    ss = nc.dram_tensor("ss", [NU, 128, H], F32, kind="Internal").ap()

    st = contextlib.ExitStack()
    with st:
        sb = lambda *a, **k: st.enter_context(nc.sbuf_tensor(*a, **k))
        xnT = sb("xnT", [128, NK, NT * 128], BF16, is_transpose_dest=True)
        wbuf = sb("wbuf", [128, NK, 512], BF16)
        xbuf = sb("xbuf", [128, H // 2], F32)
        adum = sb("adum", [128, H // 2], F32)
        xtm0 = sb("xtm0", [128, H], BF16)
        xtm1 = sb("xtm1", [128, H], BF16)
        sqsc = sb("sqsc", [128, NT * 128], BF16)
        cosv = sb("cosv", [128, NT * 128], F32)
        sinv = sb("sinv", [128, NT * 128], F32)
        stats = sb("stats", [128, NT, 8], F32)
        spx = sb("spx", [128, 128], F32)
        spd = sb("spd", [128, 16], F32)
        hwT = sb("hwT", [128, NU * 128], BF16, is_transpose_dest=True)
        hwtm = sb("hwtm", [128, 128], BF16)
        wrsb = sb("wrsb", [128, NK, 8], BF16)
        brsb = sb("brsb", [128, 8], F32)
        trisb = sb("trisb", [128, 128], BF16)
        onesb = sb("onesb", [128, 128], BF16)
        nb24 = sb("nb24", [128, 1], F32)
        idnsb = sb("idnsb", [128, 128], BF16)
        kvstg0 = sb("kvstg0", [128, 512], BF16)
        kvstg1 = sb("kvstg1", [128, 512], BF16)
        trstg0 = sb("trstg0", [128, 512], BF16, is_transpose_dest=True)
        trstg1 = sb("trstg1", [128, 512], BF16, is_transpose_dest=True)
        sstg0 = sb("sstg0", [128, 512], F32)
        sstg1 = sb("sstg1", [128, 512], F32)
        gstg0 = sb("gstg0", [128, 512], F32)
        gstg1 = sb("gstg1", [128, 512], F32)
        ostg0 = sb("ostg0", [128, 512], F32)
        ostg1 = sb("ostg1", [128, 512], F32)
        xck0 = sb("xck0", [128, 512], F32)
        xck1 = sb("xck1", [128, 512], F32)
        bbs = sb("bbs", [128, 512], F32)
        lbs0 = sb("lbs0", [128, 512], BF16)

        exp_sb = xtm0.rearrange("p (r q) -> p r q", r=8)
        qh = [xtm1[:, 0:1024], xtm1[:, 1024:2048]]
        kh = xtm1[:, 2048:2048 + NT * 128]
        vh = sqsc
        avh = [kvstg0, kvstg1]
        den_r = gstg0
        rtA, rtB = sstg0, sstg1

        pb = [st.enter_context(nc.psum_tensor(f"pb{i}", [128, 512], F32))
              for i in range(6)]
        pt = [st.enter_context(nc.psum_tensor(f"pt{i}", [128, 128], BF16))
              for i in range(2)]

        semnames = [
            "s_ph", "s_xld", "s_stat", "s_rec", "s_rsq", "s_apl",
            "s_trA0", "s_trA1", "s_sml", "s_sm2", "s_tbl", "s_pmm", "s_spx",
            "s_ksum", "s_hw", "s_hwT", "s_wld", "s_wld1", "s_kvmm", "s_kstg",
            "s_kvtr0", "s_kvtr1", "s_kvsp", "s_xnsp", "s_qld0", "s_qld1",
            "s_kvld", "s_sc", "s_exp", "s_msk", "s_av", "s_avn", "s_avsp",
            "s_avld", "s_smm", "s_scp", "s_ssp", "s_sld0", "s_sld1",
            "s_xcl0", "s_xcl1", "s_fmm", "s_fb", "s_fo", "s_fg", "s_ost",
            "s_lbl", "s_bbl", "s_msl", "s_dvf", "s_ttr", "s_tcp",
        ]
        sm = {n: st.enter_context(nc.semaphore(n)) for n in semnames}
        block = st.enter_context(nc.Block())

        xtm = [xtm0, xtm1]
        strA = [sm["s_trA0"], sm["s_trA1"]]
        kvstg = [kvstg0, kvstg1]
        trstg = [trstg0, trstg1]
        skvtr = [sm["s_kvtr0"], sm["s_kvtr1"]]
        sqld = [sm["s_qld0"], sm["s_qld1"]]
        sstg = [sstg0, sstg1]
        gstg = [gstg0, gstg1]
        ostg = [ostg0, ostg1]
        xck = [xck0, xck1]
        lbs = [lbs0, lbs0]
        ssld = [sm["s_sld0"], sm["s_sld1"]]
        sxcl = [sm["s_xcl0"], sm["s_xcl1"]]

        dvf = [0]
        wq2 = lambda k: xnT[:, k, 0:512]
        wz2 = lambda k: xnT[:, k, 1024:1536]
        wq2_full = xnT[:, :, 0:512]
        wz2_full = xnT[:, :, 1024:1536]

        def fence(v, instop):
            dvf[0] += 1
            instop.then_inc(sm["s_dvf"], 1)
            v.wait_ge(sm["s_dvf"], dvf[0])

        def ropeviews(ps, tt, outbuf):
            psv = ps.rearrange("p (h u j) -> p h u j", h=4, u=2)
            pss = bass.AP(ps, 64, [[512, 128], [128, 4], [-64, 2], [1, 64]])
            cv = bass.AP(cosv, tt * 128,
                         [[NT * 128, 128], [0, 4], [64, 2], [1, 64]])
            sv = bass.AP(sinv, tt * 128,
                         [[NT * 128, 128], [0, 4], [64, 2], [1, 64]])
            ov = outbuf.rearrange("p (h u j) -> p h u j", h=4, u=2)
            rA = rtA.rearrange("p (h u j) -> p h u j", h=4, u=2)
            rB = rtB.rearrange("p (h u j) -> p h u j", h=4, u=2)
            return psv, pss, cv, sv, ov, rA, rB

        @block.sync
        def _(sy):
            sy.dma_start(out=brsb[:, :], in_=br).then_inc(sm["s_sml"], 16)
            sy.dma_start(out=trisb[:, :], in_=tri).then_inc(sm["s_sml"], 16)
            sy.dma_start(out=onesb[:, :], in_=one).then_inc(sm["s_sml"], 16)
            sy.dma_start(out=idnsb[:, :], in_=idn).then_inc(sm["s_sml"], 16)
            sy.dma_start(out=cosv[:, :], in_=chi).then_inc(sm["s_tbl"], 16)
            sy.dma_start(out=sinv[:, :], in_=shi).then_inc(sm["s_tbl"], 16)
            sy.wait_ge(sm["s_sml"], 64)
            sy.nop().then_inc(sm["s_ph"], 1)

        @block.gpsimd
        def _(g):
            g.dma_start(out=wrsb[:, :, :], in_=wr).then_inc(sm["s_sm2"], 16)
            g.wait_ge(sm["s_tbl"], 32)
            g.dma_start(out=cosv[:, :], in_=clo, accum_op=ALU.add
                        ).then_inc(sm["s_sm2"], 16)
            g.dma_start(out=sinv[:, :], in_=slo, accum_op=ALU.add
                        ).then_inc(sm["s_sm2"], 16)
            g.wait_ge(sm["s_sm2"], 48)
            g.nop().then_inc(sm["s_ph"], 1)

        @block.vector
        def _(v):
            v.memset(nb24[:, :], ESHIFT)
            v.nop().then_inc(sm["s_ph"], 1)

        @block.scalar
        def _(s):
            s.nop().then_inc(sm["s_ph"], 1)

        @block.tensor
        def _(t):
            t.nop().then_inc(sm["s_ph"], 1)

        @block.sync
        def _(sy):
            sy.wait_ge(sm["s_ph"], 5)
            HH = H // 2
            for t in range(NT):
                for j, gate in enumerate([
                        ("apl", 2 * t), ("stat", 2 * t + 1),
                        ("stat", 2 * t + 2), ("apl", 2 * t + 1)]):
                    nm, val = gate
                    if val > 0:
                        sy.wait_ge(sm["s_apl" if nm == "apl" else "s_stat"], val)
                    half = j % 2
                    sy.dma_start(out=xbuf[:, :],
                                 in_=xs[t * 128:(t + 1) * 128,
                                        half * HH:(half + 1) * HH]
                                 ).then_inc(sm["s_xld"], 16)
            sy.nop().then_inc(sm["s_ph"], 1)

        @block.scalar
        def _(s):
            s.wait_ge(sm["s_ph"], 5)
            for t in range(NT):
                pa = t % 2
                for half in range(2):
                    s.wait_ge(sm["s_xld"], 16 * (4 * t + half + 1))
                    s.activation(adum[:, :], xbuf[:, :], AF.Copy,
                                 accum_out=stats[:, t, 0 + half:1 + half])
                    s.activation(adum[:, :], xbuf[:, :], AF.Square,
                                 accum_out=stats[:, t, 2 + half:3 + half]
                                 ).then_inc(sm["s_stat"], 1)
                s.wait_ge(sm["s_rec"], t + 1)
                s.activation(stats[:, t, 6:7], stats[:, t, 5:6], AF.Sqrt
                             ).then_inc(sm["s_rsq"], 1)
            s.nop().then_inc(sm["s_ph"], 1)

        @block.vector
        def _(v):
            v.wait_ge(sm["s_ph"], 5)
            HH = H // 2
            for t in range(NT):
                pa = t % 2
                v.wait_ge(sm["s_stat"], 2 * t + 2)
                fence(v, v.tensor_tensor(stats[:, t, 4:5], stats[:, t, 0:1],
                                          stats[:, t, 1:2], op=ALU.add))
                fence(v, v.tensor_tensor(stats[:, t, 5:6], stats[:, t, 2:3],
                                         stats[:, t, 3:4], op=ALU.add))
                fence(v, v.tensor_scalar(stats[:, t, 4:5], stats[:, t, 4:5],
                                         1.0 / H, None, op0=ALU.mult))
                fence(v, v.tensor_scalar(stats[:, t, 5:6], stats[:, t, 5:6],
                                         1.0 / H, None, op0=ALU.mult))
                fence(v, v.tensor_tensor(stats[:, t, 7:8], stats[:, t, 4:5],
                                         stats[:, t, 4:5], op=ALU.mult))
                fence(v, v.tensor_tensor(stats[:, t, 5:6], stats[:, t, 5:6],
                                         stats[:, t, 7:8], op=ALU.subtract))
                fence(v, v.tensor_scalar(stats[:, t, 5:6], stats[:, t, 5:6],
                                         1e-5, None, op0=ALU.add))
                v.reciprocal(stats[:, t, 5:6], stats[:, t, 5:6]
                             ).then_inc(sm["s_rec"], 1)
                v.wait_ge(sm["s_rsq"], t + 1)
                if t >= 2:
                    v.wait_ge(sm["s_ttr"], NK * (t - 1))
                for half in range(2):
                    v.wait_ge(sm["s_xld"], 16 * (4 * t + 2 + half + 1))
                    v.tensor_scalar(xtm[pa][:, half * HH:(half + 1) * HH],
                                    xbuf[:, :], stats[:, t, 4:5],
                                    stats[:, t, 6:7], op0=ALU.subtract,
                                    op1=ALU.mult).then_inc(sm["s_apl"], 1)
                for k in range(NK):
                    gidx = t * NK + k
                    v.wait_ge(sm["s_ttr"], gidx + 1)
                    v.tensor_copy(xnT[:, k, t * 128:(t + 1) * 128],
                                  pt[k % 2][:, :]).then_inc(sm["s_tcp"], 1)
            v.nop().then_inc(sm["s_ph"], 1)

        @block.tensor
        def _(t):
            t.wait_ge(sm["s_ph"], 5)
            t.wait_ge(sm["s_sml"], 64)
            for tt in range(NT):
                pa = tt % 2
                t.wait_ge(sm["s_apl"], 2 * tt + 2)
                for k in range(NK):
                    gidx = tt * NK + k
                    if gidx >= 2:
                        t.wait_ge(sm["s_tcp"], gidx - 1)
                    t.transpose(pt[k % 2][:, :],
                                xtm[pa][:, k * 128:(k + 1) * 128], idnsb[:, :]
                                ).then_inc(sm["s_ttr"], 1)
            t.nop().then_inc(sm["s_ph"], 1)

        @block.gpsimd
        def _(g):
            g.wait_ge(sm["s_ph"], 5)
            g.dma_start(out=wbuf[:, :, 0:128], in_=la).then_inc(sm["s_sm2"], 16)
            g.nop().then_inc(sm["s_ph"], 1)

        CMP = spx[:, 0:64].rearrange("p (i j) -> p i j", i=8)
        LG = spx[:, 64:72]
        KI = spx[:, 72:80]
        SI = spx[:, 80:88]
        KZ = spx[:, 88:96]
        CO = spx[:, 96:104]
        ZC = spx[:, 104:112]
        KS = spx[:, 112:113]
        TS = spx[:, 113:114]
        TM = spx[:, 114:115]
        RK = spx[:, 115:116]
        TAU = spx[:, 116:117]
        WW = spx[:, 120:128]

        @block.tensor
        def _(t):
            t.wait_ge(sm["s_ph"], 10)
            t.wait_ge(sm["s_sm2"], 64)
            for u in range(NU):
                tt = u + 4
                if u >= 1:
                    t.wait_ge(sm["s_spx"], u)
                    t.wait_ge(sm["s_hw"], u)
                for k in range(NK):
                    t.matmul(pb[0][:, 0:8], xnT[:, k, tt * 128:(tt + 1) * 128],
                             wrsb[:, k, :], start=(k == 0), stop=(k == NK - 1)
                             ).then_inc(sm["s_pmm"], 1) if k == NK - 1 else \
                        t.matmul(pb[0][:, 0:8], xnT[:, k, tt * 128:(tt + 1) * 128],
                                 wrsb[:, k, :], start=(k == 0), stop=False)
                for k in range(NK):
                    t.matmul(pb[1][:, 0:128], xnT[:, k, tt * 128:(tt + 1) * 128],
                             wbuf[:, k, 0:128], start=(k == 0), stop=(k == NK - 1)
                             ).then_inc(sm["s_pmm"], 1) if k == NK - 1 else \
                        t.matmul(pb[1][:, 0:128], xnT[:, k, tt * 128:(tt + 1) * 128],
                                 wbuf[:, k, 0:128], start=(k == 0), stop=False)
            t.nop().then_inc(sm["s_ph"], 1)

        @block.vector
        def _(v):
            v.wait_ge(sm["s_ph"], 10)
            for u in range(NU):
                v.wait_ge(sm["s_pmm"], 2 * u + 1)
                fence(v, v.tensor_tensor(LG, pb[0][:, 0:8], brsb[:, :], op=ALU.add))
                for i in range(8):
                    v.tensor_scalar(CMP[:, i, :], LG, LG[:, i:i + 1], None,
                                    op0=ALU.is_ge)
                fence(v, v.tensor_scalar(CMP[:, 7, :], LG, LG[:, 7:8], None,
                                         op0=ALU.is_ge))
                fence(v, v.tensor_reduce(KI, CMP, axis=AX.X, op=ALU.add))
                fence(v, v.tensor_tensor(CMP, CMP,
                                         bass.AP(spx, 64, [[128, 128], [0, 8], [1, 8]]),
                                         op=ALU.mult))
                fence(v, v.tensor_reduce(SI, CMP, axis=AX.X, op=ALU.add))
                fence(v, v.tensor_tensor(KZ, KI, LG, op=ALU.mult))
                fence(v, v.tensor_scalar(KZ, KZ, 1.0, None, op0=ALU.add))
                fence(v, v.tensor_tensor(CO, KZ, SI, op=ALU.is_gt))
                fence(v, v.tensor_tensor(ZC, LG, CO, op=ALU.mult))
                v.nop().then_inc(sm["s_spx"], 1)
                v.wait_ge(sm["s_ksum"], u + 1)
                fence(v, v.tensor_scalar(TM, TS, 1.0, None, op0=ALU.subtract))
                fence(v, v.reciprocal(RK, KS))
                fence(v, v.tensor_tensor(TAU, TM, RK, op=ALU.mult))
                fence(v, v.tensor_scalar(WW, LG, TAU, 0.0, op0=ALU.subtract,
                                         op1=ALU.max))
                v.wait_ge(sm["s_pmm"], 2 * u + 2)
                if u >= 1:
                    v.wait_ge(sm["s_hwT"], 16 * u)
                for e in range(8):
                    op = v.tensor_scalar(hwtm[:, e * 16:(e + 1) * 16],
                                         pb[1][:, e * 16:(e + 1) * 16],
                                         WW[:, e:e + 1], None, op0=ALU.mult)
                    if e == 7:
                        op.then_inc(sm["s_hw"], 1)
            v.nop().then_inc(sm["s_ph"], 1)

        @block.scalar
        def _(s):
            s.wait_ge(sm["s_ph"], 10)
            for u in range(NU):
                s.wait_ge(sm["s_spx"], u + 1)
                s.activation(spd[:, 0:8], CO, AF.Copy, accum_out=KS)
                s.activation(spd[:, 8:16], ZC, AF.Copy, accum_out=TS
                             ).then_inc(sm["s_ksum"], 1)
            s.nop().then_inc(sm["s_ph"], 1)

        @block.sync
        def _(sy):
            sy.wait_ge(sm["s_ph"], 10)
            for u in range(NU):
                sy.wait_ge(sm["s_hw"], u + 1)
                sy.dma_start_transpose(hwT[:, u * 128:(u + 1) * 128],
                                       hwtm[:, 0:128]).then_inc(sm["s_hwT"], 16)
            sy.wait_ge(sm["s_hwT"], 16 * NU)
            sy.nop().then_inc(sm["s_ph"], 1)

        @block.gpsimd
        def _(g):
            g.wait_ge(sm["s_ph"], 10)
            g.nop().then_inc(sm["s_ph"], 1)

        @block.gpsimd
        def _(g):
            g.wait_ge(sm["s_ph"], 15)
            for c in range(4):
                src = [wk, wk, wv, wv][c]
                oc = c % 2
                if c >= 1:
                    g.wait_ge(sm["s_kvmm"], c * NT)
                g.dma_start(out=wbuf[:, :, :], in_=src[oc]).then_inc(sm["s_wld"], 16)
            g.nop().then_inc(sm["s_ph"], 1)

        @block.tensor
        def _(t):
            t.wait_ge(sm["s_ph"], 15)
            for c in range(4):
                t.wait_ge(sm["s_wld"], 16 * (c + 1))
                for tt in range(NT):
                    it = c * NT + tt
                    if it >= 2:
                        t.wait_ge(sm["s_kstg"], it - 1)
                    for k in range(NK):
                        mmop = t.matmul(pb[it % 2][:, :],
                                        xnT[:, k, tt * 128:(tt + 1) * 128],
                                        wbuf[:, k, :], start=(k == 0),
                                        stop=(k == NK - 1))
                        if k == NK - 1:
                            mmop.then_inc(sm["s_kvmm"], 1)
            t.nop().then_inc(sm["s_ph"], 1)

        @block.vector
        def _(v):
            v.wait_ge(sm["s_ph"], 15)
            for c in range(4):
                isk = c < 2
                for tt in range(NT):
                    it = c * NT + tt
                    v.wait_ge(sm["s_kvmm"], it + 1)
                    if it >= 2:
                        if it - 2 < 24:
                            v.wait_ge(skvtr[it % 2], 64 * ((it - 2) // 2 + 1))
                        else:
                            v.wait_ge(sm["s_kvsp"], 16 * (it - 1))
                    if isk:
                        psv, pss, cv, sv, ov, rA, rB = ropeviews(
                            pb[it % 2], tt, kvstg[it % 2])
                        v.tensor_tensor(rA, psv, cv, op=ALU.mult)
                        fence(v, v.tensor_tensor(rB, pss, sv, op=ALU.mult))
                        v.tensor_tensor(ov, rA, rB, op=ALU.add
                                        ).then_inc(sm["s_kstg"], 1)
                    else:
                        v.tensor_copy(kvstg[it % 2][:, :], pb[it % 2][:, :]
                                      ).then_inc(sm["s_kstg"], 1)
            v.nop().then_inc(sm["s_ph"], 1)

        @block.scalar
        def _(s):
            s.wait_ge(sm["s_ph"], 15)
            for c in range(2):
                for tt in range(NT):
                    it = c * NT + tt
                    s.wait_ge(sm["s_kstg"], it + 1)
                    if it >= 2:
                        s.wait_ge(sm["s_kvsp"], 16 * (it - 1))
                    for j in range(4):
                        s.dma_start_transpose(trstg[it % 2][:, j * 128:(j + 1) * 128],
                                              kvstg[it % 2][:, j * 128:(j + 1) * 128]
                                              ).then_inc(skvtr[it % 2], 16)
            s.nop().then_inc(sm["s_ph"], 1)

        @block.sync
        def _(sy):
            sy.wait_ge(sm["s_ph"], 15)
            for c in range(4):
                isk = c < 2
                oc = c % 2
                for tt in range(NT):
                    it = c * NT + tt
                    if isk:
                        sy.wait_ge(skvtr[it % 2], 64 * (it // 2 + 1))
                        sy.dma_start(
                            out=kts[4 * oc:4 * oc + 4, tt, :, :].transpose([1, 0, 2]),
                            in_=trstg[it % 2].rearrange("p (j q) -> p j q", j=4)
                        ).then_inc(sm["s_kvsp"], 16)
                    else:
                        sy.wait_ge(sm["s_kstg"], it + 1)
                        sy.dma_start(
                            out=vts[4 * oc:4 * oc + 4, tt, :, :].transpose([1, 0, 2]),
                            in_=kvstg[it % 2].rearrange("p (j q) -> p j q", j=4)
                        ).then_inc(sm["s_kvsp"], 16)
            sy.wait_ge(sm["s_kvsp"], 16 * 48)
            sy.nop().then_inc(sm["s_ph"], 1)

        @block.gpsimd
        def _(g):
            g.wait_ge(sm["s_ph"], 20)
            for oc in range(8):
                if oc >= 2:
                    g.wait_ge(sm["s_kvmm"], 48 + (oc - 1) * NU)
                dst = wbuf[:, :, :] if oc % 2 == 0 else wq2_full
                g.dma_start(out=dst, in_=wq[oc]).then_inc(
                    sm["s_wld"] if oc % 2 == 0 else sm["s_wld1"], 16)
            g.nop().then_inc(sm["s_ph"], 1)

        @block.tensor
        def _(t):
            t.wait_ge(sm["s_ph"], 20)
            for oc in range(8):
                if oc % 2 == 0:
                    t.wait_ge(sm["s_wld"], 16 * (4 + oc // 2 + 1))
                else:
                    t.wait_ge(sm["s_wld1"], 16 * (oc // 2 + 1))
                for u in range(NU):
                    tt = u + 4
                    it = oc * NU + u
                    if it >= 2:
                        t.wait_ge(sm["s_kstg"], 48 + it - 1)
                    for k in range(NK):
                        rhsw = wbuf[:, k, :] if oc % 2 == 0 else wq2(k)
                        mmop = t.matmul(pb[it % 2][:, :],
                                        xnT[:, k, tt * 128:(tt + 1) * 128],
                                        rhsw, start=(k == 0),
                                        stop=(k == NK - 1))
                        if k == NK - 1:
                            mmop.then_inc(sm["s_kvmm"], 1)
            t.nop().then_inc(sm["s_ph"], 1)

        @block.vector
        def _(v):
            v.wait_ge(sm["s_ph"], 20)
            for oc in range(8):
                for u in range(NU):
                    tt = u + 4
                    it = oc * NU + u
                    v.wait_ge(sm["s_kvmm"], 48 + it + 1)
                    if it >= 2:
                        v.wait_ge(skvtr[it % 2], 768 + 64 * ((it - 2) // 2 + 1))
                    psv, pss, cv, sv, ov, rA, rB = ropeviews(
                        pb[it % 2], tt, kvstg[it % 2])
                    v.tensor_tensor(rA, psv, cv, op=ALU.mult)
                    fence(v, v.tensor_tensor(rB, pss, sv, op=ALU.mult))
                    v.tensor_tensor(ov, rA, rB, op=ALU.add).then_inc(sm["s_kstg"], 1)
            v.nop().then_inc(sm["s_ph"], 1)

        @block.scalar
        def _(s):
            s.wait_ge(sm["s_ph"], 20)
            for oc in range(8):
                for u in range(NU):
                    it = oc * NU + u
                    s.wait_ge(sm["s_kstg"], 48 + it + 1)
                    if it >= 2:
                        s.wait_ge(sm["s_kvsp"], 16 * (48 + it - 1))
                    for j in range(4):
                        s.dma_start_transpose(trstg[it % 2][:, j * 128:(j + 1) * 128],
                                              kvstg[it % 2][:, j * 128:(j + 1) * 128]
                                              ).then_inc(skvtr[it % 2], 16)
            s.nop().then_inc(sm["s_ph"], 1)

        @block.sync
        def _(sy):
            sy.wait_ge(sm["s_ph"], 20)
            for oc in range(8):
                for u in range(NU):
                    it = oc * NU + u
                    sy.wait_ge(skvtr[it % 2], 768 + 64 * (it // 2 + 1))
                    sy.dma_start(
                        out=qts[4 * oc:4 * oc + 4, u, :, :].transpose([1, 0, 2]),
                        in_=trstg[it % 2].rearrange("p (j q) -> p j q", j=4)
                    ).then_inc(sm["s_kvsp"], 16)
            sy.wait_ge(sm["s_kvsp"], 16 * (48 + 64))
            for g4 in range(4):
                sy.dma_start(out=xns[:, g4 * 8:(g4 + 1) * 8, :],
                             in_=xnT[:, g4 * 8:(g4 + 1) * 8, 512:NT * 128]
                             ).then_inc(sm["s_xnsp"], 16)
            sy.wait_ge(sm["s_xnsp"], 64)
            sy.nop().then_inc(sm["s_ph"], 1)

        @block.gpsimd
        def _(g):
            g.wait_ge(sm["s_ph"], 25)
            g.dma_start(out=wbuf[:, 0:8, :], in_=mlo).then_inc(sm["s_msl"], 16)
            g.nop().then_inc(sm["s_ph"], 1)

        @block.sync
        def _(sy):
            sy.wait_ge(sm["s_ph"], 25)
            for h in range(NQH):
                if h >= 2:
                    sy.wait_ge(sm["s_sc"], ((h - 2) * 2 + 2) * 8)
                sy.dma_start(out=qh[h % 2].rearrange("p (u q) -> p u q", u=8),
                             in_=qts[h].transpose([1, 0, 2])
                             ).then_inc(sqld[h % 2], 16)
                if h % 4 == 0:
                    if h >= 4:
                        sy.wait_ge(sm["s_av"], 2 * h)
                    sy.dma_start(out=kh.rearrange("p (tt q) -> p tt q", tt=NT),
                                 in_=kts[h // 4].transpose([1, 0, 2])
                                 ).then_inc(sm["s_kvld"], 16)
                    sy.dma_start(out=vh.rearrange("p (tt q) -> p tt q", tt=NT),
                                 in_=vts[h // 4].transpose([1, 0, 2])
                                 ).then_inc(sm["s_kvld"], 16)
                if h >= 1:
                    sy.wait_ge(sm["s_avn"], 2 * h)
                    sy.dma_start(out=avs[h - 1][:, 0:512], in_=kvstg0[:, :]
                                 ).then_inc(sm["s_avsp"], 16)
                    sy.dma_start(out=avs[h - 1][:, 512:1024], in_=kvstg1[:, :]
                                 ).then_inc(sm["s_avsp"], 16)
            sy.wait_ge(sm["s_avn"], 2 * NQH)
            sy.dma_start(out=avs[NQH - 1][:, 0:512], in_=kvstg0[:, :]
                         ).then_inc(sm["s_avsp"], 16)
            sy.dma_start(out=avs[NQH - 1][:, 512:1024], in_=kvstg1[:, :]
                         ).then_inc(sm["s_avsp"], 16)
            sy.wait_ge(sm["s_avsp"], 32 * NQH)
            sy.nop().then_inc(sm["s_ph"], 1)

        @block.tensor
        def _(t):
            t.wait_ge(sm["s_ph"], 25)
            for h in range(NQH):
                t.wait_ge(sqld[h % 2], 16 * (h // 2 + 1))
                t.wait_ge(sm["s_kvld"], 32 * (h // 4 + 1))
                for qc in range(2):
                    it = h * 2 + qc
                    for r in range(8):
                        kb = 4 * qc + r
                        lo, hi = _clip(r)
                        if it * 8 + r >= 2:
                            t.wait_ge(sm["s_exp"], it * 8 + r - 1)
                        t.matmul(pb[r % 2][:, lo:hi],
                                 kh[:, kb * 128:(kb + 1) * 128],
                                 qh[h % 2][:, qc * 512 + lo:qc * 512 + hi],
                                 start=True, stop=True).then_inc(sm["s_sc"], 1)
                    t.wait_ge(sm["s_msk"], (it + 1) * 8)
                    pba = pb[2 + 2 * (it % 2)]
                    pbd = pb[3 + 2 * (it % 2)]
                    if it >= 2:
                        t.wait_ge(sm["s_avn"], it - 1)
                    order = [4, 0, 1, 2, 3, 5, 6, 7]
                    for idx, r in enumerate(order):
                        kb = 4 * qc + r
                        lo, hi = _clip(r)
                        t.matmul(pba[:, lo:hi], vh[:, kb * 128:(kb + 1) * 128],
                                 exp_sb[:, r, lo:hi], start=(idx == 0),
                                 stop=(idx == 7))
                    for idx, r in enumerate(order):
                        lo, hi = _clip(r)
                        mmop = t.matmul(pbd[:, lo:hi], onesb[:, :],
                                        exp_sb[:, r, lo:hi], start=(idx == 0),
                                        stop=(idx == 7))
                        if idx == 7:
                            mmop.then_inc(sm["s_av"], 1)
            t.nop().then_inc(sm["s_ph"], 1)

        @block.scalar
        def _(s):
            s.wait_ge(sm["s_ph"], 25)
            for h in range(NQH):
                for qc in range(2):
                    it = h * 2 + qc
                    for r in range(8):
                        lo, hi = _clip(r)
                        s.wait_ge(sm["s_sc"], it * 8 + r + 1)
                        if r == 0 and it >= 1:
                            s.wait_ge(sm["s_av"], it)
                        s.activation(exp_sb[:, r, lo:hi], pb[r % 2][:, lo:hi],
                                     AF.Exp, bias=nb24[:, 0:1], scale=SCALE
                                     ).then_inc(sm["s_exp"], 1)
            s.nop().then_inc(sm["s_ph"], 1)

        @block.vector
        def _(v):
            v.wait_ge(sm["s_ph"], 25)
            v.wait_ge(sm["s_msl"], 16)
            for h in range(NQH):
                for qc in range(2):
                    it = h * 2 + qc
                    for r in range(8):
                        lo, hi = _clip(r)
                        v.wait_ge(sm["s_exp"], it * 8 + r + 1)
                        if r < 4:
                            v.tensor_tensor(exp_sb[:, r, lo:hi],
                                            exp_sb[:, r, lo:hi],
                                            wbuf[:, 4 * qc + r, lo:hi],
                                            op=ALU.mult).then_inc(sm["s_msk"], 1)
                        else:
                            d0 = 128 * (r - 4)
                            v.tensor_tensor(exp_sb[:, r, d0:d0 + 128],
                                            exp_sb[:, r, d0:d0 + 128],
                                            trisb[:, :], op=ALU.mult
                                            ).then_inc(sm["s_msk"], 1)
                    v.wait_ge(sm["s_av"], it + 1)
                    if qc == 0 and h >= 1:
                        v.wait_ge(sm["s_avsp"], 32 * h)
                    fence(v, v.reciprocal(den_r[:, :], pb[3 + 2 * (it % 2)][:, :]))
                    v.tensor_tensor(avh[qc][:, :], pb[2 + 2 * (it % 2)][:, :],
                                    den_r[:, :],
                                    op=ALU.mult).then_inc(sm["s_avn"], 1)
            v.nop().then_inc(sm["s_ph"], 1)

        @block.gpsimd
        def _(g):
            g.wait_ge(sm["s_ph"], 30)
            for oc in range(8):
                if oc >= 2:
                    g.wait_ge(sm["s_smm"], (oc - 1) * NU)
                dst = wbuf[:, :, :] if oc % 2 == 0 else wz2_full
                g.dma_start(out=dst, in_=wo[oc]).then_inc(
                    sm["s_wld"] if oc % 2 == 0 else sm["s_wld1"], 16)
            g.nop().then_inc(sm["s_ph"], 1)

        @block.sync
        def _(sy):
            sy.wait_ge(sm["s_ph"], 30)
            for h in range(NQH):
                sy.dma_start(out=xnT[:, h, 0:1024], in_=avs[h]
                             ).then_inc(sm["s_avld"], 16)
            for oc in range(8):
                for u in range(NU):
                    it = oc * NU + u
                    sy.wait_ge(sm["s_scp"], it + 1)
                    sy.dma_start(out=ss[u][:, oc * 512:(oc + 1) * 512],
                                 in_=sstg[it % 2][:, :]).then_inc(sm["s_ssp"], 16)
            sy.wait_ge(sm["s_ssp"], 16 * 64)
            sy.nop().then_inc(sm["s_ph"], 1)

        @block.tensor
        def _(t):
            t.wait_ge(sm["s_ph"], 30)
            t.wait_ge(sm["s_avld"], 16 * NQH)
            for oc in range(8):
                if oc % 2 == 0:
                    t.wait_ge(sm["s_wld"], 16 * (8 + oc // 2 + 1))
                else:
                    t.wait_ge(sm["s_wld1"], 16 * (4 + oc // 2 + 1))
                for u in range(NU):
                    it = oc * NU + u
                    if it >= 2:
                        t.wait_ge(sm["s_scp"], it - 1)
                    for k in range(NK):
                        rhsw = wbuf[:, k, :] if oc % 2 == 0 else wz2(k)
                        mmop = t.matmul(pb[u % 2][:, :], xnT[:, k, u * 128:(u + 1) * 128],
                                        rhsw, start=(k == 0),
                                        stop=(k == NK - 1))
                        if k == NK - 1:
                            mmop.then_inc(sm["s_smm"], 1)
            t.nop().then_inc(sm["s_ph"], 1)

        @block.scalar
        def _(s):
            s.wait_ge(sm["s_ph"], 30)
            for oc in range(8):
                for u in range(NU):
                    it = oc * NU + u
                    s.wait_ge(sm["s_smm"], it + 1)
                    if it >= 2:
                        s.wait_ge(sm["s_ssp"], 16 * (it - 1))
                    s.activation(sstg[it % 2][:, :], pb[u % 2][:, :], AF.Copy
                                 ).then_inc(sm["s_scp"], 1)
            s.nop().then_inc(sm["s_ph"], 1)

        @block.vector
        def _(v):
            v.wait_ge(sm["s_ph"], 30)
            v.nop().then_inc(sm["s_ph"], 1)

        @block.gpsimd
        def _(g):
            g.wait_ge(sm["s_ph"], 35)
            for oc in range(8):
                if oc >= 2:
                    g.wait_ge(sm["s_fmm"], (oc - 1) * NU)
                dst = wbuf[:, :, :] if oc % 2 == 0 else wz2_full
                g.dma_start(out=dst, in_=wb[oc]).then_inc(
                    sm["s_wld"] if oc % 2 == 0 else sm["s_wld1"], 16)
                if oc >= 1:
                    g.wait_ge(sm["s_fmm"], oc * NU)
                g.dma_start(out=lbs[oc % 2][:, :], in_=lb[:, oc * 512:(oc + 1) * 512]
                            ).then_inc(sm["s_lbl"], 16)
            g.nop().then_inc(sm["s_ph"], 1)

        @block.sync
        def _(sy):
            sy.wait_ge(sm["s_ph"], 35)
            for g4 in range(4):
                sy.dma_start(out=xnT[:, g4 * 8:(g4 + 1) * 8, 0:1024],
                             in_=xns[:, g4 * 8:(g4 + 1) * 8, :]
                             ).then_inc(sm["s_avld"], 16)
            for oc in range(8):
                if oc >= 1:
                    sy.wait_ge(sm["s_fb"], oc * NU)
                sy.dma_start(out=bbs[:, :], in_=bb[:, oc * 512:(oc + 1) * 512]
                             ).then_inc(sm["s_bbl"], 16)
                for u in range(NU):
                    it = oc * NU + u
                    if it >= 2:
                        sy.wait_ge(sm["s_fg"], it - 1)
                    sy.dma_start(out=sstg[it % 2][:, :],
                                 in_=ss[u][:, oc * 512:(oc + 1) * 512]
                                 ).then_inc(ssld[it % 2], 16)
                    sy.dma_start(out=xck[it % 2][:, :],
                                 in_=xs[(u + 4) * 128:(u + 5) * 128,
                                        oc * 512:(oc + 1) * 512]
                                 ).then_inc(sxcl[it % 2], 16)
                    sy.wait_ge(sm["s_fo"], it + 1)
                    sy.dma_start(out=os_[u * 128:(u + 1) * 128,
                                         oc * 512:(oc + 1) * 512],
                                 in_=ostg[it % 2][:, :]).then_inc(sm["s_ost"], 16)
            sy.wait_ge(sm["s_ost"], 16 * 64)

        @block.tensor
        def _(t):
            t.wait_ge(sm["s_ph"], 35)
            t.wait_ge(sm["s_avld"], 16 * (NQH + 4))
            for oc in range(8):
                if oc % 2 == 0:
                    t.wait_ge(sm["s_wld"], 16 * (12 + oc // 2 + 1))
                else:
                    t.wait_ge(sm["s_wld1"], 16 * (8 + oc // 2 + 1))
                t.wait_ge(sm["s_lbl"], 16 * (oc + 1))
                for u in range(NU):
                    it = oc * NU + u
                    if it >= 2:
                        t.wait_ge(sm["s_scp"], 64 + it - 1)
                    for k in range(NK):
                        rhsw = wbuf[:, k, :] if oc % 2 == 0 else wz2(k)
                        t.matmul(pb[u % 2][:, :], xnT[:, k, u * 128:(u + 1) * 128],
                                 rhsw, start=(k == 0), stop=False)
                    t.matmul(pb[u % 2][:, :], hwT[:, u * 128:(u + 1) * 128],
                             lbs[oc % 2][:, :], start=False, stop=True
                             ).then_inc(sm["s_fmm"], 1)
            t.nop().then_inc(sm["s_ph"], 1)

        @block.vector
        def _(v):
            v.wait_ge(sm["s_ph"], 35)
            for oc in range(8):
                for u in range(NU):
                    it = oc * NU + u
                    v.wait_ge(sm["s_fmm"], it + 1)
                    v.wait_ge(sm["s_bbl"], 16 * (oc + 1))
                    v.tensor_tensor(pb[u % 2][:, :], pb[u % 2][:, :], bbs[:, :],
                                    op=ALU.add).then_inc(sm["s_fb"], 1)
                    v.wait_ge(sm["s_scp"], 64 + it + 1)
                    v.wait_ge(ssld[it % 2], 16 * (it // 2 + 1))
                    v.wait_ge(sxcl[it % 2], 16 * (it // 2 + 1))
                    if it >= 2:
                        v.wait_ge(sm["s_ost"], 16 * (it - 1))
                    fence(v, v.tensor_tensor(ostg[it % 2][:, :], sstg[it % 2][:, :],
                                              gstg[it % 2][:, :], op=ALU.mult))
                    v.tensor_tensor(ostg[it % 2][:, :], ostg[it % 2][:, :],
                                    xck[it % 2][:, :], op=ALU.add
                                    ).then_inc(sm["s_fo"], 1)
                    v.nop().then_inc(sm["s_fg"], 1)

        @block.scalar
        def _(s):
            s.wait_ge(sm["s_ph"], 35)
            for oc in range(8):
                for u in range(NU):
                    it = oc * NU + u
                    s.wait_ge(sm["s_fb"], it + 1)
                    if it >= 2:
                        s.wait_ge(sm["s_fg"], it - 1)
                    s.activation(gstg[it % 2][:, :], pb[u % 2][:, :], AF.Silu
                                 ).then_inc(sm["s_scp"], 1)

    return nc


def _tables(h_half):
    theta = 10000.0
    inv = 1.0 / theta ** (np.arange(0, 128, 2, dtype=np.float64) / 128.0)
    loc = np.arange(NT * 128)
    if h_half == 0:
        pos = np.where(loc < HALO, 0, loc - HALO).astype(np.float64)
    else:
        pos = (loc + HALO).astype(np.float64)
    ang = pos[:, None] * inv[None, :]
    cos64 = np.cos(ang)
    sin64 = np.sin(ang)
    cosT = np.concatenate([cos64, cos64], axis=1)
    sinA = np.concatenate([-sin64, sin64], axis=1)
    out = []
    for tab in (cosT, sinA):
        hi = tab.astype(np.float32)
        lo = (tab - hi.astype(np.float64)).astype(np.float32)
        out.append(np.ascontiguousarray(
            hi.reshape(NT, 128, 128).transpose(1, 0, 2).reshape(128, NT * 128)))
        out.append(np.ascontiguousarray(
            lo.reshape(NT, 128, 128).transpose(1, 0, 2).reshape(128, NT * 128)))
    return out


def _masks(h_half):
    m = np.zeros((128, 8, 512), dtype=np.float32)
    kc = np.arange(128)[:, None]
    qc = np.arange(512)[None, :]
    for qch in range(2):
        for r in range(4):
            if h_half == 0 and qch == 0:
                continue
            lo, hi = _clip(r)
            allow = (qc <= r * 128 + kc) & (qc < hi)
            m[:, qch * 4 + r, :] = allow.astype(np.float32)
    return m.astype(ml_dtypes.bfloat16)


def _w_tiles(w, noc):
    return np.ascontiguousarray(
        w.reshape(NK, 128, noc, 512).transpose(2, 1, 0, 3))


def kernel(L0_attn_hidden_states, L0_attn_Wq, L0_attn_Wk, L0_attn_Wv, L0_attn_Wo,
           L0_ffn_W_base, L0_ffn_b_base, L0_ffn_W_router, L0_ffn_b_router,
           L0_ffn_lora_A, L0_ffn_lora_B):
    x = np.ascontiguousarray(L0_attn_hidden_states, dtype=np.float32)
    wq_h = _w_tiles(np.ascontiguousarray(L0_attn_Wq, dtype=np.float32), 8)
    wk_h = _w_tiles(np.ascontiguousarray(L0_attn_Wk, dtype=np.float32), 2)
    wv_h = _w_tiles(np.ascontiguousarray(L0_attn_Wv, dtype=np.float32), 2)
    wo_h = _w_tiles(np.ascontiguousarray(L0_attn_Wo, dtype=np.float32), 8)
    wb_h = _w_tiles(np.ascontiguousarray(L0_ffn_W_base.T, dtype=np.float32), 8)
    wr_h = np.ascontiguousarray(
        np.asarray(L0_ffn_W_router, dtype=np.float32).T.reshape(NK, 128, 8)
        .transpose(1, 0, 2))
    la_flat = np.asarray(L0_ffn_lora_A, dtype=np.float32).transpose(1, 0, 2).reshape(H, 128)
    la_h = np.ascontiguousarray(la_flat.reshape(NK, 128, 128).transpose(1, 0, 2))
    lb_h = np.ascontiguousarray(np.asarray(L0_ffn_lora_B, dtype=np.float32).reshape(128, H))
    bb_h = np.ascontiguousarray(
        np.broadcast_to(np.asarray(L0_ffn_b_base, dtype=np.float32), (128, H)))
    br_h = np.ascontiguousarray(
        np.broadcast_to(np.asarray(L0_ffn_b_router, dtype=np.float32), (128, 8)))
    tri_h = (np.arange(128)[:, None] <= np.arange(128)[None, :]
             ).astype(ml_dtypes.bfloat16)
    one_h = np.ones((128, 128), dtype=ml_dtypes.bfloat16)
    idn_h = np.eye(128, dtype=ml_dtypes.bfloat16)

    if "nc" not in _NC_CACHE:
        _NC_CACHE["nc"] = build_program()
    nc = _NC_CACHE["nc"]

    tabs = [_tables(0), _tables(1)]
    msks = [_masks(0), _masks(1)]
    in_maps = []
    for c in range(N_CORES):
        b, hh = c // 2, c % 2
        if hh == 0:
            xsl = np.concatenate([np.zeros((HALO, H), np.float32), x[b, 0:1024]],
                                 axis=0)
        else:
            xsl = np.ascontiguousarray(x[b, HALO:2048])
        chi_h, clo_h, shi_h, slo_h = tabs[hh]
        in_maps.append({
            "xs": xsl, "wq": wq_h, "wk": wk_h, "wv": wv_h, "wo": wo_h,
            "wb": wb_h, "wr": wr_h, "la": la_h, "lb": lb_h, "bb": bb_h,
            "br": br_h, "chi": chi_h, "clo": clo_h, "shi": shi_h, "slo": slo_h,
            "mlo": msks[hh], "tri": tri_h, "one": one_h, "idn": idn_h,
        })
    res = run_bass_kernel_spmd(nc, in_maps, core_ids=list(range(N_CORES)))
    out = np.empty((B, S, H), dtype=np.float32)
    for c in range(N_CORES):
        b, hh = c // 2, c % 2
        out[b, hh * 1024:(hh + 1) * 1024] = res.results[c]["os_"]
    return out


# version 06
# speedup vs baseline: 1.3312x; 1.0109x over previous
import contextlib
import math

import numpy as np
import ml_dtypes

import concourse.bass as bass
import concourse.mybir as mybir
from concourse.bass_utils import run_bass_kernel_spmd

F32 = mybir.dt.float32
BF16 = mybir.dt.bfloat16
AF = mybir.ActivationFunctionType
ALU = mybir.AluOpType
AX = mybir.AxisListType

H = 4096
S = 2048
B = 4
NT = 12
NU = 8
HALO = 512
NQH = 32
NKV = 8
NK = 32
SCALE = 1.0 / math.sqrt(128.0)
ESHIFT = -24.0
N_CORES = 8

_NC_CACHE = {}


def _clip(r):
    if r < 4:
        return 0, 128 * (r + 1)
    return 128 * (r - 4), 512


def build_program():
    nc = bass.Bass("TRN2", target_bir_lowering=False)

    xs = nc.dram_tensor("xs", [NT * 128, H], F32, kind="ExternalInput").ap()
    wq = nc.dram_tensor("wq", [8, 128, NK, 512], F32, kind="ExternalInput").ap()
    wk = nc.dram_tensor("wk", [2, 128, NK, 512], F32, kind="ExternalInput").ap()
    wv = nc.dram_tensor("wv", [2, 128, NK, 512], F32, kind="ExternalInput").ap()
    wo = nc.dram_tensor("wo", [8, 128, NK, 512], F32, kind="ExternalInput").ap()
    wb = nc.dram_tensor("wb", [8, 128, NK, 512], F32, kind="ExternalInput").ap()
    wr = nc.dram_tensor("wr", [128, NK, 8], F32, kind="ExternalInput").ap()
    la = nc.dram_tensor("la", [128, NK, 128], F32, kind="ExternalInput").ap()
    lb = nc.dram_tensor("lb", [128, H], F32, kind="ExternalInput").ap()
    bb = nc.dram_tensor("bb", [128, H], F32, kind="ExternalInput").ap()
    br = nc.dram_tensor("br", [128, 8], F32, kind="ExternalInput").ap()
    chi = nc.dram_tensor("chi", [128, NT * 128], F32, kind="ExternalInput").ap()
    clo = nc.dram_tensor("clo", [128, NT * 128], F32, kind="ExternalInput").ap()
    shi = nc.dram_tensor("shi", [128, NT * 128], F32, kind="ExternalInput").ap()
    slo = nc.dram_tensor("slo", [128, NT * 128], F32, kind="ExternalInput").ap()
    mlo = nc.dram_tensor("mlo", [128, 8, 512], BF16, kind="ExternalInput").ap()
    tri = nc.dram_tensor("tri", [128, 128], BF16, kind="ExternalInput").ap()
    one = nc.dram_tensor("one", [128, 128], BF16, kind="ExternalInput").ap()
    idn = nc.dram_tensor("idn", [128, 128], BF16, kind="ExternalInput").ap()
    os_ = nc.dram_tensor("os_", [NU * 128, H], F32, kind="ExternalOutput").ap()
    kts = nc.dram_tensor("kts", [NKV, NT, 128, 128], BF16, kind="Internal").ap()
    vts = nc.dram_tensor("vts", [NKV, NT, 128, 128], BF16, kind="Internal").ap()
    qts = nc.dram_tensor("qts", [NQH, NU, 128, 128], BF16, kind="Internal").ap()
    avs = nc.dram_tensor("avs", [NQH, 128, 1024], BF16, kind="Internal").ap()
    xns = nc.dram_tensor("xns", [128, NK, 1024], BF16, kind="Internal").ap()
    ss = nc.dram_tensor("ss", [NU, 128, H], F32, kind="Internal").ap()

    st = contextlib.ExitStack()
    with st:
        sb = lambda *a, **k: st.enter_context(nc.sbuf_tensor(*a, **k))
        xnT = sb("xnT", [128, NK, NT * 128], BF16, is_transpose_dest=True)
        wbuf = sb("wbuf", [128, NK, 512], BF16)
        xbuf = sb("xbuf", [128, H // 2], F32)
        adum = sb("adum", [128, H // 2], F32)
        xtm0 = sb("xtm0", [128, H], BF16)
        xtm1 = sb("xtm1", [128, H], BF16)
        sqsc = sb("sqsc", [128, NT * 128], BF16)
        cosv = sb("cosv", [128, NT * 128], F32)
        sinv = sb("sinv", [128, NT * 128], F32)
        stats = sb("stats", [128, NT, 8], F32)
        spx = sb("spx", [128, 128], F32)
        spd = sb("spd", [128, 16], F32)
        hwT = sb("hwT", [128, NU * 128], BF16, is_transpose_dest=True)
        hwtm = sb("hwtm", [128, 128], BF16)
        wrsb = sb("wrsb", [128, NK, 8], BF16)
        brsb = sb("brsb", [128, 8], F32)
        trisb = sb("trisb", [128, 128], BF16)
        onesb = sb("onesb", [128, 128], BF16)
        nb24 = sb("nb24", [128, 1], F32)
        idnsb = sb("idnsb", [128, 128], BF16)
        kvstg0 = sb("kvstg0", [128, 512], BF16)
        kvstg1 = sb("kvstg1", [128, 512], BF16)
        trstg0 = sb("trstg0", [128, 512], BF16, is_transpose_dest=True)
        trstg1 = sb("trstg1", [128, 512], BF16, is_transpose_dest=True)
        sstg0 = sb("sstg0", [128, 512], F32)
        sstg1 = sb("sstg1", [128, 512], F32)
        gstg0 = sb("gstg0", [128, 512], F32)
        gstg1 = sb("gstg1", [128, 512], F32)
        ostg0 = sb("ostg0", [128, 512], F32)
        ostg1 = sb("ostg1", [128, 512], F32)
        xck0 = sb("xck0", [128, 512], F32)
        xck1 = sb("xck1", [128, 512], F32)
        bbs = sb("bbs", [128, 512], F32)
        lbs0 = sb("lbs0", [128, 512], BF16)

        exp_sb = xtm0.rearrange("p (r q) -> p r q", r=8)
        qh = [xtm1[:, 0:1024], xtm1[:, 1024:2048]]
        kh = xtm1[:, 2048:2048 + NT * 128]
        vh = sqsc
        avh = [kvstg0, kvstg1]
        den_r = gstg0
        rtA, rtB = sstg0, sstg1

        pb = [st.enter_context(nc.psum_tensor(f"pb{i}", [128, 512], F32))
              for i in range(6)]
        pt = [st.enter_context(nc.psum_tensor(f"pt{i}", [128, 128], BF16))
              for i in range(2)]

        semnames = [
            "s_ph", "s_xld", "s_stat", "s_rec", "s_rsq", "s_apl",
            "s_trA0", "s_trA1", "s_sml", "s_sm2", "s_tbl", "s_pmm", "s_spx",
            "s_ksum", "s_hw", "s_hwT", "s_wld", "s_wld1", "s_kvmm", "s_kstg",
            "s_kvtr0", "s_kvtr1", "s_kvsp", "s_xnsp", "s_qld0", "s_qld1",
            "s_kvld", "s_sc", "s_exp", "s_msk", "s_av", "s_avn", "s_avsp",
            "s_avld", "s_smm", "s_scp", "s_ssp", "s_sld0", "s_sld1",
            "s_xcl0", "s_xcl1", "s_fmm", "s_fb", "s_fo", "s_fg", "s_ost",
            "s_lbl", "s_bbl", "s_msl", "s_dvf", "s_ttr", "s_tcp", "s_avB",
        ]
        sm = {n: st.enter_context(nc.semaphore(n)) for n in semnames}
        block = st.enter_context(nc.Block())

        xtm = [xtm0, xtm1]
        strA = [sm["s_trA0"], sm["s_trA1"]]
        kvstg = [kvstg0, kvstg1]
        trstg = [trstg0, trstg1]
        skvtr = [sm["s_kvtr0"], sm["s_kvtr1"]]
        sqld = [sm["s_qld0"], sm["s_qld1"]]
        sstg = [sstg0, sstg1]
        gstg = [gstg0, gstg1]
        ostg = [ostg0, ostg1]
        xck = [xck0, xck1]
        lbs = [lbs0, lbs0]
        ssld = [sm["s_sld0"], sm["s_sld1"]]
        sxcl = [sm["s_xcl0"], sm["s_xcl1"]]

        dvf = [0]
        wq2 = lambda k: xnT[:, k, 0:512]
        wz2 = lambda k: xnT[:, k, 1024:1536]
        wq2_full = xnT[:, :, 0:512]
        wz2_full = xnT[:, :, 1024:1536]

        def fence(v, instop):
            dvf[0] += 1
            instop.then_inc(sm["s_dvf"], 1)
            v.wait_ge(sm["s_dvf"], dvf[0])

        def ropeviews(ps, tt, outbuf):
            psv = ps.rearrange("p (h u j) -> p h u j", h=4, u=2)
            pss = bass.AP(ps, 64, [[512, 128], [128, 4], [-64, 2], [1, 64]])
            cv = bass.AP(cosv, tt * 128,
                         [[NT * 128, 128], [0, 4], [64, 2], [1, 64]])
            sv = bass.AP(sinv, tt * 128,
                         [[NT * 128, 128], [0, 4], [64, 2], [1, 64]])
            ov = outbuf.rearrange("p (h u j) -> p h u j", h=4, u=2)
            rA = rtA.rearrange("p (h u j) -> p h u j", h=4, u=2)
            rB = rtB.rearrange("p (h u j) -> p h u j", h=4, u=2)
            return psv, pss, cv, sv, ov, rA, rB

        @block.sync
        def _(sy):
            sy.dma_start(out=brsb[:, :], in_=br).then_inc(sm["s_sml"], 16)
            sy.dma_start(out=trisb[:, :], in_=tri).then_inc(sm["s_sml"], 16)
            sy.dma_start(out=onesb[:, :], in_=one).then_inc(sm["s_sml"], 16)
            sy.dma_start(out=idnsb[:, :], in_=idn).then_inc(sm["s_sml"], 16)
            sy.dma_start(out=cosv[:, :], in_=chi).then_inc(sm["s_tbl"], 16)
            sy.dma_start(out=sinv[:, :], in_=shi).then_inc(sm["s_tbl"], 16)
            sy.wait_ge(sm["s_sml"], 64)
            sy.nop().then_inc(sm["s_ph"], 1)

        @block.gpsimd
        def _(g):
            g.dma_start(out=wrsb[:, :, :], in_=wr).then_inc(sm["s_sm2"], 16)
            g.wait_ge(sm["s_tbl"], 32)
            g.dma_start(out=cosv[:, :], in_=clo, accum_op=ALU.add
                        ).then_inc(sm["s_sm2"], 16)
            g.dma_start(out=sinv[:, :], in_=slo, accum_op=ALU.add
                        ).then_inc(sm["s_sm2"], 16)
            g.wait_ge(sm["s_sm2"], 48)
            g.nop().then_inc(sm["s_ph"], 1)

        @block.vector
        def _(v):
            v.memset(nb24[:, :], ESHIFT)
            v.nop().then_inc(sm["s_ph"], 1)

        @block.scalar
        def _(s):
            s.nop().then_inc(sm["s_ph"], 1)

        @block.tensor
        def _(t):
            t.nop().then_inc(sm["s_ph"], 1)

        @block.sync
        def _(sy):
            sy.wait_ge(sm["s_ph"], 5)
            HH = H // 2
            for t in range(NT):
                for j, gate in enumerate([
                        ("apl", 2 * t), ("stat", 2 * t + 1),
                        ("stat", 2 * t + 2), ("apl", 2 * t + 1)]):
                    nm, val = gate
                    if val > 0:
                        sy.wait_ge(sm["s_apl" if nm == "apl" else "s_stat"], val)
                    half = j % 2
                    sy.dma_start(out=xbuf[:, :],
                                 in_=xs[t * 128:(t + 1) * 128,
                                        half * HH:(half + 1) * HH]
                                 ).then_inc(sm["s_xld"], 16)
            sy.nop().then_inc(sm["s_ph"], 1)

        @block.scalar
        def _(s):
            s.wait_ge(sm["s_ph"], 5)
            for t in range(NT):
                pa = t % 2
                for half in range(2):
                    s.wait_ge(sm["s_xld"], 16 * (4 * t + half + 1))
                    s.activation(adum[:, :], xbuf[:, :], AF.Copy,
                                 accum_out=stats[:, t, 0 + half:1 + half])
                    s.activation(adum[:, :], xbuf[:, :], AF.Square,
                                 accum_out=stats[:, t, 2 + half:3 + half]
                                 ).then_inc(sm["s_stat"], 1)
                s.wait_ge(sm["s_rec"], t + 1)
                s.activation(stats[:, t, 6:7], stats[:, t, 5:6], AF.Sqrt
                             ).then_inc(sm["s_rsq"], 1)
            s.nop().then_inc(sm["s_ph"], 1)

        @block.vector
        def _(v):
            v.wait_ge(sm["s_ph"], 5)
            HH = H // 2
            for t in range(NT):
                pa = t % 2
                v.wait_ge(sm["s_stat"], 2 * t + 2)
                fence(v, v.tensor_tensor(stats[:, t, 4:5], stats[:, t, 0:1],
                                          stats[:, t, 1:2], op=ALU.add))
                fence(v, v.tensor_tensor(stats[:, t, 5:6], stats[:, t, 2:3],
                                         stats[:, t, 3:4], op=ALU.add))
                fence(v, v.tensor_scalar(stats[:, t, 4:5], stats[:, t, 4:5],
                                         1.0 / H, None, op0=ALU.mult))
                fence(v, v.tensor_scalar(stats[:, t, 5:6], stats[:, t, 5:6],
                                         1.0 / H, None, op0=ALU.mult))
                fence(v, v.tensor_tensor(stats[:, t, 7:8], stats[:, t, 4:5],
                                         stats[:, t, 4:5], op=ALU.mult))
                fence(v, v.tensor_tensor(stats[:, t, 5:6], stats[:, t, 5:6],
                                         stats[:, t, 7:8], op=ALU.subtract))
                fence(v, v.tensor_scalar(stats[:, t, 5:6], stats[:, t, 5:6],
                                         1e-5, None, op0=ALU.add))
                v.reciprocal(stats[:, t, 5:6], stats[:, t, 5:6]
                             ).then_inc(sm["s_rec"], 1)
                v.wait_ge(sm["s_rsq"], t + 1)
                if t >= 2:
                    v.wait_ge(sm["s_ttr"], NK * (t - 1))
                for half in range(2):
                    v.wait_ge(sm["s_xld"], 16 * (4 * t + 2 + half + 1))
                    v.tensor_scalar(xtm[pa][:, half * HH:(half + 1) * HH],
                                    xbuf[:, :], stats[:, t, 4:5],
                                    stats[:, t, 6:7], op0=ALU.subtract,
                                    op1=ALU.mult).then_inc(sm["s_apl"], 1)
                for k in range(NK):
                    gidx = t * NK + k
                    v.wait_ge(sm["s_ttr"], gidx + 1)
                    v.tensor_copy(xnT[:, k, t * 128:(t + 1) * 128],
                                  pt[k % 2][:, :]).then_inc(sm["s_tcp"], 1)
            v.nop().then_inc(sm["s_ph"], 1)

        @block.tensor
        def _(t):
            t.wait_ge(sm["s_ph"], 5)
            t.wait_ge(sm["s_sml"], 64)
            for tt in range(NT):
                pa = tt % 2
                t.wait_ge(sm["s_apl"], 2 * tt + 2)
                for k in range(NK):
                    gidx = tt * NK + k
                    if gidx >= 2:
                        t.wait_ge(sm["s_tcp"], gidx - 1)
                    t.transpose(pt[k % 2][:, :],
                                xtm[pa][:, k * 128:(k + 1) * 128], idnsb[:, :]
                                ).then_inc(sm["s_ttr"], 1)
            t.nop().then_inc(sm["s_ph"], 1)

        @block.gpsimd
        def _(g):
            g.wait_ge(sm["s_ph"], 5)
            g.dma_start(out=wbuf[:, :, 0:128], in_=la).then_inc(sm["s_sm2"], 16)
            g.nop().then_inc(sm["s_ph"], 1)

        CMP = spx[:, 0:64].rearrange("p (i j) -> p i j", i=8)
        LG = spx[:, 64:72]
        KI = spx[:, 72:80]
        SI = spx[:, 80:88]
        KZ = spx[:, 88:96]
        CO = spx[:, 96:104]
        ZC = spx[:, 104:112]
        KS = spx[:, 112:113]
        TS = spx[:, 113:114]
        TM = spx[:, 114:115]
        RK = spx[:, 115:116]
        TAU = spx[:, 116:117]
        WW = spx[:, 120:128]

        @block.tensor
        def _(t):
            t.wait_ge(sm["s_ph"], 10)
            t.wait_ge(sm["s_sm2"], 64)
            for u in range(NU):
                tt = u + 4
                if u >= 1:
                    t.wait_ge(sm["s_spx"], u)
                    t.wait_ge(sm["s_hw"], u)
                for k in range(NK):
                    t.matmul(pb[0][:, 0:8], xnT[:, k, tt * 128:(tt + 1) * 128],
                             wrsb[:, k, :], start=(k == 0), stop=(k == NK - 1)
                             ).then_inc(sm["s_pmm"], 1) if k == NK - 1 else \
                        t.matmul(pb[0][:, 0:8], xnT[:, k, tt * 128:(tt + 1) * 128],
                                 wrsb[:, k, :], start=(k == 0), stop=False)
                for k in range(NK):
                    t.matmul(pb[1][:, 0:128], xnT[:, k, tt * 128:(tt + 1) * 128],
                             wbuf[:, k, 0:128], start=(k == 0), stop=(k == NK - 1)
                             ).then_inc(sm["s_pmm"], 1) if k == NK - 1 else \
                        t.matmul(pb[1][:, 0:128], xnT[:, k, tt * 128:(tt + 1) * 128],
                                 wbuf[:, k, 0:128], start=(k == 0), stop=False)
            t.nop().then_inc(sm["s_ph"], 1)

        @block.vector
        def _(v):
            v.wait_ge(sm["s_ph"], 10)
            for u in range(NU):
                v.wait_ge(sm["s_pmm"], 2 * u + 1)
                fence(v, v.tensor_tensor(LG, pb[0][:, 0:8], brsb[:, :], op=ALU.add))
                for i in range(8):
                    v.tensor_scalar(CMP[:, i, :], LG, LG[:, i:i + 1], None,
                                    op0=ALU.is_ge)
                fence(v, v.tensor_scalar(CMP[:, 7, :], LG, LG[:, 7:8], None,
                                         op0=ALU.is_ge))
                fence(v, v.tensor_reduce(KI, CMP, axis=AX.X, op=ALU.add))
                fence(v, v.tensor_tensor(CMP, CMP,
                                         bass.AP(spx, 64, [[128, 128], [0, 8], [1, 8]]),
                                         op=ALU.mult))
                fence(v, v.tensor_reduce(SI, CMP, axis=AX.X, op=ALU.add))
                fence(v, v.tensor_tensor(KZ, KI, LG, op=ALU.mult))
                fence(v, v.tensor_scalar(KZ, KZ, 1.0, None, op0=ALU.add))
                fence(v, v.tensor_tensor(CO, KZ, SI, op=ALU.is_gt))
                fence(v, v.tensor_tensor(ZC, LG, CO, op=ALU.mult))
                v.nop().then_inc(sm["s_spx"], 1)
                v.wait_ge(sm["s_ksum"], u + 1)
                fence(v, v.tensor_scalar(TM, TS, 1.0, None, op0=ALU.subtract))
                fence(v, v.reciprocal(RK, KS))
                fence(v, v.tensor_tensor(TAU, TM, RK, op=ALU.mult))
                fence(v, v.tensor_scalar(WW, LG, TAU, 0.0, op0=ALU.subtract,
                                         op1=ALU.max))
                v.wait_ge(sm["s_pmm"], 2 * u + 2)
                if u >= 1:
                    v.wait_ge(sm["s_hwT"], 16 * u)
                for e in range(8):
                    op = v.tensor_scalar(hwtm[:, e * 16:(e + 1) * 16],
                                         pb[1][:, e * 16:(e + 1) * 16],
                                         WW[:, e:e + 1], None, op0=ALU.mult)
                    if e == 7:
                        op.then_inc(sm["s_hw"], 1)
            v.nop().then_inc(sm["s_ph"], 1)

        @block.scalar
        def _(s):
            s.wait_ge(sm["s_ph"], 10)
            for u in range(NU):
                s.wait_ge(sm["s_spx"], u + 1)
                s.activation(spd[:, 0:8], CO, AF.Copy, accum_out=KS)
                s.activation(spd[:, 8:16], ZC, AF.Copy, accum_out=TS
                             ).then_inc(sm["s_ksum"], 1)
            s.nop().then_inc(sm["s_ph"], 1)

        @block.sync
        def _(sy):
            sy.wait_ge(sm["s_ph"], 10)
            for u in range(NU):
                sy.wait_ge(sm["s_hw"], u + 1)
                sy.dma_start_transpose(hwT[:, u * 128:(u + 1) * 128],
                                       hwtm[:, 0:128]).then_inc(sm["s_hwT"], 16)
            sy.wait_ge(sm["s_hwT"], 16 * NU)
            sy.nop().then_inc(sm["s_ph"], 1)

        @block.gpsimd
        def _(g):
            g.wait_ge(sm["s_ph"], 10)
            g.nop().then_inc(sm["s_ph"], 1)

        @block.gpsimd
        def _(g):
            g.wait_ge(sm["s_ph"], 15)
            for c in range(4):
                src = [wk, wk, wv, wv][c]
                oc = c % 2
                if c >= 1:
                    g.wait_ge(sm["s_kvmm"], c * NT)
                g.dma_start(out=wbuf[:, :, :], in_=src[oc]).then_inc(sm["s_wld"], 16)
            g.nop().then_inc(sm["s_ph"], 1)

        @block.tensor
        def _(t):
            t.wait_ge(sm["s_ph"], 15)
            for c in range(4):
                t.wait_ge(sm["s_wld"], 16 * (c + 1))
                for tt in range(NT):
                    it = c * NT + tt
                    if it >= 2:
                        t.wait_ge(sm["s_kstg"], it - 1)
                    for k in range(NK):
                        mmop = t.matmul(pb[it % 2][:, :],
                                        xnT[:, k, tt * 128:(tt + 1) * 128],
                                        wbuf[:, k, :], start=(k == 0),
                                        stop=(k == NK - 1))
                        if k == NK - 1:
                            mmop.then_inc(sm["s_kvmm"], 1)
            t.nop().then_inc(sm["s_ph"], 1)

        @block.vector
        def _(v):
            v.wait_ge(sm["s_ph"], 15)
            for c in range(4):
                isk = c < 2
                for tt in range(NT):
                    it = c * NT + tt
                    v.wait_ge(sm["s_kvmm"], it + 1)
                    if it >= 2:
                        if it - 2 < 24:
                            v.wait_ge(skvtr[it % 2], 64 * ((it - 2) // 2 + 1))
                        else:
                            v.wait_ge(sm["s_kvsp"], 16 * (it - 1))
                    if isk:
                        psv, pss, cv, sv, ov, rA, rB = ropeviews(
                            pb[it % 2], tt, kvstg[it % 2])
                        v.tensor_tensor(rA, psv, cv, op=ALU.mult)
                        fence(v, v.tensor_tensor(rB, pss, sv, op=ALU.mult))
                        v.tensor_tensor(ov, rA, rB, op=ALU.add
                                        ).then_inc(sm["s_kstg"], 1)
                    else:
                        v.tensor_copy(kvstg[it % 2][:, :], pb[it % 2][:, :]
                                      ).then_inc(sm["s_kstg"], 1)
            v.nop().then_inc(sm["s_ph"], 1)

        @block.scalar
        def _(s):
            s.wait_ge(sm["s_ph"], 15)
            for c in range(2):
                for tt in range(NT):
                    it = c * NT + tt
                    s.wait_ge(sm["s_kstg"], it + 1)
                    if it >= 2:
                        s.wait_ge(sm["s_kvsp"], 16 * (it - 1))
                    for j in range(4):
                        s.dma_start_transpose(trstg[it % 2][:, j * 128:(j + 1) * 128],
                                              kvstg[it % 2][:, j * 128:(j + 1) * 128]
                                              ).then_inc(skvtr[it % 2], 16)
            s.nop().then_inc(sm["s_ph"], 1)

        @block.sync
        def _(sy):
            sy.wait_ge(sm["s_ph"], 15)
            for c in range(4):
                isk = c < 2
                oc = c % 2
                for tt in range(NT):
                    it = c * NT + tt
                    if isk:
                        sy.wait_ge(skvtr[it % 2], 64 * (it // 2 + 1))
                        sy.dma_start(
                            out=kts[4 * oc:4 * oc + 4, tt, :, :].transpose([1, 0, 2]),
                            in_=trstg[it % 2].rearrange("p (j q) -> p j q", j=4)
                        ).then_inc(sm["s_kvsp"], 16)
                    else:
                        sy.wait_ge(sm["s_kstg"], it + 1)
                        sy.dma_start(
                            out=vts[4 * oc:4 * oc + 4, tt, :, :].transpose([1, 0, 2]),
                            in_=kvstg[it % 2].rearrange("p (j q) -> p j q", j=4)
                        ).then_inc(sm["s_kvsp"], 16)
            sy.wait_ge(sm["s_kvsp"], 16 * 48)
            sy.nop().then_inc(sm["s_ph"], 1)

        @block.gpsimd
        def _(g):
            g.wait_ge(sm["s_ph"], 20)
            for oc in range(8):
                if oc >= 2:
                    g.wait_ge(sm["s_kvmm"], 48 + (oc - 1) * NU)
                dst = wbuf[:, :, :] if oc % 2 == 0 else wq2_full
                g.dma_start(out=dst, in_=wq[oc]).then_inc(
                    sm["s_wld"] if oc % 2 == 0 else sm["s_wld1"], 16)
            g.nop().then_inc(sm["s_ph"], 1)

        @block.tensor
        def _(t):
            t.wait_ge(sm["s_ph"], 20)
            for oc in range(8):
                if oc % 2 == 0:
                    t.wait_ge(sm["s_wld"], 16 * (4 + oc // 2 + 1))
                else:
                    t.wait_ge(sm["s_wld1"], 16 * (oc // 2 + 1))
                for u in range(NU):
                    tt = u + 4
                    it = oc * NU + u
                    if it >= 2:
                        t.wait_ge(sm["s_kstg"], 48 + it - 1)
                    for k in range(NK):
                        rhsw = wbuf[:, k, :] if oc % 2 == 0 else wq2(k)
                        mmop = t.matmul(pb[it % 2][:, :],
                                        xnT[:, k, tt * 128:(tt + 1) * 128],
                                        rhsw, start=(k == 0),
                                        stop=(k == NK - 1))
                        if k == NK - 1:
                            mmop.then_inc(sm["s_kvmm"], 1)
            t.nop().then_inc(sm["s_ph"], 1)

        @block.vector
        def _(v):
            v.wait_ge(sm["s_ph"], 20)
            for oc in range(8):
                for u in range(NU):
                    tt = u + 4
                    it = oc * NU + u
                    v.wait_ge(sm["s_kvmm"], 48 + it + 1)
                    if it >= 2:
                        v.wait_ge(skvtr[it % 2], 768 + 64 * ((it - 2) // 2 + 1))
                    psv, pss, cv, sv, ov, rA, rB = ropeviews(
                        pb[it % 2], tt, kvstg[it % 2])
                    v.tensor_tensor(rA, psv, cv, op=ALU.mult)
                    fence(v, v.tensor_tensor(rB, pss, sv, op=ALU.mult))
                    v.tensor_tensor(ov, rA, rB, op=ALU.add).then_inc(sm["s_kstg"], 1)
            v.nop().then_inc(sm["s_ph"], 1)

        @block.scalar
        def _(s):
            s.wait_ge(sm["s_ph"], 20)
            for oc in range(8):
                for u in range(NU):
                    it = oc * NU + u
                    s.wait_ge(sm["s_kstg"], 48 + it + 1)
                    if it >= 2:
                        s.wait_ge(sm["s_kvsp"], 16 * (48 + it - 1))
                    for j in range(4):
                        s.dma_start_transpose(trstg[it % 2][:, j * 128:(j + 1) * 128],
                                              kvstg[it % 2][:, j * 128:(j + 1) * 128]
                                              ).then_inc(skvtr[it % 2], 16)
            s.nop().then_inc(sm["s_ph"], 1)

        @block.sync
        def _(sy):
            sy.wait_ge(sm["s_ph"], 20)
            for oc in range(8):
                for u in range(NU):
                    it = oc * NU + u
                    sy.wait_ge(skvtr[it % 2], 768 + 64 * (it // 2 + 1))
                    sy.dma_start(
                        out=qts[4 * oc:4 * oc + 4, u, :, :].transpose([1, 0, 2]),
                        in_=trstg[it % 2].rearrange("p (j q) -> p j q", j=4)
                    ).then_inc(sm["s_kvsp"], 16)
            sy.wait_ge(sm["s_kvsp"], 16 * (48 + 64))
            for g4 in range(4):
                sy.dma_start(out=xns[:, g4 * 8:(g4 + 1) * 8, :],
                             in_=xnT[:, g4 * 8:(g4 + 1) * 8, 512:NT * 128]
                             ).then_inc(sm["s_xnsp"], 16)
            sy.wait_ge(sm["s_xnsp"], 64)
            sy.nop().then_inc(sm["s_ph"], 1)

        @block.gpsimd
        def _(g):
            g.wait_ge(sm["s_ph"], 25)
            g.dma_start(out=wbuf[:, 0:8, :], in_=mlo).then_inc(sm["s_msl"], 16)
            g.nop().then_inc(sm["s_ph"], 1)

        @block.sync
        def _(sy):
            sy.wait_ge(sm["s_ph"], 25)
            for h in range(NQH):
                if h >= 2:
                    sy.wait_ge(sm["s_sc"], ((h - 2) * 2 + 2) * 8)
                sy.dma_start(out=qh[h % 2].rearrange("p (u q) -> p u q", u=8),
                             in_=qts[h].transpose([1, 0, 2])
                             ).then_inc(sqld[h % 2], 16)
                if h % 4 == 0:
                    if h >= 4:
                        sy.wait_ge(sm["s_av"], 2 * h)
                    sy.dma_start(out=kh.rearrange("p (tt q) -> p tt q", tt=NT),
                                 in_=kts[h // 4].transpose([1, 0, 2])
                                 ).then_inc(sm["s_kvld"], 16)
                    sy.dma_start(out=vh.rearrange("p (tt q) -> p tt q", tt=NT),
                                 in_=vts[h // 4].transpose([1, 0, 2])
                                 ).then_inc(sm["s_kvld"], 16)
                if h >= 1:
                    sy.wait_ge(sm["s_avn"], 2 * h)
                    sy.dma_start(out=avs[h - 1][:, 0:512], in_=kvstg0[:, :]
                                 ).then_inc(sm["s_avsp"], 16)
                    sy.dma_start(out=avs[h - 1][:, 512:1024], in_=kvstg1[:, :]
                                 ).then_inc(sm["s_avsp"], 16)
            sy.wait_ge(sm["s_avn"], 2 * NQH)
            sy.dma_start(out=avs[NQH - 1][:, 0:512], in_=kvstg0[:, :]
                         ).then_inc(sm["s_avsp"], 16)
            sy.dma_start(out=avs[NQH - 1][:, 512:1024], in_=kvstg1[:, :]
                         ).then_inc(sm["s_avsp"], 16)
            sy.wait_ge(sm["s_avsp"], 32 * NQH)
            sy.nop().then_inc(sm["s_ph"], 1)

        @block.tensor
        def _(t):
            t.wait_ge(sm["s_ph"], 25)
            for h in range(NQH):
                t.wait_ge(sqld[h % 2], 16 * (h // 2 + 1))
                t.wait_ge(sm["s_kvld"], 32 * (h // 4 + 1))
                for qc in range(2):
                    it = h * 2 + qc
                    RORD = [4, 5, 6, 7, 0, 1, 2, 3]
                    for idx, r in enumerate(RORD):
                        kb = 4 * qc + r
                        lo, hi = _clip(r)
                        if it * 8 + idx >= 2:
                            t.wait_ge(sm["s_exp"], it * 8 + idx - 1)
                        t.matmul(pb[idx % 2][:, lo:hi],
                                 kh[:, kb * 128:(kb + 1) * 128],
                                 qh[h % 2][:, qc * 512 + lo:qc * 512 + hi],
                                 start=True, stop=True).then_inc(sm["s_sc"], 1)
                    pba = pb[2 + 2 * (it % 2)]
                    pbd = pb[3 + 2 * (it % 2)]
                    if it >= 2:
                        t.wait_ge(sm["s_avn"], it - 1)
                    t.wait_ge(sm["s_msk"], it * 8 + 4)
                    for idx, r in enumerate([4, 5, 6, 7]):
                        kb = 4 * qc + r
                        lo, hi = _clip(r)
                        t.matmul(pba[:, lo:hi], vh[:, kb * 128:(kb + 1) * 128],
                                 exp_sb[:, r, lo:hi], start=(idx == 0), stop=False)
                    for idx, r in enumerate([4, 5, 6, 7]):
                        lo, hi = _clip(r)
                        mmop = t.matmul(pbd[:, lo:hi], onesb[:, :],
                                        exp_sb[:, r, lo:hi], start=(idx == 0),
                                        stop=False)
                        if idx == 3:
                            mmop.then_inc(sm["s_avB"], 1)
                    t.wait_ge(sm["s_msk"], (it + 1) * 8)
                    for r in [0, 1, 2, 3]:
                        kb = 4 * qc + r
                        lo, hi = _clip(r)
                        t.matmul(pba[:, lo:hi], vh[:, kb * 128:(kb + 1) * 128],
                                 exp_sb[:, r, lo:hi], start=False, stop=(r == 3))
                    for r in [0, 1, 2, 3]:
                        lo, hi = _clip(r)
                        mmop = t.matmul(pbd[:, lo:hi], onesb[:, :],
                                        exp_sb[:, r, lo:hi], start=False,
                                        stop=(r == 3))
                        if r == 3:
                            mmop.then_inc(sm["s_av"], 1)
            t.nop().then_inc(sm["s_ph"], 1)

        @block.scalar
        def _(s):
            s.wait_ge(sm["s_ph"], 25)
            for h in range(NQH):
                for qc in range(2):
                    it = h * 2 + qc
                    RORD = [4, 5, 6, 7, 0, 1, 2, 3]
                    for idx, r in enumerate(RORD):
                        lo, hi = _clip(r)
                        s.wait_ge(sm["s_sc"], it * 8 + idx + 1)
                        if it >= 1 and idx == 0:
                            s.wait_ge(sm["s_avB"], it)
                        if it >= 1 and idx == 4:
                            s.wait_ge(sm["s_av"], it)
                        s.activation(exp_sb[:, r, lo:hi], pb[idx % 2][:, lo:hi],
                                     AF.Exp, bias=nb24[:, 0:1], scale=SCALE
                                     ).then_inc(sm["s_exp"], 1)
            s.nop().then_inc(sm["s_ph"], 1)

        @block.vector
        def _(v):
            v.wait_ge(sm["s_ph"], 25)
            v.wait_ge(sm["s_msl"], 16)
            for h in range(NQH):
                for qc in range(2):
                    it = h * 2 + qc
                    RORD = [4, 5, 6, 7, 0, 1, 2, 3]
                    for idx, r in enumerate(RORD):
                        lo, hi = _clip(r)
                        v.wait_ge(sm["s_exp"], it * 8 + idx + 1)
                        if r < 4:
                            v.tensor_tensor(exp_sb[:, r, lo:hi],
                                            exp_sb[:, r, lo:hi],
                                            wbuf[:, 4 * qc + r, lo:hi],
                                            op=ALU.mult).then_inc(sm["s_msk"], 1)
                        else:
                            d0 = 128 * (r - 4)
                            v.tensor_tensor(exp_sb[:, r, d0:d0 + 128],
                                            exp_sb[:, r, d0:d0 + 128],
                                            trisb[:, :], op=ALU.mult
                                            ).then_inc(sm["s_msk"], 1)
                    v.wait_ge(sm["s_av"], it + 1)
                    if qc == 0 and h >= 1:
                        v.wait_ge(sm["s_avsp"], 32 * h)
                    fence(v, v.reciprocal(den_r[:, :], pb[3 + 2 * (it % 2)][:, :]))
                    v.tensor_tensor(avh[qc][:, :], pb[2 + 2 * (it % 2)][:, :],
                                    den_r[:, :],
                                    op=ALU.mult).then_inc(sm["s_avn"], 1)
            v.nop().then_inc(sm["s_ph"], 1)

        @block.gpsimd
        def _(g):
            g.wait_ge(sm["s_ph"], 30)
            for oc in range(8):
                if oc >= 2:
                    g.wait_ge(sm["s_smm"], (oc - 1) * NU)
                dst = wbuf[:, :, :] if oc % 2 == 0 else wz2_full
                g.dma_start(out=dst, in_=wo[oc]).then_inc(
                    sm["s_wld"] if oc % 2 == 0 else sm["s_wld1"], 16)
            g.nop().then_inc(sm["s_ph"], 1)

        @block.sync
        def _(sy):
            sy.wait_ge(sm["s_ph"], 30)
            for h in range(NQH):
                sy.dma_start(out=xnT[:, h, 0:1024], in_=avs[h]
                             ).then_inc(sm["s_avld"], 16)
            for oc in range(8):
                for u in range(NU):
                    it = oc * NU + u
                    sy.wait_ge(sm["s_scp"], it + 1)
                    sy.dma_start(out=ss[u][:, oc * 512:(oc + 1) * 512],
                                 in_=sstg[it % 2][:, :]).then_inc(sm["s_ssp"], 16)
            sy.wait_ge(sm["s_ssp"], 16 * 64)
            sy.nop().then_inc(sm["s_ph"], 1)

        @block.tensor
        def _(t):
            t.wait_ge(sm["s_ph"], 30)
            t.wait_ge(sm["s_avld"], 16 * NQH)
            for oc in range(8):
                if oc % 2 == 0:
                    t.wait_ge(sm["s_wld"], 16 * (8 + oc // 2 + 1))
                else:
                    t.wait_ge(sm["s_wld1"], 16 * (4 + oc // 2 + 1))
                for u in range(NU):
                    it = oc * NU + u
                    if it >= 4:
                        t.wait_ge(sm["s_scp"], it - 3)
                    for k in range(NK):
                        rhsw = wbuf[:, k, :] if oc % 2 == 0 else wz2(k)
                        mmop = t.matmul(pb[it % 4][:, :], xnT[:, k, u * 128:(u + 1) * 128],
                                        rhsw, start=(k == 0),
                                        stop=(k == NK - 1))
                        if k == NK - 1:
                            mmop.then_inc(sm["s_smm"], 1)
            t.nop().then_inc(sm["s_ph"], 1)

        @block.scalar
        def _(s):
            s.wait_ge(sm["s_ph"], 30)
            for oc in range(8):
                for u in range(NU):
                    it = oc * NU + u
                    s.wait_ge(sm["s_smm"], it + 1)
                    if it >= 2:
                        s.wait_ge(sm["s_ssp"], 16 * (it - 1))
                    s.activation(sstg[it % 2][:, :], pb[it % 4][:, :], AF.Copy
                                 ).then_inc(sm["s_scp"], 1)
            s.nop().then_inc(sm["s_ph"], 1)

        @block.vector
        def _(v):
            v.wait_ge(sm["s_ph"], 30)
            v.nop().then_inc(sm["s_ph"], 1)

        @block.gpsimd
        def _(g):
            g.wait_ge(sm["s_ph"], 35)
            for oc in range(8):
                if oc >= 2:
                    g.wait_ge(sm["s_fmm"], (oc - 1) * NU)
                dst = wbuf[:, :, :] if oc % 2 == 0 else wz2_full
                g.dma_start(out=dst, in_=wb[oc]).then_inc(
                    sm["s_wld"] if oc % 2 == 0 else sm["s_wld1"], 16)
                if oc >= 1:
                    g.wait_ge(sm["s_fmm"], oc * NU)
                g.dma_start(out=lbs[oc % 2][:, :], in_=lb[:, oc * 512:(oc + 1) * 512]
                            ).then_inc(sm["s_lbl"], 16)
            g.nop().then_inc(sm["s_ph"], 1)

        @block.sync
        def _(sy):
            sy.wait_ge(sm["s_ph"], 35)
            for g4 in range(4):
                sy.dma_start(out=xnT[:, g4 * 8:(g4 + 1) * 8, 0:1024],
                             in_=xns[:, g4 * 8:(g4 + 1) * 8, :]
                             ).then_inc(sm["s_avld"], 16)
            for oc in range(8):
                if oc >= 1:
                    sy.wait_ge(sm["s_fb"], oc * NU)
                sy.dma_start(out=bbs[:, :], in_=bb[:, oc * 512:(oc + 1) * 512]
                             ).then_inc(sm["s_bbl"], 16)
                for u in range(NU):
                    it = oc * NU + u
                    if it >= 2:
                        sy.wait_ge(sm["s_fg"], it - 1)
                    sy.dma_start(out=sstg[it % 2][:, :],
                                 in_=ss[u][:, oc * 512:(oc + 1) * 512]
                                 ).then_inc(ssld[it % 2], 16)
                    sy.dma_start(out=xck[it % 2][:, :],
                                 in_=xs[(u + 4) * 128:(u + 5) * 128,
                                        oc * 512:(oc + 1) * 512]
                                 ).then_inc(sxcl[it % 2], 16)
                    sy.wait_ge(sm["s_fo"], it + 1)
                    sy.dma_start(out=os_[u * 128:(u + 1) * 128,
                                         oc * 512:(oc + 1) * 512],
                                 in_=ostg[it % 2][:, :]).then_inc(sm["s_ost"], 16)
            sy.wait_ge(sm["s_ost"], 16 * 64)

        @block.tensor
        def _(t):
            t.wait_ge(sm["s_ph"], 35)
            t.wait_ge(sm["s_avld"], 16 * (NQH + 4))
            for oc in range(8):
                if oc % 2 == 0:
                    t.wait_ge(sm["s_wld"], 16 * (12 + oc // 2 + 1))
                else:
                    t.wait_ge(sm["s_wld1"], 16 * (8 + oc // 2 + 1))
                t.wait_ge(sm["s_lbl"], 16 * (oc + 1))
                for u in range(NU):
                    it = oc * NU + u
                    if it >= 4:
                        t.wait_ge(sm["s_scp"], 64 + it - 3)
                    for k in range(NK):
                        rhsw = wbuf[:, k, :] if oc % 2 == 0 else wz2(k)
                        t.matmul(pb[it % 4][:, :], xnT[:, k, u * 128:(u + 1) * 128],
                                 rhsw, start=(k == 0), stop=False)
                    t.matmul(pb[it % 4][:, :], hwT[:, u * 128:(u + 1) * 128],
                             lbs[oc % 2][:, :], start=False, stop=True
                             ).then_inc(sm["s_fmm"], 1)
            t.nop().then_inc(sm["s_ph"], 1)

        @block.vector
        def _(v):
            v.wait_ge(sm["s_ph"], 35)
            for oc in range(8):
                for u in range(NU):
                    it = oc * NU + u
                    v.wait_ge(sm["s_fmm"], it + 1)
                    v.wait_ge(sm["s_bbl"], 16 * (oc + 1))
                    v.tensor_tensor(pb[it % 4][:, :], pb[it % 4][:, :], bbs[:, :],
                                    op=ALU.add).then_inc(sm["s_fb"], 1)
                    v.wait_ge(sm["s_scp"], 64 + it + 1)
                    v.wait_ge(ssld[it % 2], 16 * (it // 2 + 1))
                    v.wait_ge(sxcl[it % 2], 16 * (it // 2 + 1))
                    if it >= 2:
                        v.wait_ge(sm["s_ost"], 16 * (it - 1))
                    fence(v, v.tensor_tensor(ostg[it % 2][:, :], sstg[it % 2][:, :],
                                              gstg[it % 2][:, :], op=ALU.mult))
                    v.tensor_tensor(ostg[it % 2][:, :], ostg[it % 2][:, :],
                                    xck[it % 2][:, :], op=ALU.add
                                    ).then_inc(sm["s_fo"], 1)
                    v.nop().then_inc(sm["s_fg"], 1)

        @block.scalar
        def _(s):
            s.wait_ge(sm["s_ph"], 35)
            for oc in range(8):
                for u in range(NU):
                    it = oc * NU + u
                    s.wait_ge(sm["s_fb"], it + 1)
                    if it >= 2:
                        s.wait_ge(sm["s_fg"], it - 1)
                    s.activation(gstg[it % 2][:, :], pb[it % 4][:, :], AF.Silu
                                 ).then_inc(sm["s_scp"], 1)

    return nc


def _tables(h_half):
    theta = 10000.0
    inv = 1.0 / theta ** (np.arange(0, 128, 2, dtype=np.float64) / 128.0)
    loc = np.arange(NT * 128)
    if h_half == 0:
        pos = np.where(loc < HALO, 0, loc - HALO).astype(np.float64)
    else:
        pos = (loc + HALO).astype(np.float64)
    ang = pos[:, None] * inv[None, :]
    cos64 = np.cos(ang)
    sin64 = np.sin(ang)
    cosT = np.concatenate([cos64, cos64], axis=1)
    sinA = np.concatenate([-sin64, sin64], axis=1)
    out = []
    for tab in (cosT, sinA):
        hi = tab.astype(np.float32)
        lo = (tab - hi.astype(np.float64)).astype(np.float32)
        out.append(np.ascontiguousarray(
            hi.reshape(NT, 128, 128).transpose(1, 0, 2).reshape(128, NT * 128)))
        out.append(np.ascontiguousarray(
            lo.reshape(NT, 128, 128).transpose(1, 0, 2).reshape(128, NT * 128)))
    return out


def _masks(h_half):
    m = np.zeros((128, 8, 512), dtype=np.float32)
    kc = np.arange(128)[:, None]
    qc = np.arange(512)[None, :]
    for qch in range(2):
        for r in range(4):
            if h_half == 0 and qch == 0:
                continue
            lo, hi = _clip(r)
            allow = (qc <= r * 128 + kc) & (qc < hi)
            m[:, qch * 4 + r, :] = allow.astype(np.float32)
    return m.astype(ml_dtypes.bfloat16)


def _w_tiles(w, noc):
    return np.ascontiguousarray(
        w.reshape(NK, 128, noc, 512).transpose(2, 1, 0, 3))


def kernel(L0_attn_hidden_states, L0_attn_Wq, L0_attn_Wk, L0_attn_Wv, L0_attn_Wo,
           L0_ffn_W_base, L0_ffn_b_base, L0_ffn_W_router, L0_ffn_b_router,
           L0_ffn_lora_A, L0_ffn_lora_B):
    x = np.ascontiguousarray(L0_attn_hidden_states, dtype=np.float32)
    wq_h = _w_tiles(np.ascontiguousarray(L0_attn_Wq, dtype=np.float32), 8)
    wk_h = _w_tiles(np.ascontiguousarray(L0_attn_Wk, dtype=np.float32), 2)
    wv_h = _w_tiles(np.ascontiguousarray(L0_attn_Wv, dtype=np.float32), 2)
    wo_h = _w_tiles(np.ascontiguousarray(L0_attn_Wo, dtype=np.float32), 8)
    wb_h = _w_tiles(np.ascontiguousarray(L0_ffn_W_base.T, dtype=np.float32), 8)
    wr_h = np.ascontiguousarray(
        np.asarray(L0_ffn_W_router, dtype=np.float32).T.reshape(NK, 128, 8)
        .transpose(1, 0, 2))
    la_flat = np.asarray(L0_ffn_lora_A, dtype=np.float32).transpose(1, 0, 2).reshape(H, 128)
    la_h = np.ascontiguousarray(la_flat.reshape(NK, 128, 128).transpose(1, 0, 2))
    lb_h = np.ascontiguousarray(np.asarray(L0_ffn_lora_B, dtype=np.float32).reshape(128, H))
    bb_h = np.ascontiguousarray(
        np.broadcast_to(np.asarray(L0_ffn_b_base, dtype=np.float32), (128, H)))
    br_h = np.ascontiguousarray(
        np.broadcast_to(np.asarray(L0_ffn_b_router, dtype=np.float32), (128, 8)))
    tri_h = (np.arange(128)[:, None] <= np.arange(128)[None, :]
             ).astype(ml_dtypes.bfloat16)
    one_h = np.ones((128, 128), dtype=ml_dtypes.bfloat16)
    idn_h = np.eye(128, dtype=ml_dtypes.bfloat16)

    if "nc" not in _NC_CACHE:
        _NC_CACHE["nc"] = build_program()
    nc = _NC_CACHE["nc"]

    tabs = [_tables(0), _tables(1)]
    msks = [_masks(0), _masks(1)]
    in_maps = []
    for c in range(N_CORES):
        b, hh = c // 2, c % 2
        if hh == 0:
            xsl = np.concatenate([np.zeros((HALO, H), np.float32), x[b, 0:1024]],
                                 axis=0)
        else:
            xsl = np.ascontiguousarray(x[b, HALO:2048])
        chi_h, clo_h, shi_h, slo_h = tabs[hh]
        in_maps.append({
            "xs": xsl, "wq": wq_h, "wk": wk_h, "wv": wv_h, "wo": wo_h,
            "wb": wb_h, "wr": wr_h, "la": la_h, "lb": lb_h, "bb": bb_h,
            "br": br_h, "chi": chi_h, "clo": clo_h, "shi": shi_h, "slo": slo_h,
            "mlo": msks[hh], "tri": tri_h, "one": one_h, "idn": idn_h,
        })
    res = run_bass_kernel_spmd(nc, in_maps, core_ids=list(range(N_CORES)))
    out = np.empty((B, S, H), dtype=np.float32)
    for c in range(N_CORES):
        b, hh = c // 2, c % 2
        out[b, hh * 1024:(hh + 1) * 1024] = res.results[c]["os_"]
    return out


# version 09
# speedup vs baseline: 1.3781x; 1.0352x over previous
import contextlib
import math

import numpy as np
import ml_dtypes

import concourse.bass as bass
import concourse.mybir as mybir
from concourse.bass_utils import run_bass_kernel_spmd

F32 = mybir.dt.float32
BF16 = mybir.dt.bfloat16
AF = mybir.ActivationFunctionType
ALU = mybir.AluOpType
AX = mybir.AxisListType

H = 4096
S = 2048
B = 4
NT = 12
NU = 8
HALO = 512
NQH = 32
NKV = 8
NK = 32
SCALE = 1.0 / math.sqrt(128.0)
ESHIFT = -24.0
N_CORES = 8

_NC_CACHE = {}


def _clip(r):
    if r < 4:
        return 0, 128 * (r + 1)
    return 128 * (r - 4), 512


def build_program():
    nc = bass.Bass("TRN2", target_bir_lowering=False)

    xs = nc.dram_tensor("xs", [NT * 128, H], F32, kind="ExternalInput").ap()
    wq = nc.dram_tensor("wq", [8, 128, NK, 512], F32, kind="ExternalInput").ap()
    wk = nc.dram_tensor("wk", [2, 128, NK, 512], F32, kind="ExternalInput").ap()
    wv = nc.dram_tensor("wv", [2, 128, NK, 512], F32, kind="ExternalInput").ap()
    wo = nc.dram_tensor("wo", [8, 128, NK, 512], F32, kind="ExternalInput").ap()
    wb = nc.dram_tensor("wb", [8, 128, NK, 512], F32, kind="ExternalInput").ap()
    wr = nc.dram_tensor("wr", [128, NK, 8], F32, kind="ExternalInput").ap()
    la = nc.dram_tensor("la", [128, NK, 128], F32, kind="ExternalInput").ap()
    lb = nc.dram_tensor("lb", [128, H], F32, kind="ExternalInput").ap()
    bb = nc.dram_tensor("bb", [128, H], F32, kind="ExternalInput").ap()
    br = nc.dram_tensor("br", [128, 8], F32, kind="ExternalInput").ap()
    chi = nc.dram_tensor("chi", [128, NT * 128], F32, kind="ExternalInput").ap()
    clo = nc.dram_tensor("clo", [128, NT * 128], F32, kind="ExternalInput").ap()
    shi = nc.dram_tensor("shi", [128, NT * 128], F32, kind="ExternalInput").ap()
    slo = nc.dram_tensor("slo", [128, NT * 128], F32, kind="ExternalInput").ap()
    mlo = nc.dram_tensor("mlo", [128, 8, 512], BF16, kind="ExternalInput").ap()
    tri = nc.dram_tensor("tri", [128, 128], BF16, kind="ExternalInput").ap()
    one = nc.dram_tensor("one", [128, 128], BF16, kind="ExternalInput").ap()
    idn = nc.dram_tensor("idn", [128, 128], BF16, kind="ExternalInput").ap()
    os_ = nc.dram_tensor("os_", [NU * 128, H], F32, kind="ExternalOutput").ap()
    kts = nc.dram_tensor("kts", [NKV, NT, 128, 128], BF16, kind="Internal").ap()
    vts = nc.dram_tensor("vts", [NKV, NT, 128, 128], BF16, kind="Internal").ap()
    qts = nc.dram_tensor("qts", [NQH, NU, 128, 128], BF16, kind="Internal").ap()
    avs = nc.dram_tensor("avs", [NQH, 128, 1024], BF16, kind="Internal").ap()
    xns = nc.dram_tensor("xns", [128, NK, 1024], BF16, kind="Internal").ap()
    ss = nc.dram_tensor("ss", [NU, 128, H], F32, kind="Internal").ap()

    st = contextlib.ExitStack()
    with st:
        sb = lambda *a, **k: st.enter_context(nc.sbuf_tensor(*a, **k))
        xnT = sb("xnT", [128, NK, NT * 128], BF16, is_transpose_dest=True)
        wbuf = sb("wbuf", [128, NK, 512], BF16)
        xbuf = sb("xbuf", [128, H // 2], F32)
        adum = sb("adum", [128, H // 2], F32)
        xtm0 = sb("xtm0", [128, H], BF16)
        xtm1 = sb("xtm1", [128, H], BF16)
        sqsc = sb("sqsc", [128, NT * 128], BF16)
        cosv = sb("cosv", [128, NT * 128], F32)
        sinv = sb("sinv", [128, NT * 128], F32)
        stats = sb("stats", [128, NT, 8], F32)
        spx = sb("spx", [128, 128], F32)
        spd = sb("spd", [128, 16], F32)
        hwT = sb("hwT", [128, NU * 128], BF16, is_transpose_dest=True)
        hwtm = sb("hwtm", [128, 128], BF16)
        wrsb = sb("wrsb", [128, NK, 8], BF16)
        brsb = sb("brsb", [128, 8], F32)
        trisb = sb("trisb", [128, 128], BF16)
        onesb = sb("onesb", [128, 128], BF16)
        nb24 = sb("nb24", [128, 1], F32)
        idnsb = sb("idnsb", [128, 128], BF16)
        kvstg0 = sb("kvstg0", [128, 512], BF16)
        kvstg1 = sb("kvstg1", [128, 512], BF16)
        trstg0 = sb("trstg0", [128, 512], BF16, is_transpose_dest=True)
        trstg1 = sb("trstg1", [128, 512], BF16, is_transpose_dest=True)
        sstg0 = sb("sstg0", [128, 512], F32)
        sstg1 = sb("sstg1", [128, 512], F32)
        gstg0 = sb("gstg0", [128, 512], F32)
        gstg1 = sb("gstg1", [128, 512], F32)
        ostg0 = sb("ostg0", [128, 512], F32)
        ostg1 = sb("ostg1", [128, 512], F32)
        xck0 = sb("xck0", [128, 512], F32)
        xck1 = sb("xck1", [128, 512], F32)
        bbs = sb("bbs", [128, 512], F32)
        lbs0 = sb("lbs0", [128, 512], BF16)

        exp_sb = xtm0.rearrange("p (r q) -> p r q", r=8)
        qh = [xtm1[:, 0:1024], xtm1[:, 1024:2048]]
        kh = xtm1[:, 2048:2048 + NT * 128]
        vh = sqsc
        avh = [kvstg0, kvstg1]
        den_r = gstg0
        rtA, rtB = sstg0, sstg1

        pb = [st.enter_context(nc.psum_tensor(f"pb{i}", [128, 512], F32))
              for i in range(6)]
        pt = [st.enter_context(nc.psum_tensor(f"pt{i}", [128, 128], BF16))
              for i in range(2)]

        semnames = [
            "s_ph", "s_xld", "s_stat", "s_rec", "s_rsq", "s_apl",
            "s_trA0", "s_trA1", "s_sml", "s_sm2", "s_tbl", "s_pmm", "s_spx",
            "s_ksum", "s_hw", "s_hwT", "s_wld", "s_wld1", "s_kvmm", "s_kstg",
            "s_kvtr0", "s_kvtr1", "s_kvsp", "s_xnsp", "s_qld0", "s_qld1",
            "s_kvld", "s_sc", "s_exp", "s_msk", "s_av", "s_avn", "s_avsp",
            "s_avld", "s_smm", "s_scp", "s_ssp", "s_sld0", "s_sld1",
            "s_xcl0", "s_xcl1", "s_fmm", "s_fb", "s_fo", "s_fg", "s_ost",
            "s_lbl", "s_bbl", "s_msl", "s_dvf", "s_ttr", "s_tcp", "s_avB",
        ]
        sm = {n: st.enter_context(nc.semaphore(n)) for n in semnames}
        block = st.enter_context(nc.Block())

        xtm = [xtm0, xtm1]
        strA = [sm["s_trA0"], sm["s_trA1"]]
        kvstg = [kvstg0, kvstg1]
        trstg = [trstg0, trstg1]
        skvtr = [sm["s_kvtr0"], sm["s_kvtr1"]]
        sqld = [sm["s_qld0"], sm["s_qld1"]]
        sstg = [sstg0, sstg1]
        gstg = [gstg0, gstg1]
        ostg = [ostg0, ostg1]
        xck = [xck0, xck1]
        lbs = [lbs0, lbs0]
        ssld = [sm["s_sld0"], sm["s_sld1"]]
        sxcl = [sm["s_xcl0"], sm["s_xcl1"]]

        dvf = [0]
        wq2 = lambda k: xnT[:, k, 0:512]
        wz2 = lambda k: xnT[:, k, 1024:1536]
        wq2_full = xnT[:, :, 0:512]
        wz2_full = xnT[:, :, 1024:1536]

        def fence(v, instop):
            dvf[0] += 1
            instop.then_inc(sm["s_dvf"], 1)
            v.wait_ge(sm["s_dvf"], dvf[0])

        def ropeviews(ps, tt, outbuf):
            psv = ps.rearrange("p (h u j) -> p h u j", h=4, u=2)
            pss = bass.AP(ps, 64, [[512, 128], [128, 4], [-64, 2], [1, 64]])
            cv = bass.AP(cosv, tt * 128,
                         [[NT * 128, 128], [0, 4], [64, 2], [1, 64]])
            sv = bass.AP(sinv, tt * 128,
                         [[NT * 128, 128], [0, 4], [64, 2], [1, 64]])
            ov = outbuf.rearrange("p (h u j) -> p h u j", h=4, u=2)
            rA = rtA.rearrange("p (h u j) -> p h u j", h=4, u=2)
            rB = rtB.rearrange("p (h u j) -> p h u j", h=4, u=2)
            return psv, pss, cv, sv, ov, rA, rB

        @block.sync
        def _(sy):
            sy.dma_start(out=brsb[:, :], in_=br).then_inc(sm["s_sml"], 16)
            sy.dma_start(out=trisb[:, :], in_=tri).then_inc(sm["s_sml"], 16)
            sy.dma_start(out=onesb[:, :], in_=one).then_inc(sm["s_sml"], 16)
            sy.dma_start(out=idnsb[:, :], in_=idn).then_inc(sm["s_sml"], 16)
            sy.dma_start(out=cosv[:, :], in_=chi).then_inc(sm["s_tbl"], 16)
            sy.dma_start(out=sinv[:, :], in_=shi).then_inc(sm["s_tbl"], 16)
            sy.wait_ge(sm["s_sml"], 64)
            sy.nop().then_inc(sm["s_ph"], 1)

        @block.gpsimd
        def _(g):
            g.dma_start(out=wrsb[:, :, :], in_=wr).then_inc(sm["s_sm2"], 16)
            g.wait_ge(sm["s_tbl"], 32)
            g.dma_start(out=cosv[:, :], in_=clo, accum_op=ALU.add
                        ).then_inc(sm["s_sm2"], 16)
            g.dma_start(out=sinv[:, :], in_=slo, accum_op=ALU.add
                        ).then_inc(sm["s_sm2"], 16)
            g.wait_ge(sm["s_sm2"], 48)
            g.nop().then_inc(sm["s_ph"], 1)

        @block.vector
        def _(v):
            v.memset(nb24[:, :], ESHIFT)
            v.nop().then_inc(sm["s_ph"], 1)

        @block.scalar
        def _(s):
            s.nop().then_inc(sm["s_ph"], 1)

        @block.tensor
        def _(t):
            t.nop().then_inc(sm["s_ph"], 1)

        @block.sync
        def _(sy):
            sy.wait_ge(sm["s_ph"], 5)
            HH = H // 2
            for t in range(NT):
                for j, gate in enumerate([
                        ("apl", 2 * t), ("stat", 2 * t + 1),
                        ("stat", 2 * t + 2), ("apl", 2 * t + 1)]):
                    nm, val = gate
                    if val > 0:
                        sy.wait_ge(sm["s_apl" if nm == "apl" else "s_stat"], val)
                    half = j % 2
                    sy.dma_start(out=xbuf[:, :],
                                 in_=xs[t * 128:(t + 1) * 128,
                                        half * HH:(half + 1) * HH]
                                 ).then_inc(sm["s_xld"], 16)
            sy.nop().then_inc(sm["s_ph"], 1)

        @block.scalar
        def _(s):
            s.wait_ge(sm["s_ph"], 5)
            for t in range(NT):
                pa = t % 2
                for half in range(2):
                    s.wait_ge(sm["s_xld"], 16 * (4 * t + half + 1))
                    s.activation(adum[:, :], xbuf[:, :], AF.Copy,
                                 accum_out=stats[:, t, 0 + half:1 + half])
                    s.activation(adum[:, :], xbuf[:, :], AF.Square,
                                 accum_out=stats[:, t, 2 + half:3 + half]
                                 ).then_inc(sm["s_stat"], 1)
                s.wait_ge(sm["s_rec"], t + 1)
                s.activation(stats[:, t, 6:7], stats[:, t, 5:6], AF.Sqrt
                             ).then_inc(sm["s_rsq"], 1)
            s.nop().then_inc(sm["s_ph"], 1)

        @block.vector
        def _(v):
            v.wait_ge(sm["s_ph"], 5)
            HH = H // 2
            for t in range(NT):
                pa = t % 2
                v.wait_ge(sm["s_stat"], 2 * t + 2)
                fence(v, v.tensor_tensor(stats[:, t, 4:5], stats[:, t, 0:1],
                                          stats[:, t, 1:2], op=ALU.add))
                fence(v, v.tensor_tensor(stats[:, t, 5:6], stats[:, t, 2:3],
                                         stats[:, t, 3:4], op=ALU.add))
                fence(v, v.tensor_scalar(stats[:, t, 4:5], stats[:, t, 4:5],
                                         1.0 / H, None, op0=ALU.mult))
                fence(v, v.tensor_scalar(stats[:, t, 5:6], stats[:, t, 5:6],
                                         1.0 / H, None, op0=ALU.mult))
                fence(v, v.tensor_tensor(stats[:, t, 7:8], stats[:, t, 4:5],
                                         stats[:, t, 4:5], op=ALU.mult))
                fence(v, v.tensor_tensor(stats[:, t, 5:6], stats[:, t, 5:6],
                                         stats[:, t, 7:8], op=ALU.subtract))
                fence(v, v.tensor_scalar(stats[:, t, 5:6], stats[:, t, 5:6],
                                         1e-5, None, op0=ALU.add))
                v.reciprocal(stats[:, t, 5:6], stats[:, t, 5:6]
                             ).then_inc(sm["s_rec"], 1)
                v.wait_ge(sm["s_rsq"], t + 1)
                if t >= 2:
                    v.wait_ge(sm["s_ttr"], NK * (t - 1))
                for half in range(2):
                    v.wait_ge(sm["s_xld"], 16 * (4 * t + 2 + half + 1))
                    v.tensor_scalar(xtm[pa][:, half * HH:(half + 1) * HH],
                                    xbuf[:, :], stats[:, t, 4:5],
                                    stats[:, t, 6:7], op0=ALU.subtract,
                                    op1=ALU.mult).then_inc(sm["s_apl"], 1)
                for k in range(NK):
                    gidx = t * NK + k
                    v.wait_ge(sm["s_ttr"], gidx + 1)
                    v.tensor_copy(xnT[:, k, t * 128:(t + 1) * 128],
                                  pt[k % 2][:, :]).then_inc(sm["s_tcp"], 1)
            v.nop().then_inc(sm["s_ph"], 1)

        @block.tensor
        def _(t):
            t.wait_ge(sm["s_ph"], 5)
            t.wait_ge(sm["s_sml"], 64)
            for tt in range(NT):
                pa = tt % 2
                t.wait_ge(sm["s_apl"], 2 * tt + 2)
                for k in range(NK):
                    gidx = tt * NK + k
                    if gidx >= 2:
                        t.wait_ge(sm["s_tcp"], gidx - 1)
                    t.transpose(pt[k % 2][:, :],
                                xtm[pa][:, k * 128:(k + 1) * 128], idnsb[:, :]
                                ).then_inc(sm["s_ttr"], 1)
            t.nop().then_inc(sm["s_ph"], 1)

        @block.gpsimd
        def _(g):
            g.wait_ge(sm["s_ph"], 5)
            g.dma_start(out=wbuf[:, :, 0:128], in_=la).then_inc(sm["s_sm2"], 16)
            g.nop().then_inc(sm["s_ph"], 1)

        CMP = spx[:, 0:64].rearrange("p (i j) -> p i j", i=8)
        LG = spx[:, 64:72]
        KI = spx[:, 72:80]
        SI = spx[:, 80:88]
        KZ = spx[:, 88:96]
        CO = spx[:, 96:104]
        ZC = spx[:, 104:112]
        KS = spx[:, 112:113]
        TS = spx[:, 113:114]
        TM = spx[:, 114:115]
        RK = spx[:, 115:116]
        TAU = spx[:, 116:117]
        WW = spx[:, 120:128]

        @block.tensor
        def _(t):
            t.wait_ge(sm["s_ph"], 10)
            t.wait_ge(sm["s_sm2"], 64)
            for u in range(NU):
                tt = u + 4
                if u >= 1:
                    t.wait_ge(sm["s_spx"], u)
                    t.wait_ge(sm["s_hw"], u)
                for k in range(NK):
                    t.matmul(pb[0][:, 0:8], xnT[:, k, tt * 128:(tt + 1) * 128],
                             wrsb[:, k, :], start=(k == 0), stop=(k == NK - 1)
                             ).then_inc(sm["s_pmm"], 1) if k == NK - 1 else \
                        t.matmul(pb[0][:, 0:8], xnT[:, k, tt * 128:(tt + 1) * 128],
                                 wrsb[:, k, :], start=(k == 0), stop=False)
                for k in range(NK):
                    t.matmul(pb[1][:, 0:128], xnT[:, k, tt * 128:(tt + 1) * 128],
                             wbuf[:, k, 0:128], start=(k == 0), stop=(k == NK - 1)
                             ).then_inc(sm["s_pmm"], 1) if k == NK - 1 else \
                        t.matmul(pb[1][:, 0:128], xnT[:, k, tt * 128:(tt + 1) * 128],
                                 wbuf[:, k, 0:128], start=(k == 0), stop=False)
            t.nop().then_inc(sm["s_ph"], 1)

        @block.vector
        def _(v):
            v.wait_ge(sm["s_ph"], 10)
            for u in range(NU):
                v.wait_ge(sm["s_pmm"], 2 * u + 1)
                fence(v, v.tensor_tensor(LG, pb[0][:, 0:8], brsb[:, :], op=ALU.add))
                for i in range(8):
                    v.tensor_scalar(CMP[:, i, :], LG, LG[:, i:i + 1], None,
                                    op0=ALU.is_ge)
                fence(v, v.tensor_scalar(CMP[:, 7, :], LG, LG[:, 7:8], None,
                                         op0=ALU.is_ge))
                fence(v, v.tensor_reduce(KI, CMP, axis=AX.X, op=ALU.add))
                fence(v, v.tensor_tensor(CMP, CMP,
                                         bass.AP(spx, 64, [[128, 128], [0, 8], [1, 8]]),
                                         op=ALU.mult))
                fence(v, v.tensor_reduce(SI, CMP, axis=AX.X, op=ALU.add))
                fence(v, v.tensor_tensor(KZ, KI, LG, op=ALU.mult))
                fence(v, v.tensor_scalar(KZ, KZ, 1.0, None, op0=ALU.add))
                fence(v, v.tensor_tensor(CO, KZ, SI, op=ALU.is_gt))
                fence(v, v.tensor_tensor(ZC, LG, CO, op=ALU.mult))
                v.nop().then_inc(sm["s_spx"], 1)
                v.wait_ge(sm["s_ksum"], u + 1)
                fence(v, v.tensor_scalar(TM, TS, 1.0, None, op0=ALU.subtract))
                fence(v, v.reciprocal(RK, KS))
                fence(v, v.tensor_tensor(TAU, TM, RK, op=ALU.mult))
                fence(v, v.tensor_scalar(WW, LG, TAU, 0.0, op0=ALU.subtract,
                                         op1=ALU.max))
                v.wait_ge(sm["s_pmm"], 2 * u + 2)
                if u >= 1:
                    v.wait_ge(sm["s_hwT"], 16 * u)
                for e in range(8):
                    op = v.tensor_scalar(hwtm[:, e * 16:(e + 1) * 16],
                                         pb[1][:, e * 16:(e + 1) * 16],
                                         WW[:, e:e + 1], None, op0=ALU.mult)
                    if e == 7:
                        op.then_inc(sm["s_hw"], 1)
            v.nop().then_inc(sm["s_ph"], 1)

        @block.scalar
        def _(s):
            s.wait_ge(sm["s_ph"], 10)
            for u in range(NU):
                s.wait_ge(sm["s_spx"], u + 1)
                s.activation(spd[:, 0:8], CO, AF.Copy, accum_out=KS)
                s.activation(spd[:, 8:16], ZC, AF.Copy, accum_out=TS
                             ).then_inc(sm["s_ksum"], 1)
            s.nop().then_inc(sm["s_ph"], 1)

        @block.sync
        def _(sy):
            sy.wait_ge(sm["s_ph"], 10)
            for u in range(NU):
                sy.wait_ge(sm["s_hw"], u + 1)
                sy.dma_start_transpose(hwT[:, u * 128:(u + 1) * 128],
                                       hwtm[:, 0:128]).then_inc(sm["s_hwT"], 16)
            sy.wait_ge(sm["s_hwT"], 16 * NU)
            sy.nop().then_inc(sm["s_ph"], 1)

        @block.gpsimd
        def _(g):
            g.wait_ge(sm["s_ph"], 10)
            g.nop().then_inc(sm["s_ph"], 1)

        @block.gpsimd
        def _(g):
            g.wait_ge(sm["s_ph"], 15)
            for c in range(4):
                src = [wk, wk, wv, wv][c]
                oc = c % 2
                if c >= 1:
                    g.wait_ge(sm["s_kvmm"], c * NT)
                g.dma_start(out=wbuf[:, :, :], in_=src[oc]).then_inc(sm["s_wld"], 16)
            g.nop().then_inc(sm["s_ph"], 1)

        @block.tensor
        def _(t):
            t.wait_ge(sm["s_ph"], 15)
            for c in range(4):
                t.wait_ge(sm["s_wld"], 16 * (c + 1))
                for tt in range(NT):
                    it = c * NT + tt
                    if it >= 2:
                        t.wait_ge(sm["s_kstg"], it - 1)
                    for k in range(NK):
                        mmop = t.matmul(pb[it % 2][:, :],
                                        xnT[:, k, tt * 128:(tt + 1) * 128],
                                        wbuf[:, k, :], start=(k == 0),
                                        stop=(k == NK - 1))
                        if k == NK - 1:
                            mmop.then_inc(sm["s_kvmm"], 1)
            t.nop().then_inc(sm["s_ph"], 1)

        @block.vector
        def _(v):
            v.wait_ge(sm["s_ph"], 15)
            for c in range(4):
                isk = c < 2
                for tt in range(NT):
                    it = c * NT + tt
                    v.wait_ge(sm["s_kvmm"], it + 1)
                    if it >= 2:
                        if it - 2 < 24:
                            v.wait_ge(skvtr[it % 2], 64 * ((it - 2) // 2 + 1))
                        else:
                            v.wait_ge(sm["s_kvsp"], 16 * (it - 1))
                    if isk:
                        psv, pss, cv, sv, ov, rA, rB = ropeviews(
                            pb[it % 2], tt, kvstg[it % 2])
                        v.tensor_tensor(rA, psv, cv, op=ALU.mult)
                        fence(v, v.tensor_tensor(rB, pss, sv, op=ALU.mult))
                        v.tensor_tensor(ov, rA, rB, op=ALU.add
                                        ).then_inc(sm["s_kstg"], 1)
                    else:
                        v.tensor_copy(kvstg[it % 2][:, :], pb[it % 2][:, :]
                                      ).then_inc(sm["s_kstg"], 1)
            v.nop().then_inc(sm["s_ph"], 1)

        @block.scalar
        def _(s):
            s.wait_ge(sm["s_ph"], 15)
            for c in range(2):
                for tt in range(NT):
                    it = c * NT + tt
                    s.wait_ge(sm["s_kstg"], it + 1)
                    if it >= 2:
                        s.wait_ge(sm["s_kvsp"], 16 * (it - 1))
                    for j in range(4):
                        s.dma_start_transpose(trstg[it % 2][:, j * 128:(j + 1) * 128],
                                              kvstg[it % 2][:, j * 128:(j + 1) * 128]
                                              ).then_inc(skvtr[it % 2], 16)
            s.nop().then_inc(sm["s_ph"], 1)

        @block.sync
        def _(sy):
            sy.wait_ge(sm["s_ph"], 15)
            for c in range(4):
                isk = c < 2
                oc = c % 2
                for tt in range(NT):
                    it = c * NT + tt
                    if isk:
                        sy.wait_ge(skvtr[it % 2], 64 * (it // 2 + 1))
                        sy.dma_start(
                            out=kts[4 * oc:4 * oc + 4, tt, :, :].transpose([1, 0, 2]),
                            in_=trstg[it % 2].rearrange("p (j q) -> p j q", j=4)
                        ).then_inc(sm["s_kvsp"], 16)
                    else:
                        sy.wait_ge(sm["s_kstg"], it + 1)
                        sy.dma_start(
                            out=vts[4 * oc:4 * oc + 4, tt, :, :].transpose([1, 0, 2]),
                            in_=kvstg[it % 2].rearrange("p (j q) -> p j q", j=4)
                        ).then_inc(sm["s_kvsp"], 16)
            sy.wait_ge(sm["s_kvsp"], 16 * 48)
            sy.nop().then_inc(sm["s_ph"], 1)

        @block.gpsimd
        def _(g):
            g.wait_ge(sm["s_ph"], 20)
            for oc in range(8):
                if oc >= 2:
                    g.wait_ge(sm["s_kvmm"], 48 + (oc - 1) * NU)
                dst = wbuf[:, :, :] if oc % 2 == 0 else wq2_full
                g.dma_start(out=dst, in_=wq[oc]).then_inc(
                    sm["s_wld"] if oc % 2 == 0 else sm["s_wld1"], 16)
            g.nop().then_inc(sm["s_ph"], 1)

        @block.tensor
        def _(t):
            t.wait_ge(sm["s_ph"], 20)
            for oc in range(8):
                if oc % 2 == 0:
                    t.wait_ge(sm["s_wld"], 16 * (4 + oc // 2 + 1))
                else:
                    t.wait_ge(sm["s_wld1"], 16 * (oc // 2 + 1))
                for u in range(NU):
                    tt = u + 4
                    it = oc * NU + u
                    if it >= 2:
                        t.wait_ge(sm["s_kstg"], 48 + it - 1)
                    for k in range(NK):
                        rhsw = wbuf[:, k, :] if oc % 2 == 0 else wq2(k)
                        mmop = t.matmul(pb[it % 2][:, :],
                                        xnT[:, k, tt * 128:(tt + 1) * 128],
                                        rhsw, start=(k == 0),
                                        stop=(k == NK - 1))
                        if k == NK - 1:
                            mmop.then_inc(sm["s_kvmm"], 1)
            t.nop().then_inc(sm["s_ph"], 1)

        @block.vector
        def _(v):
            v.wait_ge(sm["s_ph"], 20)
            for oc in range(8):
                for u in range(NU):
                    tt = u + 4
                    it = oc * NU + u
                    v.wait_ge(sm["s_kvmm"], 48 + it + 1)
                    if it >= 2:
                        v.wait_ge(skvtr[it % 2], 768 + 64 * ((it - 2) // 2 + 1))
                    psv, pss, cv, sv, ov, rA, rB = ropeviews(
                        pb[it % 2], tt, kvstg[it % 2])
                    v.tensor_tensor(rA, psv, cv, op=ALU.mult)
                    fence(v, v.tensor_tensor(rB, pss, sv, op=ALU.mult))
                    v.tensor_tensor(ov, rA, rB, op=ALU.add).then_inc(sm["s_kstg"], 1)
            v.nop().then_inc(sm["s_ph"], 1)

        @block.scalar
        def _(s):
            s.wait_ge(sm["s_ph"], 20)
            for oc in range(8):
                for u in range(NU):
                    it = oc * NU + u
                    s.wait_ge(sm["s_kstg"], 48 + it + 1)
                    if it >= 2:
                        s.wait_ge(sm["s_kvsp"], 16 * (48 + it - 1))
                    for j in range(4):
                        s.dma_start_transpose(trstg[it % 2][:, j * 128:(j + 1) * 128],
                                              kvstg[it % 2][:, j * 128:(j + 1) * 128]
                                              ).then_inc(skvtr[it % 2], 16)
            s.nop().then_inc(sm["s_ph"], 1)

        @block.sync
        def _(sy):
            sy.wait_ge(sm["s_ph"], 20)
            for oc in range(8):
                for u in range(NU):
                    it = oc * NU + u
                    sy.wait_ge(skvtr[it % 2], 768 + 64 * (it // 2 + 1))
                    sy.dma_start(
                        out=qts[4 * oc:4 * oc + 4, u, :, :].transpose([1, 0, 2]),
                        in_=trstg[it % 2].rearrange("p (j q) -> p j q", j=4)
                    ).then_inc(sm["s_kvsp"], 16)
            sy.wait_ge(sm["s_kvsp"], 16 * (48 + 64))
            for g4 in range(4):
                sy.dma_start(out=xns[:, g4 * 8:(g4 + 1) * 8, :],
                             in_=xnT[:, g4 * 8:(g4 + 1) * 8, 512:NT * 128]
                             ).then_inc(sm["s_xnsp"], 16)
            sy.wait_ge(sm["s_xnsp"], 64)
            sy.nop().then_inc(sm["s_ph"], 1)

        @block.gpsimd
        def _(g):
            g.wait_ge(sm["s_ph"], 25)
            g.dma_start(out=wbuf[:, 0:8, :], in_=mlo).then_inc(sm["s_msl"], 16)
            g.nop().then_inc(sm["s_ph"], 1)

        @block.sync
        def _(sy):
            sy.wait_ge(sm["s_ph"], 25)
            for h in range(NQH):
                if h >= 2:
                    sy.wait_ge(sm["s_sc"], ((h - 2) * 2 + 2) * 8)
                sy.dma_start(out=qh[h % 2].rearrange("p (u q) -> p u q", u=8),
                             in_=qts[h].transpose([1, 0, 2])
                             ).then_inc(sqld[h % 2], 16)
                if h % 4 == 0:
                    if h >= 4:
                        sy.wait_ge(sm["s_av"], 2 * h)
                    sy.dma_start(out=kh.rearrange("p (tt q) -> p tt q", tt=NT),
                                 in_=kts[h // 4].transpose([1, 0, 2])
                                 ).then_inc(sm["s_kvld"], 16)
                    sy.dma_start(out=vh.rearrange("p (tt q) -> p tt q", tt=NT),
                                 in_=vts[h // 4].transpose([1, 0, 2])
                                 ).then_inc(sm["s_kvld"], 16)
                if h >= 1:
                    sy.wait_ge(sm["s_avn"], 2 * h)
                    sy.dma_start(out=avs[h - 1][:, 0:512], in_=kvstg0[:, :]
                                 ).then_inc(sm["s_avsp"], 16)
                    sy.dma_start(out=avs[h - 1][:, 512:1024], in_=kvstg1[:, :]
                                 ).then_inc(sm["s_avsp"], 16)
            sy.wait_ge(sm["s_avn"], 2 * NQH)
            sy.dma_start(out=avs[NQH - 1][:, 0:512], in_=kvstg0[:, :]
                         ).then_inc(sm["s_avsp"], 16)
            sy.dma_start(out=avs[NQH - 1][:, 512:1024], in_=kvstg1[:, :]
                         ).then_inc(sm["s_avsp"], 16)
            sy.wait_ge(sm["s_avsp"], 32 * NQH)
            sy.nop().then_inc(sm["s_ph"], 1)

        @block.tensor
        def _(t):
            t.wait_ge(sm["s_ph"], 25)
            for h in range(NQH):
                t.wait_ge(sqld[h % 2], 16 * (h // 2 + 1))
                t.wait_ge(sm["s_kvld"], 32 * (h // 4 + 1))
                for qc in range(2):
                    it = h * 2 + qc
                    RORD = [4, 5, 6, 7, 0, 1, 2, 3]
                    for idx, r in enumerate(RORD):
                        kb = 4 * qc + r
                        lo, hi = _clip(r)
                        if it * 8 + idx >= 2:
                            t.wait_ge(sm["s_exp"], it * 8 + idx - 1)
                        t.matmul(pb[idx % 2][:, lo:hi],
                                 kh[:, kb * 128:(kb + 1) * 128],
                                 qh[h % 2][:, qc * 512 + lo:qc * 512 + hi],
                                 start=True, stop=True).then_inc(sm["s_sc"], 1)
                    pba = pb[2 + 2 * (it % 2)]
                    pbd = pb[3 + 2 * (it % 2)]
                    if it >= 2:
                        t.wait_ge(sm["s_avn"], it - 1)
                    t.wait_ge(sm["s_msk"], it * 8 + 4)
                    for idx, r in enumerate([4, 5, 6, 7]):
                        kb = 4 * qc + r
                        lo, hi = _clip(r)
                        t.matmul(pba[:, lo:hi], vh[:, kb * 128:(kb + 1) * 128],
                                 exp_sb[:, r, lo:hi], start=(idx == 0), stop=False)
                    for idx, r in enumerate([4, 5, 6, 7]):
                        lo, hi = _clip(r)
                        mmop = t.matmul(pbd[:, lo:hi], onesb[:, :],
                                        exp_sb[:, r, lo:hi], start=(idx == 0),
                                        stop=False)
                        if idx == 3:
                            mmop.then_inc(sm["s_avB"], 1)
                    t.wait_ge(sm["s_msk"], (it + 1) * 8)
                    for r in [0, 1, 2, 3]:
                        kb = 4 * qc + r
                        lo, hi = _clip(r)
                        t.matmul(pba[:, lo:hi], vh[:, kb * 128:(kb + 1) * 128],
                                 exp_sb[:, r, lo:hi], start=False, stop=(r == 3))
                    for r in [0, 1, 2, 3]:
                        lo, hi = _clip(r)
                        mmop = t.matmul(pbd[:, lo:hi], onesb[:, :],
                                        exp_sb[:, r, lo:hi], start=False,
                                        stop=(r == 3))
                        if r == 3:
                            mmop.then_inc(sm["s_av"], 1)
            t.nop().then_inc(sm["s_ph"], 1)

        @block.scalar
        def _(s):
            s.wait_ge(sm["s_ph"], 25)
            for h in range(NQH):
                for qc in range(2):
                    it = h * 2 + qc
                    RORD = [4, 5, 6, 7, 0, 1, 2, 3]
                    for idx, r in enumerate(RORD):
                        lo, hi = _clip(r)
                        s.wait_ge(sm["s_sc"], it * 8 + idx + 1)
                        if it >= 1 and idx == 0:
                            s.wait_ge(sm["s_avB"], it)
                        if it >= 1 and idx == 4:
                            s.wait_ge(sm["s_av"], it)
                        s.activation(exp_sb[:, r, lo:hi], pb[idx % 2][:, lo:hi],
                                     AF.Exp, bias=nb24[:, 0:1], scale=SCALE
                                     ).then_inc(sm["s_exp"], 1)
            s.nop().then_inc(sm["s_ph"], 1)

        @block.vector
        def _(v):
            v.wait_ge(sm["s_ph"], 25)
            v.wait_ge(sm["s_msl"], 16)
            for h in range(NQH):
                for qc in range(2):
                    it = h * 2 + qc
                    RORD = [4, 5, 6, 7, 0, 1, 2, 3]
                    for idx, r in enumerate(RORD):
                        lo, hi = _clip(r)
                        v.wait_ge(sm["s_exp"], it * 8 + idx + 1)
                        if r < 4:
                            v.tensor_tensor(exp_sb[:, r, lo:hi],
                                            exp_sb[:, r, lo:hi],
                                            wbuf[:, 4 * qc + r, lo:hi],
                                            op=ALU.mult).then_inc(sm["s_msk"], 1)
                        else:
                            d0 = 128 * (r - 4)
                            v.tensor_tensor(exp_sb[:, r, d0:d0 + 128],
                                            exp_sb[:, r, d0:d0 + 128],
                                            trisb[:, :], op=ALU.mult
                                            ).then_inc(sm["s_msk"], 1)
                    v.wait_ge(sm["s_av"], it + 1)
                    if qc == 0 and h >= 1:
                        v.wait_ge(sm["s_avsp"], 32 * h)
                    fence(v, v.reciprocal(den_r[:, :], pb[3 + 2 * (it % 2)][:, :]))
                    v.tensor_tensor(avh[qc][:, :], pb[2 + 2 * (it % 2)][:, :],
                                    den_r[:, :],
                                    op=ALU.mult).then_inc(sm["s_avn"], 1)
            v.nop().then_inc(sm["s_ph"], 1)

        @block.gpsimd
        def _(g):
            g.wait_ge(sm["s_ph"], 30)
            for oc in range(8):
                if oc >= 2:
                    g.wait_ge(sm["s_smm"], (oc - 1) * NU)
                dst = wbuf[:, :, :] if oc % 2 == 0 else wz2_full
                g.dma_start(out=dst, in_=wo[oc]).then_inc(
                    sm["s_wld"] if oc % 2 == 0 else sm["s_wld1"], 16)
            g.nop().then_inc(sm["s_ph"], 1)

        @block.sync
        def _(sy):
            sy.wait_ge(sm["s_ph"], 30)
            for h in range(NQH):
                sy.dma_start(out=xnT[:, h, 0:1024], in_=avs[h]
                             ).then_inc(sm["s_avld"], 16)
            sy.wait_ge(sm["s_ssp"], 16 * 64)
            sy.nop().then_inc(sm["s_ph"], 1)

        @block.tensor
        def _(t):
            t.wait_ge(sm["s_ph"], 30)
            t.wait_ge(sm["s_avld"], 16 * NQH)
            for oc in range(8):
                if oc % 2 == 0:
                    t.wait_ge(sm["s_wld"], 16 * (8 + oc // 2 + 1))
                else:
                    t.wait_ge(sm["s_wld1"], 16 * (4 + oc // 2 + 1))
                for u in range(NU):
                    it = oc * NU + u
                    if it >= 4:
                        t.wait_ge(sm["s_scp"], it - 3)
                    for k in range(NK):
                        rhsw = wbuf[:, k, :] if oc % 2 == 0 else wz2(k)
                        mmop = t.matmul(pb[it % 4][:, :], xnT[:, k, u * 128:(u + 1) * 128],
                                        rhsw, start=(k == 0),
                                        stop=(k == NK - 1))
                        if k == NK - 1:
                            mmop.then_inc(sm["s_smm"], 1)
            t.nop().then_inc(sm["s_ph"], 1)

        @block.scalar
        def _(s):
            s.wait_ge(sm["s_ph"], 30)
            for oc in range(8):
                for u in range(NU):
                    it = oc * NU + u
                    s.wait_ge(sm["s_smm"], it + 1)
                    if it >= 2:
                        s.wait_ge(sm["s_ssp"], 16 * (it - 1))
                    s.activation(sstg[it % 2][:, :], pb[it % 4][:, :], AF.Copy
                                 ).then_inc(sm["s_scp"], 1)
                    s.wait_ge(sm["s_scp"], it + 1)
                    s.dma_start(out=ss[u][:, oc * 512:(oc + 1) * 512],
                                in_=sstg[it % 2][:, :]).then_inc(sm["s_ssp"], 16)
            s.nop().then_inc(sm["s_ph"], 1)

        @block.vector
        def _(v):
            v.wait_ge(sm["s_ph"], 30)
            v.nop().then_inc(sm["s_ph"], 1)

        @block.gpsimd
        def _(g):
            g.wait_ge(sm["s_ph"], 35)
            for oc in range(8):
                if oc >= 2:
                    g.wait_ge(sm["s_fmm"], (oc - 1) * NU)
                dst = wbuf[:, :, :] if oc % 2 == 0 else wz2_full
                g.dma_start(out=dst, in_=wb[oc]).then_inc(
                    sm["s_wld"] if oc % 2 == 0 else sm["s_wld1"], 16)
                if oc >= 1:
                    g.wait_ge(sm["s_fmm"], oc * NU)
                g.dma_start(out=lbs[oc % 2][:, :], in_=lb[:, oc * 512:(oc + 1) * 512]
                            ).then_inc(sm["s_lbl"], 16)
            g.nop().then_inc(sm["s_ph"], 1)

        @block.sync
        def _(sy):
            sy.wait_ge(sm["s_ph"], 35)
            for g4 in range(4):
                sy.dma_start(out=xnT[:, g4 * 8:(g4 + 1) * 8, 0:1024],
                             in_=xns[:, g4 * 8:(g4 + 1) * 8, :]
                             ).then_inc(sm["s_avld"], 16)
            for oc in range(8):
                if oc >= 1:
                    sy.wait_ge(sm["s_fb"], oc * NU)
                sy.dma_start(out=bbs[:, :], in_=bb[:, oc * 512:(oc + 1) * 512]
                             ).then_inc(sm["s_bbl"], 16)
                for u in range(NU):
                    it = oc * NU + u
                    sy.wait_ge(sm["s_fo"], it + 1)
                    sy.dma_start(out=os_[u * 128:(u + 1) * 128,
                                         oc * 512:(oc + 1) * 512],
                                 in_=ostg[it % 2][:, :]).then_inc(sm["s_ost"], 16)
            sy.wait_ge(sm["s_ost"], 16 * 64)

        @block.tensor
        def _(t):
            t.wait_ge(sm["s_ph"], 35)
            t.wait_ge(sm["s_avld"], 16 * (NQH + 4))
            for oc in range(8):
                if oc % 2 == 0:
                    t.wait_ge(sm["s_wld"], 16 * (12 + oc // 2 + 1))
                else:
                    t.wait_ge(sm["s_wld1"], 16 * (8 + oc // 2 + 1))
                t.wait_ge(sm["s_lbl"], 16 * (oc + 1))
                for u in range(NU):
                    it = oc * NU + u
                    if it >= 4:
                        t.wait_ge(sm["s_scp"], 64 + it - 3)
                    for k in range(NK):
                        rhsw = wbuf[:, k, :] if oc % 2 == 0 else wz2(k)
                        t.matmul(pb[it % 4][:, :], xnT[:, k, u * 128:(u + 1) * 128],
                                 rhsw, start=(k == 0), stop=False)
                    t.matmul(pb[it % 4][:, :], hwT[:, u * 128:(u + 1) * 128],
                             lbs[oc % 2][:, :], start=False, stop=True
                             ).then_inc(sm["s_fmm"], 1)
            t.nop().then_inc(sm["s_ph"], 1)

        @block.vector
        def _(v):
            v.wait_ge(sm["s_ph"], 35)
            for oc in range(8):
                for u in range(NU):
                    it = oc * NU + u
                    v.wait_ge(sm["s_fmm"], it + 1)
                    v.wait_ge(sm["s_bbl"], 16 * (oc + 1))
                    v.tensor_tensor(pb[it % 4][:, :], pb[it % 4][:, :], bbs[:, :],
                                    op=ALU.add).then_inc(sm["s_fb"], 1)
                    v.wait_ge(sm["s_scp"], 64 + it + 1)
                    v.wait_ge(ssld[it % 2], 16 * (it // 2 + 1))
                    v.wait_ge(sxcl[it % 2], 16 * (it // 2 + 1))
                    if it >= 2:
                        v.wait_ge(sm["s_ost"], 16 * (it - 1))
                    fence(v, v.tensor_tensor(ostg[it % 2][:, :], sstg[it % 2][:, :],
                                              gstg[it % 2][:, :], op=ALU.mult))
                    v.tensor_tensor(ostg[it % 2][:, :], ostg[it % 2][:, :],
                                    xck[it % 2][:, :], op=ALU.add
                                    ).then_inc(sm["s_fo"], 1)
                    v.nop().then_inc(sm["s_fg"], 1)

        @block.scalar
        def _(s):
            s.wait_ge(sm["s_ph"], 35)
            for oc in range(8):
                for u in range(NU):
                    it = oc * NU + u
                    if it >= 2:
                        s.wait_ge(sm["s_fg"], it - 1)
                    s.dma_start(out=sstg[it % 2][:, :],
                                in_=ss[u][:, oc * 512:(oc + 1) * 512]
                                ).then_inc(ssld[it % 2], 16)
                    s.dma_start(out=xck[it % 2][:, :],
                                in_=xs[(u + 4) * 128:(u + 5) * 128,
                                       oc * 512:(oc + 1) * 512]
                                ).then_inc(sxcl[it % 2], 16)
                    s.wait_ge(sm["s_fb"], it + 1)
                    s.activation(gstg[it % 2][:, :], pb[it % 4][:, :], AF.Silu
                                 ).then_inc(sm["s_scp"], 1)

    return nc


def _tables(h_half):
    theta = 10000.0
    inv = 1.0 / theta ** (np.arange(0, 128, 2, dtype=np.float64) / 128.0)
    loc = np.arange(NT * 128)
    if h_half == 0:
        pos = np.where(loc < HALO, 0, loc - HALO).astype(np.float64)
    else:
        pos = (loc + HALO).astype(np.float64)
    ang = pos[:, None] * inv[None, :]
    cos64 = np.cos(ang)
    sin64 = np.sin(ang)
    cosT = np.concatenate([cos64, cos64], axis=1)
    sinA = np.concatenate([-sin64, sin64], axis=1)
    out = []
    for tab in (cosT, sinA):
        hi = tab.astype(np.float32)
        lo = (tab - hi.astype(np.float64)).astype(np.float32)
        out.append(np.ascontiguousarray(
            hi.reshape(NT, 128, 128).transpose(1, 0, 2).reshape(128, NT * 128)))
        out.append(np.ascontiguousarray(
            lo.reshape(NT, 128, 128).transpose(1, 0, 2).reshape(128, NT * 128)))
    return out


def _masks(h_half):
    m = np.zeros((128, 8, 512), dtype=np.float32)
    kc = np.arange(128)[:, None]
    qc = np.arange(512)[None, :]
    for qch in range(2):
        for r in range(4):
            if h_half == 0 and qch == 0:
                continue
            lo, hi = _clip(r)
            allow = (qc <= r * 128 + kc) & (qc < hi)
            m[:, qch * 4 + r, :] = allow.astype(np.float32)
    return m.astype(ml_dtypes.bfloat16)


def _w_tiles(w, noc):
    return np.ascontiguousarray(
        w.reshape(NK, 128, noc, 512).transpose(2, 1, 0, 3))


def kernel(L0_attn_hidden_states, L0_attn_Wq, L0_attn_Wk, L0_attn_Wv, L0_attn_Wo,
           L0_ffn_W_base, L0_ffn_b_base, L0_ffn_W_router, L0_ffn_b_router,
           L0_ffn_lora_A, L0_ffn_lora_B):
    x = np.ascontiguousarray(L0_attn_hidden_states, dtype=np.float32)
    wq_h = _w_tiles(np.ascontiguousarray(L0_attn_Wq, dtype=np.float32), 8)
    wk_h = _w_tiles(np.ascontiguousarray(L0_attn_Wk, dtype=np.float32), 2)
    wv_h = _w_tiles(np.ascontiguousarray(L0_attn_Wv, dtype=np.float32), 2)
    wo_h = _w_tiles(np.ascontiguousarray(L0_attn_Wo, dtype=np.float32), 8)
    wb_h = _w_tiles(np.ascontiguousarray(L0_ffn_W_base.T, dtype=np.float32), 8)
    wr_h = np.ascontiguousarray(
        np.asarray(L0_ffn_W_router, dtype=np.float32).T.reshape(NK, 128, 8)
        .transpose(1, 0, 2))
    la_flat = np.asarray(L0_ffn_lora_A, dtype=np.float32).transpose(1, 0, 2).reshape(H, 128)
    la_h = np.ascontiguousarray(la_flat.reshape(NK, 128, 128).transpose(1, 0, 2))
    lb_h = np.ascontiguousarray(np.asarray(L0_ffn_lora_B, dtype=np.float32).reshape(128, H))
    bb_h = np.ascontiguousarray(
        np.broadcast_to(np.asarray(L0_ffn_b_base, dtype=np.float32), (128, H)))
    br_h = np.ascontiguousarray(
        np.broadcast_to(np.asarray(L0_ffn_b_router, dtype=np.float32), (128, 8)))
    tri_h = (np.arange(128)[:, None] <= np.arange(128)[None, :]
             ).astype(ml_dtypes.bfloat16)
    one_h = np.ones((128, 128), dtype=ml_dtypes.bfloat16)
    idn_h = np.eye(128, dtype=ml_dtypes.bfloat16)

    if "nc" not in _NC_CACHE:
        _NC_CACHE["nc"] = build_program()
    nc = _NC_CACHE["nc"]

    tabs = [_tables(0), _tables(1)]
    msks = [_masks(0), _masks(1)]
    in_maps = []
    for c in range(N_CORES):
        b, hh = c // 2, c % 2
        if hh == 0:
            xsl = np.concatenate([np.zeros((HALO, H), np.float32), x[b, 0:1024]],
                                 axis=0)
        else:
            xsl = np.ascontiguousarray(x[b, HALO:2048])
        chi_h, clo_h, shi_h, slo_h = tabs[hh]
        in_maps.append({
            "xs": xsl, "wq": wq_h, "wk": wk_h, "wv": wv_h, "wo": wo_h,
            "wb": wb_h, "wr": wr_h, "la": la_h, "lb": lb_h, "bb": bb_h,
            "br": br_h, "chi": chi_h, "clo": clo_h, "shi": shi_h, "slo": slo_h,
            "mlo": msks[hh], "tri": tri_h, "one": one_h, "idn": idn_h,
        })
    res = run_bass_kernel_spmd(nc, in_maps, core_ids=list(range(N_CORES)))
    out = np.empty((B, S, H), dtype=np.float32)
    for c in range(N_CORES):
        b, hh = c // 2, c % 2
        out[b, hh * 1024:(hh + 1) * 1024] = res.results[c]["os_"]
    return out


# version 10
# speedup vs baseline: 1.4317x; 1.0389x over previous
import contextlib
import math

import numpy as np
import ml_dtypes

import concourse.bass as bass
import concourse.mybir as mybir
from concourse.bass_utils import run_bass_kernel_spmd

F32 = mybir.dt.float32
BF16 = mybir.dt.bfloat16
AF = mybir.ActivationFunctionType
ALU = mybir.AluOpType
AX = mybir.AxisListType

H = 4096
S = 2048
B = 4
NT = 12
NU = 8
HALO = 512
NQH = 32
NKV = 8
NK = 32
SCALE = 1.0 / math.sqrt(128.0)
ESHIFT = -24.0
N_CORES = 8

_NC_CACHE = {}


def _clip(r):
    if r < 4:
        return 0, 128 * (r + 1)
    return 128 * (r - 4), 512


def build_program():
    nc = bass.Bass("TRN2", target_bir_lowering=False)

    xs = nc.dram_tensor("xs", [NT * 128, H], F32, kind="ExternalInput").ap()
    wq = nc.dram_tensor("wq", [8, 128, NK, 512], F32, kind="ExternalInput").ap()
    wk = nc.dram_tensor("wk", [2, 128, NK, 512], F32, kind="ExternalInput").ap()
    wv = nc.dram_tensor("wv", [2, 128, NK, 512], F32, kind="ExternalInput").ap()
    wo = nc.dram_tensor("wo", [8, 128, NK, 512], F32, kind="ExternalInput").ap()
    wb = nc.dram_tensor("wb", [8, 128, NK, 512], F32, kind="ExternalInput").ap()
    wr = nc.dram_tensor("wr", [128, NK, 8], F32, kind="ExternalInput").ap()
    la = nc.dram_tensor("la", [128, NK, 128], F32, kind="ExternalInput").ap()
    lb = nc.dram_tensor("lb", [128, H], F32, kind="ExternalInput").ap()
    bb = nc.dram_tensor("bb", [128, H], F32, kind="ExternalInput").ap()
    br = nc.dram_tensor("br", [128, 8], F32, kind="ExternalInput").ap()
    chi = nc.dram_tensor("chi", [128, NT * 128], F32, kind="ExternalInput").ap()
    clo = nc.dram_tensor("clo", [128, NT * 128], F32, kind="ExternalInput").ap()
    shi = nc.dram_tensor("shi", [128, NT * 128], F32, kind="ExternalInput").ap()
    slo = nc.dram_tensor("slo", [128, NT * 128], F32, kind="ExternalInput").ap()
    mlo = nc.dram_tensor("mlo", [128, 8, 512], BF16, kind="ExternalInput").ap()
    tri = nc.dram_tensor("tri", [128, 128], BF16, kind="ExternalInput").ap()
    one = nc.dram_tensor("one", [128, 128], BF16, kind="ExternalInput").ap()
    idn = nc.dram_tensor("idn", [128, 128], BF16, kind="ExternalInput").ap()
    os_ = nc.dram_tensor("os_", [NU * 128, H], F32, kind="ExternalOutput").ap()
    kts = nc.dram_tensor("kts", [NKV, NT, 128, 128], BF16, kind="Internal").ap()
    vts = nc.dram_tensor("vts", [NKV, NT, 128, 128], BF16, kind="Internal").ap()
    qts = nc.dram_tensor("qts", [NQH, NU, 128, 128], BF16, kind="Internal").ap()
    avs = nc.dram_tensor("avs", [NQH, 128, 1024], BF16, kind="Internal").ap()
    xns = nc.dram_tensor("xns", [128, NK, 1024], BF16, kind="Internal").ap()
    ss = nc.dram_tensor("ss", [NU, 128, H], F32, kind="Internal").ap()

    st = contextlib.ExitStack()
    with st:
        sb = lambda *a, **k: st.enter_context(nc.sbuf_tensor(*a, **k))
        xnT = sb("xnT", [128, NK, NT * 128], BF16, is_transpose_dest=True)
        wbuf = sb("wbuf", [128, NK, 512], BF16)
        xbuf0 = sb("xbuf0", [128, H // 2], F32)
        xbuf1 = sb("xbuf1", [128, H // 2], F32)
        xtm0 = sb("xtm0", [128, H], BF16)
        xtm1 = sb("xtm1", [128, H], BF16)
        sqsc = sb("sqsc", [128, NT * 128], BF16)
        cosv = sb("cosv", [128, NT * 128], F32)
        sinv = sb("sinv", [128, NT * 128], F32)
        stats = sb("stats", [128, NT, 8], F32)
        spx = sb("spx", [128, 128], F32)
        spd = sb("spd", [128, 16], F32)
        hwT = sb("hwT", [128, NU * 128], BF16, is_transpose_dest=True)
        hwtm = sb("hwtm", [128, 128], BF16)
        wrsb = sb("wrsb", [128, NK, 8], BF16)
        brsb = sb("brsb", [128, 8], F32)
        trisb = sb("trisb", [128, 128], BF16)
        onesb = sb("onesb", [128, 128], BF16)
        nb24 = sb("nb24", [128, 1], F32)
        idnsb = sb("idnsb", [128, 128], BF16)
        kvstg0 = sb("kvstg0", [128, 512], BF16)
        kvstg1 = sb("kvstg1", [128, 512], BF16)
        trstg0 = sb("trstg0", [128, 512], BF16, is_transpose_dest=True)
        trstg1 = sb("trstg1", [128, 512], BF16, is_transpose_dest=True)
        sstg0 = sb("sstg0", [128, 512], F32)
        sstg1 = sb("sstg1", [128, 512], F32)
        gstg0 = sb("gstg0", [128, 512], F32)
        gstg1 = sb("gstg1", [128, 512], F32)
        ostg0 = sb("ostg0", [128, 512], F32)
        ostg1 = sb("ostg1", [128, 512], F32)
        xck0 = sb("xck0", [128, 512], F32)
        xck1 = sb("xck1", [128, 512], F32)
        bbs = sb("bbs", [128, 512], F32)
        lbs0 = sb("lbs0", [128, 512], BF16)

        exp_sb = xtm0.rearrange("p (r q) -> p r q", r=8)
        qh = [xtm1[:, 0:1024], xtm1[:, 1024:2048]]
        kh = xtm1[:, 2048:2048 + NT * 128]
        vh = sqsc
        avh = [kvstg0, kvstg1]
        den_r = gstg0
        rtA, rtB = sstg0, sstg1

        pb = [st.enter_context(nc.psum_tensor(f"pb{i}", [128, 512], F32))
              for i in range(6)]
        pt = [st.enter_context(nc.psum_tensor(f"pt{i}", [128, 128], BF16))
              for i in range(2)]

        semnames = [
            "s_ph", "s_xld", "s_xld1", "s_stat", "s_rec", "s_rsq", "s_apl",
            "s_trA0", "s_trA1", "s_sml", "s_sm2", "s_tbl", "s_pmm", "s_spx",
            "s_ksum", "s_hw", "s_hwT", "s_wld", "s_wld1", "s_kvmm", "s_kstg",
            "s_kvtr0", "s_kvtr1", "s_kvsp", "s_xnsp", "s_qld0", "s_qld1",
            "s_kvld", "s_sc", "s_exp", "s_msk", "s_av", "s_avn", "s_avsp",
            "s_avld", "s_smm", "s_scp", "s_ssp", "s_sld0", "s_sld1",
            "s_xcl0", "s_xcl1", "s_fmm", "s_fb", "s_fo", "s_fg", "s_ost",
            "s_lbl", "s_bbl", "s_msl", "s_dvf", "s_ttr", "s_tcp", "s_avB",
        ]
        sm = {n: st.enter_context(nc.semaphore(n)) for n in semnames}
        block = st.enter_context(nc.Block())

        xtm = [xtm0, xtm1]
        xbuf = [xbuf0, xbuf1]
        sxl = [sm["s_xld"], sm["s_xld1"]]
        strA = [sm["s_trA0"], sm["s_trA1"]]
        kvstg = [kvstg0, kvstg1]
        trstg = [trstg0, trstg1]
        skvtr = [sm["s_kvtr0"], sm["s_kvtr1"]]
        sqld = [sm["s_qld0"], sm["s_qld1"]]
        sstg = [sstg0, sstg1]
        gstg = [gstg0, gstg1]
        ostg = [ostg0, ostg1]
        xck = [xck0, xck1]
        lbs = [lbs0, lbs0]
        ssld = [sm["s_sld0"], sm["s_sld1"]]
        sxcl = [sm["s_xcl0"], sm["s_xcl1"]]

        dvf = [0]
        wq2 = lambda k: xnT[:, k, 0:512]
        wz2 = lambda k: xnT[:, k, 1024:1536]
        wq2_full = xnT[:, :, 0:512]
        wz2_full = xnT[:, :, 1024:1536]

        def fence(v, instop):
            dvf[0] += 1
            instop.then_inc(sm["s_dvf"], 1)
            v.wait_ge(sm["s_dvf"], dvf[0])

        def ropeviews(ps, tt, outbuf):
            psv = ps.rearrange("p (h u j) -> p h u j", h=4, u=2)
            pss = bass.AP(ps, 64, [[512, 128], [128, 4], [-64, 2], [1, 64]])
            cv = bass.AP(cosv, tt * 128,
                         [[NT * 128, 128], [0, 4], [64, 2], [1, 64]])
            sv = bass.AP(sinv, tt * 128,
                         [[NT * 128, 128], [0, 4], [64, 2], [1, 64]])
            ov = outbuf.rearrange("p (h u j) -> p h u j", h=4, u=2)
            rA = rtA.rearrange("p (h u j) -> p h u j", h=4, u=2)
            rB = rtB.rearrange("p (h u j) -> p h u j", h=4, u=2)
            return psv, pss, cv, sv, ov, rA, rB

        @block.sync
        def _(sy):
            sy.dma_start(out=brsb[:, :], in_=br).then_inc(sm["s_sml"], 16)
            sy.dma_start(out=trisb[:, :], in_=tri).then_inc(sm["s_sml"], 16)
            sy.dma_start(out=onesb[:, :], in_=one).then_inc(sm["s_sml"], 16)
            sy.dma_start(out=idnsb[:, :], in_=idn).then_inc(sm["s_sml"], 16)
            sy.dma_start(out=cosv[:, :], in_=chi).then_inc(sm["s_tbl"], 16)
            sy.dma_start(out=sinv[:, :], in_=shi).then_inc(sm["s_tbl"], 16)
            sy.wait_ge(sm["s_sml"], 64)
            sy.nop().then_inc(sm["s_ph"], 1)

        @block.gpsimd
        def _(g):
            g.dma_start(out=wrsb[:, :, :], in_=wr).then_inc(sm["s_sm2"], 16)
            g.wait_ge(sm["s_tbl"], 32)
            g.dma_start(out=cosv[:, :], in_=clo, accum_op=ALU.add
                        ).then_inc(sm["s_sm2"], 16)
            g.dma_start(out=sinv[:, :], in_=slo, accum_op=ALU.add
                        ).then_inc(sm["s_sm2"], 16)
            g.wait_ge(sm["s_sm2"], 48)
            g.nop().then_inc(sm["s_ph"], 1)

        @block.vector
        def _(v):
            v.memset(nb24[:, :], ESHIFT)
            v.nop().then_inc(sm["s_ph"], 1)

        @block.scalar
        def _(s):
            s.nop().then_inc(sm["s_ph"], 1)

        @block.tensor
        def _(t):
            t.nop().then_inc(sm["s_ph"], 1)

        @block.sync
        def _(sy):
            sy.wait_ge(sm["s_ph"], 5)
            HH = H // 2
            for t in range(NT):
                gates = [("apl", 2 * t - 1), ("apl", 2 * t),
                         ("stat", 2 * t + 1), ("stat", 2 * t + 2)]
                for j, (nm, val) in enumerate(gates):
                    if val > 0:
                        sy.wait_ge(sm["s_apl" if nm == "apl" else "s_stat"], val)
                    half = j % 2
                    sy.dma_start(out=xbuf[j % 2][:, :],
                                 in_=xs[t * 128:(t + 1) * 128,
                                        half * HH:(half + 1) * HH]
                                 ).then_inc(sxl[j % 2], 16)
            sy.nop().then_inc(sm["s_ph"], 1)

        @block.scalar
        def _(s):
            s.wait_ge(sm["s_ph"], 5)
            for t in range(NT):
                pa = t % 2
                for half in range(2):
                    s.wait_ge(sxl[half], 16 * (2 * t + 1))
                    s.activation(xbuf[half][:, :], xbuf[half][:, :], AF.Copy,
                                 accum_out=stats[:, t, 0 + half:1 + half])
                    s.activation(xbuf[half][:, :], xbuf[half][:, :], AF.Square,
                                 accum_out=stats[:, t, 2 + half:3 + half]
                                 ).then_inc(sm["s_stat"], 1)
                s.wait_ge(sm["s_rec"], t + 1)
                s.activation(stats[:, t, 6:7], stats[:, t, 5:6], AF.Sqrt
                             ).then_inc(sm["s_rsq"], 1)
            s.nop().then_inc(sm["s_ph"], 1)

        @block.vector
        def _(v):
            v.wait_ge(sm["s_ph"], 5)
            HH = H // 2
            for t in range(NT):
                pa = t % 2
                v.wait_ge(sm["s_stat"], 2 * t + 2)
                fence(v, v.tensor_tensor(stats[:, t, 4:5], stats[:, t, 0:1],
                                          stats[:, t, 1:2], op=ALU.add))
                fence(v, v.tensor_tensor(stats[:, t, 5:6], stats[:, t, 2:3],
                                         stats[:, t, 3:4], op=ALU.add))
                fence(v, v.tensor_scalar(stats[:, t, 4:5], stats[:, t, 4:5],
                                         1.0 / H, None, op0=ALU.mult))
                fence(v, v.tensor_scalar(stats[:, t, 5:6], stats[:, t, 5:6],
                                         1.0 / H, None, op0=ALU.mult))
                fence(v, v.tensor_tensor(stats[:, t, 7:8], stats[:, t, 4:5],
                                         stats[:, t, 4:5], op=ALU.mult))
                fence(v, v.tensor_tensor(stats[:, t, 5:6], stats[:, t, 5:6],
                                         stats[:, t, 7:8], op=ALU.subtract))
                fence(v, v.tensor_scalar(stats[:, t, 5:6], stats[:, t, 5:6],
                                         1e-5, None, op0=ALU.add))
                v.reciprocal(stats[:, t, 5:6], stats[:, t, 5:6]
                             ).then_inc(sm["s_rec"], 1)
                v.wait_ge(sm["s_rsq"], t + 1)
                if t >= 2:
                    v.wait_ge(sm["s_ttr"], NK * (t - 1))
                for half in range(2):
                    v.wait_ge(sxl[half], 16 * (2 * t + 2))
                    v.tensor_scalar(xtm[pa][:, half * HH:(half + 1) * HH],
                                    xbuf[half][:, :], stats[:, t, 4:5],
                                    stats[:, t, 6:7], op0=ALU.subtract,
                                    op1=ALU.mult).then_inc(sm["s_apl"], 1)
                for k in range(NK):
                    gidx = t * NK + k
                    v.wait_ge(sm["s_ttr"], gidx + 1)
                    v.tensor_copy(xnT[:, k, t * 128:(t + 1) * 128],
                                  pt[k % 2][:, :]).then_inc(sm["s_tcp"], 1)
            v.nop().then_inc(sm["s_ph"], 1)

        @block.tensor
        def _(t):
            t.wait_ge(sm["s_ph"], 5)
            t.wait_ge(sm["s_sml"], 64)
            for tt in range(NT):
                pa = tt % 2
                t.wait_ge(sm["s_apl"], 2 * tt + 2)
                for k in range(NK):
                    gidx = tt * NK + k
                    if gidx >= 2:
                        t.wait_ge(sm["s_tcp"], gidx - 1)
                    t.transpose(pt[k % 2][:, :],
                                xtm[pa][:, k * 128:(k + 1) * 128], idnsb[:, :]
                                ).then_inc(sm["s_ttr"], 1)
            t.nop().then_inc(sm["s_ph"], 1)

        @block.gpsimd
        def _(g):
            g.wait_ge(sm["s_ph"], 5)
            g.dma_start(out=wbuf[:, :, 0:128], in_=la).then_inc(sm["s_sm2"], 16)
            g.nop().then_inc(sm["s_ph"], 1)

        CMP = spx[:, 0:64].rearrange("p (i j) -> p i j", i=8)
        LG = spx[:, 64:72]
        KI = spx[:, 72:80]
        SI = spx[:, 80:88]
        KZ = spx[:, 88:96]
        CO = spx[:, 96:104]
        ZC = spx[:, 104:112]
        KS = spx[:, 112:113]
        TS = spx[:, 113:114]
        TM = spx[:, 114:115]
        RK = spx[:, 115:116]
        TAU = spx[:, 116:117]
        WW = spx[:, 120:128]

        @block.tensor
        def _(t):
            t.wait_ge(sm["s_ph"], 10)
            t.wait_ge(sm["s_sm2"], 64)
            for u in range(NU):
                tt = u + 4
                if u >= 1:
                    t.wait_ge(sm["s_spx"], u)
                    t.wait_ge(sm["s_hw"], u)
                for k in range(NK):
                    t.matmul(pb[0][:, 0:8], xnT[:, k, tt * 128:(tt + 1) * 128],
                             wrsb[:, k, :], start=(k == 0), stop=(k == NK - 1)
                             ).then_inc(sm["s_pmm"], 1) if k == NK - 1 else \
                        t.matmul(pb[0][:, 0:8], xnT[:, k, tt * 128:(tt + 1) * 128],
                                 wrsb[:, k, :], start=(k == 0), stop=False)
                for k in range(NK):
                    t.matmul(pb[1][:, 0:128], xnT[:, k, tt * 128:(tt + 1) * 128],
                             wbuf[:, k, 0:128], start=(k == 0), stop=(k == NK - 1)
                             ).then_inc(sm["s_pmm"], 1) if k == NK - 1 else \
                        t.matmul(pb[1][:, 0:128], xnT[:, k, tt * 128:(tt + 1) * 128],
                                 wbuf[:, k, 0:128], start=(k == 0), stop=False)
            t.nop().then_inc(sm["s_ph"], 1)

        @block.vector
        def _(v):
            v.wait_ge(sm["s_ph"], 10)
            for u in range(NU):
                v.wait_ge(sm["s_pmm"], 2 * u + 1)
                fence(v, v.tensor_tensor(LG, pb[0][:, 0:8], brsb[:, :], op=ALU.add))
                for i in range(8):
                    v.tensor_scalar(CMP[:, i, :], LG, LG[:, i:i + 1], None,
                                    op0=ALU.is_ge)
                fence(v, v.tensor_scalar(CMP[:, 7, :], LG, LG[:, 7:8], None,
                                         op0=ALU.is_ge))
                fence(v, v.tensor_reduce(KI, CMP, axis=AX.X, op=ALU.add))
                fence(v, v.tensor_tensor(CMP, CMP,
                                         bass.AP(spx, 64, [[128, 128], [0, 8], [1, 8]]),
                                         op=ALU.mult))
                fence(v, v.tensor_reduce(SI, CMP, axis=AX.X, op=ALU.add))
                fence(v, v.tensor_tensor(KZ, KI, LG, op=ALU.mult))
                fence(v, v.tensor_scalar(KZ, KZ, 1.0, None, op0=ALU.add))
                fence(v, v.tensor_tensor(CO, KZ, SI, op=ALU.is_gt))
                fence(v, v.tensor_tensor(ZC, LG, CO, op=ALU.mult))
                v.nop().then_inc(sm["s_spx"], 1)
                v.wait_ge(sm["s_ksum"], u + 1)
                fence(v, v.tensor_scalar(TM, TS, 1.0, None, op0=ALU.subtract))
                fence(v, v.reciprocal(RK, KS))
                fence(v, v.tensor_tensor(TAU, TM, RK, op=ALU.mult))
                fence(v, v.tensor_scalar(WW, LG, TAU, 0.0, op0=ALU.subtract,
                                         op1=ALU.max))
                v.wait_ge(sm["s_pmm"], 2 * u + 2)
                if u >= 1:
                    v.wait_ge(sm["s_hwT"], 16 * u)
                for e in range(8):
                    op = v.tensor_scalar(hwtm[:, e * 16:(e + 1) * 16],
                                         pb[1][:, e * 16:(e + 1) * 16],
                                         WW[:, e:e + 1], None, op0=ALU.mult)
                    if e == 7:
                        op.then_inc(sm["s_hw"], 1)
            v.nop().then_inc(sm["s_ph"], 1)

        @block.scalar
        def _(s):
            s.wait_ge(sm["s_ph"], 10)
            for u in range(NU):
                s.wait_ge(sm["s_spx"], u + 1)
                s.activation(spd[:, 0:8], CO, AF.Copy, accum_out=KS)
                s.activation(spd[:, 8:16], ZC, AF.Copy, accum_out=TS
                             ).then_inc(sm["s_ksum"], 1)
            s.nop().then_inc(sm["s_ph"], 1)

        @block.sync
        def _(sy):
            sy.wait_ge(sm["s_ph"], 10)
            for u in range(NU):
                sy.wait_ge(sm["s_hw"], u + 1)
                sy.dma_start_transpose(hwT[:, u * 128:(u + 1) * 128],
                                       hwtm[:, 0:128]).then_inc(sm["s_hwT"], 16)
            sy.wait_ge(sm["s_hwT"], 16 * NU)
            sy.nop().then_inc(sm["s_ph"], 1)

        @block.gpsimd
        def _(g):
            g.wait_ge(sm["s_ph"], 10)
            g.nop().then_inc(sm["s_ph"], 1)

        @block.gpsimd
        def _(g):
            g.wait_ge(sm["s_ph"], 15)
            for c in range(4):
                src = [wk, wk, wv, wv][c]
                oc = c % 2
                if c >= 1:
                    g.wait_ge(sm["s_kvmm"], c * NT)
                g.dma_start(out=wbuf[:, :, :], in_=src[oc]).then_inc(sm["s_wld"], 16)
            g.nop().then_inc(sm["s_ph"], 1)

        @block.tensor
        def _(t):
            t.wait_ge(sm["s_ph"], 15)
            for c in range(4):
                t.wait_ge(sm["s_wld"], 16 * (c + 1))
                for tt in range(NT):
                    it = c * NT + tt
                    if it >= 2:
                        t.wait_ge(sm["s_kstg"], it - 1)
                    for k in range(NK):
                        mmop = t.matmul(pb[it % 2][:, :],
                                        xnT[:, k, tt * 128:(tt + 1) * 128],
                                        wbuf[:, k, :], start=(k == 0),
                                        stop=(k == NK - 1))
                        if k == NK - 1:
                            mmop.then_inc(sm["s_kvmm"], 1)
            t.nop().then_inc(sm["s_ph"], 1)

        @block.vector
        def _(v):
            v.wait_ge(sm["s_ph"], 15)
            for c in range(4):
                isk = c < 2
                for tt in range(NT):
                    it = c * NT + tt
                    v.wait_ge(sm["s_kvmm"], it + 1)
                    if it >= 2:
                        if it - 2 < 24:
                            v.wait_ge(skvtr[it % 2], 64 * ((it - 2) // 2 + 1))
                        else:
                            v.wait_ge(sm["s_kvsp"], 16 * (it - 1))
                    if isk:
                        psv, pss, cv, sv, ov, rA, rB = ropeviews(
                            pb[it % 2], tt, kvstg[it % 2])
                        v.tensor_tensor(rA, psv, cv, op=ALU.mult)
                        fence(v, v.tensor_tensor(rB, pss, sv, op=ALU.mult))
                        v.tensor_tensor(ov, rA, rB, op=ALU.add
                                        ).then_inc(sm["s_kstg"], 1)
                    else:
                        v.tensor_copy(kvstg[it % 2][:, :], pb[it % 2][:, :]
                                      ).then_inc(sm["s_kstg"], 1)
            v.nop().then_inc(sm["s_ph"], 1)

        @block.scalar
        def _(s):
            s.wait_ge(sm["s_ph"], 15)
            for c in range(2):
                for tt in range(NT):
                    it = c * NT + tt
                    s.wait_ge(sm["s_kstg"], it + 1)
                    if it >= 2:
                        s.wait_ge(sm["s_kvsp"], 16 * (it - 1))
                    for j in range(4):
                        s.dma_start_transpose(trstg[it % 2][:, j * 128:(j + 1) * 128],
                                              kvstg[it % 2][:, j * 128:(j + 1) * 128]
                                              ).then_inc(skvtr[it % 2], 16)
            s.nop().then_inc(sm["s_ph"], 1)

        @block.sync
        def _(sy):
            sy.wait_ge(sm["s_ph"], 15)
            for c in range(4):
                isk = c < 2
                oc = c % 2
                for tt in range(NT):
                    it = c * NT + tt
                    if isk:
                        sy.wait_ge(skvtr[it % 2], 64 * (it // 2 + 1))
                        sy.dma_start(
                            out=kts[4 * oc:4 * oc + 4, tt, :, :].transpose([1, 0, 2]),
                            in_=trstg[it % 2].rearrange("p (j q) -> p j q", j=4)
                        ).then_inc(sm["s_kvsp"], 16)
                    else:
                        sy.wait_ge(sm["s_kstg"], it + 1)
                        sy.dma_start(
                            out=vts[4 * oc:4 * oc + 4, tt, :, :].transpose([1, 0, 2]),
                            in_=kvstg[it % 2].rearrange("p (j q) -> p j q", j=4)
                        ).then_inc(sm["s_kvsp"], 16)
            sy.wait_ge(sm["s_kvsp"], 16 * 48)
            sy.nop().then_inc(sm["s_ph"], 1)

        @block.gpsimd
        def _(g):
            g.wait_ge(sm["s_ph"], 20)
            for oc in range(8):
                if oc >= 2:
                    g.wait_ge(sm["s_kvmm"], 48 + (oc - 1) * NU)
                dst = wbuf[:, :, :] if oc % 2 == 0 else wq2_full
                g.dma_start(out=dst, in_=wq[oc]).then_inc(
                    sm["s_wld"] if oc % 2 == 0 else sm["s_wld1"], 16)
            g.nop().then_inc(sm["s_ph"], 1)

        @block.tensor
        def _(t):
            t.wait_ge(sm["s_ph"], 20)
            for oc in range(8):
                if oc % 2 == 0:
                    t.wait_ge(sm["s_wld"], 16 * (4 + oc // 2 + 1))
                else:
                    t.wait_ge(sm["s_wld1"], 16 * (oc // 2 + 1))
                for u in range(NU):
                    tt = u + 4
                    it = oc * NU + u
                    if it >= 2:
                        t.wait_ge(sm["s_kstg"], 48 + it - 1)
                    for k in range(NK):
                        rhsw = wbuf[:, k, :] if oc % 2 == 0 else wq2(k)
                        mmop = t.matmul(pb[it % 2][:, :],
                                        xnT[:, k, tt * 128:(tt + 1) * 128],
                                        rhsw, start=(k == 0),
                                        stop=(k == NK - 1))
                        if k == NK - 1:
                            mmop.then_inc(sm["s_kvmm"], 1)
            t.nop().then_inc(sm["s_ph"], 1)

        @block.vector
        def _(v):
            v.wait_ge(sm["s_ph"], 20)
            for oc in range(8):
                for u in range(NU):
                    tt = u + 4
                    it = oc * NU + u
                    v.wait_ge(sm["s_kvmm"], 48 + it + 1)
                    if it >= 2:
                        v.wait_ge(skvtr[it % 2], 768 + 64 * ((it - 2) // 2 + 1))
                    psv, pss, cv, sv, ov, rA, rB = ropeviews(
                        pb[it % 2], tt, kvstg[it % 2])
                    v.tensor_tensor(rA, psv, cv, op=ALU.mult)
                    fence(v, v.tensor_tensor(rB, pss, sv, op=ALU.mult))
                    v.tensor_tensor(ov, rA, rB, op=ALU.add).then_inc(sm["s_kstg"], 1)
            v.nop().then_inc(sm["s_ph"], 1)

        @block.scalar
        def _(s):
            s.wait_ge(sm["s_ph"], 20)
            for oc in range(8):
                for u in range(NU):
                    it = oc * NU + u
                    s.wait_ge(sm["s_kstg"], 48 + it + 1)
                    if it >= 2:
                        s.wait_ge(sm["s_kvsp"], 16 * (48 + it - 1))
                    for j in range(4):
                        s.dma_start_transpose(trstg[it % 2][:, j * 128:(j + 1) * 128],
                                              kvstg[it % 2][:, j * 128:(j + 1) * 128]
                                              ).then_inc(skvtr[it % 2], 16)
            s.nop().then_inc(sm["s_ph"], 1)

        @block.sync
        def _(sy):
            sy.wait_ge(sm["s_ph"], 20)
            for oc in range(8):
                for u in range(NU):
                    it = oc * NU + u
                    sy.wait_ge(skvtr[it % 2], 768 + 64 * (it // 2 + 1))
                    sy.dma_start(
                        out=qts[4 * oc:4 * oc + 4, u, :, :].transpose([1, 0, 2]),
                        in_=trstg[it % 2].rearrange("p (j q) -> p j q", j=4)
                    ).then_inc(sm["s_kvsp"], 16)
            sy.wait_ge(sm["s_kvsp"], 16 * (48 + 64))
            for g4 in range(4):
                sy.dma_start(out=xns[:, g4 * 8:(g4 + 1) * 8, :],
                             in_=xnT[:, g4 * 8:(g4 + 1) * 8, 512:NT * 128]
                             ).then_inc(sm["s_xnsp"], 16)
            sy.wait_ge(sm["s_xnsp"], 64)
            sy.nop().then_inc(sm["s_ph"], 1)

        @block.gpsimd
        def _(g):
            g.wait_ge(sm["s_ph"], 25)
            g.dma_start(out=wbuf[:, 0:8, :], in_=mlo).then_inc(sm["s_msl"], 16)
            g.nop().then_inc(sm["s_ph"], 1)

        @block.sync
        def _(sy):
            sy.wait_ge(sm["s_ph"], 25)
            for h in range(NQH):
                if h >= 2:
                    sy.wait_ge(sm["s_sc"], ((h - 2) * 2 + 2) * 8)
                sy.dma_start(out=qh[h % 2].rearrange("p (u q) -> p u q", u=8),
                             in_=qts[h].transpose([1, 0, 2])
                             ).then_inc(sqld[h % 2], 16)
                if h % 4 == 0:
                    if h >= 4:
                        sy.wait_ge(sm["s_av"], 2 * h)
                    sy.dma_start(out=kh.rearrange("p (tt q) -> p tt q", tt=NT),
                                 in_=kts[h // 4].transpose([1, 0, 2])
                                 ).then_inc(sm["s_kvld"], 16)
                    sy.dma_start(out=vh.rearrange("p (tt q) -> p tt q", tt=NT),
                                 in_=vts[h // 4].transpose([1, 0, 2])
                                 ).then_inc(sm["s_kvld"], 16)
                if h >= 1:
                    sy.wait_ge(sm["s_avn"], 2 * h)
                    sy.dma_start(out=avs[h - 1][:, 0:512], in_=kvstg0[:, :]
                                 ).then_inc(sm["s_avsp"], 16)
                    sy.dma_start(out=avs[h - 1][:, 512:1024], in_=kvstg1[:, :]
                                 ).then_inc(sm["s_avsp"], 16)
            sy.wait_ge(sm["s_avn"], 2 * NQH)
            sy.dma_start(out=avs[NQH - 1][:, 0:512], in_=kvstg0[:, :]
                         ).then_inc(sm["s_avsp"], 16)
            sy.dma_start(out=avs[NQH - 1][:, 512:1024], in_=kvstg1[:, :]
                         ).then_inc(sm["s_avsp"], 16)
            sy.wait_ge(sm["s_avsp"], 32 * NQH)
            sy.nop().then_inc(sm["s_ph"], 1)

        @block.tensor
        def _(t):
            t.wait_ge(sm["s_ph"], 25)
            for h in range(NQH):
                t.wait_ge(sqld[h % 2], 16 * (h // 2 + 1))
                t.wait_ge(sm["s_kvld"], 32 * (h // 4 + 1))
                for qc in range(2):
                    it = h * 2 + qc
                    RORD = [4, 5, 6, 7, 0, 1, 2, 3]
                    for idx, r in enumerate(RORD):
                        kb = 4 * qc + r
                        lo, hi = _clip(r)
                        if it * 8 + idx >= 2:
                            t.wait_ge(sm["s_exp"], it * 8 + idx - 1)
                        t.matmul(pb[idx % 2][:, lo:hi],
                                 kh[:, kb * 128:(kb + 1) * 128],
                                 qh[h % 2][:, qc * 512 + lo:qc * 512 + hi],
                                 start=True, stop=True).then_inc(sm["s_sc"], 1)
                    pba = pb[2 + 2 * (it % 2)]
                    pbd = pb[3 + 2 * (it % 2)]
                    if it >= 2:
                        t.wait_ge(sm["s_avn"], it - 1)
                    t.wait_ge(sm["s_msk"], it * 8 + 4)
                    for idx, r in enumerate([4, 5, 6, 7]):
                        kb = 4 * qc + r
                        lo, hi = _clip(r)
                        t.matmul(pba[:, lo:hi], vh[:, kb * 128:(kb + 1) * 128],
                                 exp_sb[:, r, lo:hi], start=(idx == 0), stop=False)
                    for idx, r in enumerate([4, 5, 6, 7]):
                        lo, hi = _clip(r)
                        mmop = t.matmul(pbd[:, lo:hi], onesb[:, :],
                                        exp_sb[:, r, lo:hi], start=(idx == 0),
                                        stop=False)
                        if idx == 3:
                            mmop.then_inc(sm["s_avB"], 1)
                    t.wait_ge(sm["s_msk"], (it + 1) * 8)
                    for r in [0, 1, 2, 3]:
                        kb = 4 * qc + r
                        lo, hi = _clip(r)
                        t.matmul(pba[:, lo:hi], vh[:, kb * 128:(kb + 1) * 128],
                                 exp_sb[:, r, lo:hi], start=False, stop=(r == 3))
                    for r in [0, 1, 2, 3]:
                        lo, hi = _clip(r)
                        mmop = t.matmul(pbd[:, lo:hi], onesb[:, :],
                                        exp_sb[:, r, lo:hi], start=False,
                                        stop=(r == 3))
                        if r == 3:
                            mmop.then_inc(sm["s_av"], 1)
            t.nop().then_inc(sm["s_ph"], 1)

        @block.scalar
        def _(s):
            s.wait_ge(sm["s_ph"], 25)
            for h in range(NQH):
                for qc in range(2):
                    it = h * 2 + qc
                    RORD = [4, 5, 6, 7, 0, 1, 2, 3]
                    for idx, r in enumerate(RORD):
                        lo, hi = _clip(r)
                        s.wait_ge(sm["s_sc"], it * 8 + idx + 1)
                        if it >= 1 and idx == 0:
                            s.wait_ge(sm["s_avB"], it)
                        if it >= 1 and idx == 4:
                            s.wait_ge(sm["s_av"], it)
                        s.activation(exp_sb[:, r, lo:hi], pb[idx % 2][:, lo:hi],
                                     AF.Exp, bias=nb24[:, 0:1], scale=SCALE
                                     ).then_inc(sm["s_exp"], 1)
            s.nop().then_inc(sm["s_ph"], 1)

        @block.vector
        def _(v):
            v.wait_ge(sm["s_ph"], 25)
            v.wait_ge(sm["s_msl"], 16)
            for h in range(NQH):
                for qc in range(2):
                    it = h * 2 + qc
                    RORD = [4, 5, 6, 7, 0, 1, 2, 3]
                    for idx, r in enumerate(RORD):
                        lo, hi = _clip(r)
                        v.wait_ge(sm["s_exp"], it * 8 + idx + 1)
                        if r < 4:
                            v.tensor_tensor(exp_sb[:, r, lo:hi],
                                            exp_sb[:, r, lo:hi],
                                            wbuf[:, 4 * qc + r, lo:hi],
                                            op=ALU.mult).then_inc(sm["s_msk"], 1)
                        else:
                            d0 = 128 * (r - 4)
                            v.tensor_tensor(exp_sb[:, r, d0:d0 + 128],
                                            exp_sb[:, r, d0:d0 + 128],
                                            trisb[:, :], op=ALU.mult
                                            ).then_inc(sm["s_msk"], 1)
                    v.wait_ge(sm["s_av"], it + 1)
                    if qc == 0 and h >= 1:
                        v.wait_ge(sm["s_avsp"], 32 * h)
                    fence(v, v.reciprocal(den_r[:, :], pb[3 + 2 * (it % 2)][:, :]))
                    v.tensor_tensor(avh[qc][:, :], pb[2 + 2 * (it % 2)][:, :],
                                    den_r[:, :],
                                    op=ALU.mult).then_inc(sm["s_avn"], 1)
            v.nop().then_inc(sm["s_ph"], 1)

        @block.gpsimd
        def _(g):
            g.wait_ge(sm["s_ph"], 30)
            for oc in range(8):
                if oc >= 2:
                    g.wait_ge(sm["s_smm"], (oc - 1) * NU)
                dst = wbuf[:, :, :] if oc % 2 == 0 else wz2_full
                g.dma_start(out=dst, in_=wo[oc]).then_inc(
                    sm["s_wld"] if oc % 2 == 0 else sm["s_wld1"], 16)
            g.nop().then_inc(sm["s_ph"], 1)

        @block.sync
        def _(sy):
            sy.wait_ge(sm["s_ph"], 30)
            for h in range(NQH):
                sy.dma_start(out=xnT[:, h, 0:1024], in_=avs[h]
                             ).then_inc(sm["s_avld"], 16)
            sy.wait_ge(sm["s_ssp"], 16 * 64)
            sy.nop().then_inc(sm["s_ph"], 1)

        @block.tensor
        def _(t):
            t.wait_ge(sm["s_ph"], 30)
            t.wait_ge(sm["s_avld"], 16 * NQH)
            for oc in range(8):
                if oc % 2 == 0:
                    t.wait_ge(sm["s_wld"], 16 * (8 + oc // 2 + 1))
                else:
                    t.wait_ge(sm["s_wld1"], 16 * (4 + oc // 2 + 1))
                for u in range(NU):
                    it = oc * NU + u
                    if it >= 4:
                        t.wait_ge(sm["s_scp"], it - 3)
                    for k in range(NK):
                        rhsw = wbuf[:, k, :] if oc % 2 == 0 else wz2(k)
                        mmop = t.matmul(pb[it % 4][:, :], xnT[:, k, u * 128:(u + 1) * 128],
                                        rhsw, start=(k == 0),
                                        stop=(k == NK - 1))
                        if k == NK - 1:
                            mmop.then_inc(sm["s_smm"], 1)
            t.nop().then_inc(sm["s_ph"], 1)

        @block.scalar
        def _(s):
            s.wait_ge(sm["s_ph"], 30)
            for oc in range(8):
                for u in range(NU):
                    it = oc * NU + u
                    s.wait_ge(sm["s_smm"], it + 1)
                    if it >= 2:
                        s.wait_ge(sm["s_ssp"], 16 * (it - 1))
                    s.activation(sstg[it % 2][:, :], pb[it % 4][:, :], AF.Copy
                                 ).then_inc(sm["s_scp"], 1)
                    s.wait_ge(sm["s_scp"], it + 1)
                    s.dma_start(out=ss[u][:, oc * 512:(oc + 1) * 512],
                                in_=sstg[it % 2][:, :]).then_inc(sm["s_ssp"], 16)
            s.nop().then_inc(sm["s_ph"], 1)

        @block.vector
        def _(v):
            v.wait_ge(sm["s_ph"], 30)
            v.nop().then_inc(sm["s_ph"], 1)

        @block.gpsimd
        def _(g):
            g.wait_ge(sm["s_ph"], 35)
            for oc in range(8):
                if oc >= 2:
                    g.wait_ge(sm["s_fmm"], (oc - 1) * NU)
                dst = wbuf[:, :, :] if oc % 2 == 0 else wz2_full
                g.dma_start(out=dst, in_=wb[oc]).then_inc(
                    sm["s_wld"] if oc % 2 == 0 else sm["s_wld1"], 16)
                if oc >= 1:
                    g.wait_ge(sm["s_fmm"], oc * NU)
                g.dma_start(out=lbs[oc % 2][:, :], in_=lb[:, oc * 512:(oc + 1) * 512]
                            ).then_inc(sm["s_lbl"], 16)
            g.nop().then_inc(sm["s_ph"], 1)

        @block.sync
        def _(sy):
            sy.wait_ge(sm["s_ph"], 35)
            for g4 in range(4):
                sy.dma_start(out=xnT[:, g4 * 8:(g4 + 1) * 8, 0:1024],
                             in_=xns[:, g4 * 8:(g4 + 1) * 8, :]
                             ).then_inc(sm["s_avld"], 16)
            for oc in range(8):
                if oc >= 1:
                    sy.wait_ge(sm["s_fb"], oc * NU)
                sy.dma_start(out=bbs[:, :], in_=bb[:, oc * 512:(oc + 1) * 512]
                             ).then_inc(sm["s_bbl"], 16)
                for u in range(NU):
                    it = oc * NU + u
                    sy.wait_ge(sm["s_fo"], it + 1)
                    sy.dma_start(out=os_[u * 128:(u + 1) * 128,
                                         oc * 512:(oc + 1) * 512],
                                 in_=ostg[it % 2][:, :]).then_inc(sm["s_ost"], 16)
            sy.wait_ge(sm["s_ost"], 16 * 64)

        @block.tensor
        def _(t):
            t.wait_ge(sm["s_ph"], 35)
            t.wait_ge(sm["s_avld"], 16 * (NQH + 4))
            for oc in range(8):
                if oc % 2 == 0:
                    t.wait_ge(sm["s_wld"], 16 * (12 + oc // 2 + 1))
                else:
                    t.wait_ge(sm["s_wld1"], 16 * (8 + oc // 2 + 1))
                t.wait_ge(sm["s_lbl"], 16 * (oc + 1))
                for u in range(NU):
                    it = oc * NU + u
                    if it >= 4:
                        t.wait_ge(sm["s_scp"], 64 + it - 3)
                    for k in range(NK):
                        rhsw = wbuf[:, k, :] if oc % 2 == 0 else wz2(k)
                        t.matmul(pb[it % 4][:, :], xnT[:, k, u * 128:(u + 1) * 128],
                                 rhsw, start=(k == 0), stop=False)
                    t.matmul(pb[it % 4][:, :], hwT[:, u * 128:(u + 1) * 128],
                             lbs[oc % 2][:, :], start=False, stop=True
                             ).then_inc(sm["s_fmm"], 1)
            t.nop().then_inc(sm["s_ph"], 1)

        @block.vector
        def _(v):
            v.wait_ge(sm["s_ph"], 35)
            for oc in range(8):
                for u in range(NU):
                    it = oc * NU + u
                    v.wait_ge(sm["s_fmm"], it + 1)
                    v.wait_ge(sm["s_bbl"], 16 * (oc + 1))
                    v.tensor_tensor(pb[it % 4][:, :], pb[it % 4][:, :], bbs[:, :],
                                    op=ALU.add).then_inc(sm["s_fb"], 1)
                    v.wait_ge(sm["s_scp"], 64 + it + 1)
                    v.wait_ge(ssld[it % 2], 16 * (it // 2 + 1))
                    v.wait_ge(sxcl[it % 2], 16 * (it // 2 + 1))
                    if it >= 2:
                        v.wait_ge(sm["s_ost"], 16 * (it - 1))
                    fence(v, v.tensor_tensor(ostg[it % 2][:, :], sstg[it % 2][:, :],
                                              gstg[it % 2][:, :], op=ALU.mult))
                    v.tensor_tensor(ostg[it % 2][:, :], ostg[it % 2][:, :],
                                    xck[it % 2][:, :], op=ALU.add
                                    ).then_inc(sm["s_fo"], 1)
                    v.nop().then_inc(sm["s_fg"], 1)

        @block.scalar
        def _(s):
            s.wait_ge(sm["s_ph"], 35)
            for oc in range(8):
                for u in range(NU):
                    it = oc * NU + u
                    if it >= 2:
                        s.wait_ge(sm["s_fg"], it - 1)
                    s.dma_start(out=sstg[it % 2][:, :],
                                in_=ss[u][:, oc * 512:(oc + 1) * 512]
                                ).then_inc(ssld[it % 2], 16)
                    s.dma_start(out=xck[it % 2][:, :],
                                in_=xs[(u + 4) * 128:(u + 5) * 128,
                                       oc * 512:(oc + 1) * 512]
                                ).then_inc(sxcl[it % 2], 16)
                    s.wait_ge(sm["s_fb"], it + 1)
                    s.activation(gstg[it % 2][:, :], pb[it % 4][:, :], AF.Silu
                                 ).then_inc(sm["s_scp"], 1)

    return nc


def _tables(h_half):
    theta = 10000.0
    inv = 1.0 / theta ** (np.arange(0, 128, 2, dtype=np.float64) / 128.0)
    loc = np.arange(NT * 128)
    if h_half == 0:
        pos = np.where(loc < HALO, 0, loc - HALO).astype(np.float64)
    else:
        pos = (loc + HALO).astype(np.float64)
    ang = pos[:, None] * inv[None, :]
    cos64 = np.cos(ang)
    sin64 = np.sin(ang)
    cosT = np.concatenate([cos64, cos64], axis=1)
    sinA = np.concatenate([-sin64, sin64], axis=1)
    out = []
    for tab in (cosT, sinA):
        hi = tab.astype(np.float32)
        lo = (tab - hi.astype(np.float64)).astype(np.float32)
        out.append(np.ascontiguousarray(
            hi.reshape(NT, 128, 128).transpose(1, 0, 2).reshape(128, NT * 128)))
        out.append(np.ascontiguousarray(
            lo.reshape(NT, 128, 128).transpose(1, 0, 2).reshape(128, NT * 128)))
    return out


def _masks(h_half):
    m = np.zeros((128, 8, 512), dtype=np.float32)
    kc = np.arange(128)[:, None]
    qc = np.arange(512)[None, :]
    for qch in range(2):
        for r in range(4):
            if h_half == 0 and qch == 0:
                continue
            lo, hi = _clip(r)
            allow = (qc <= r * 128 + kc) & (qc < hi)
            m[:, qch * 4 + r, :] = allow.astype(np.float32)
    return m.astype(ml_dtypes.bfloat16)


def _w_tiles(w, noc):
    return np.ascontiguousarray(
        w.reshape(NK, 128, noc, 512).transpose(2, 1, 0, 3))


def kernel(L0_attn_hidden_states, L0_attn_Wq, L0_attn_Wk, L0_attn_Wv, L0_attn_Wo,
           L0_ffn_W_base, L0_ffn_b_base, L0_ffn_W_router, L0_ffn_b_router,
           L0_ffn_lora_A, L0_ffn_lora_B):
    x = np.ascontiguousarray(L0_attn_hidden_states, dtype=np.float32)
    wq_h = _w_tiles(np.ascontiguousarray(L0_attn_Wq, dtype=np.float32), 8)
    wk_h = _w_tiles(np.ascontiguousarray(L0_attn_Wk, dtype=np.float32), 2)
    wv_h = _w_tiles(np.ascontiguousarray(L0_attn_Wv, dtype=np.float32), 2)
    wo_h = _w_tiles(np.ascontiguousarray(L0_attn_Wo, dtype=np.float32), 8)
    wb_h = _w_tiles(np.ascontiguousarray(L0_ffn_W_base.T, dtype=np.float32), 8)
    wr_h = np.ascontiguousarray(
        np.asarray(L0_ffn_W_router, dtype=np.float32).T.reshape(NK, 128, 8)
        .transpose(1, 0, 2))
    la_flat = np.asarray(L0_ffn_lora_A, dtype=np.float32).transpose(1, 0, 2).reshape(H, 128)
    la_h = np.ascontiguousarray(la_flat.reshape(NK, 128, 128).transpose(1, 0, 2))
    lb_h = np.ascontiguousarray(np.asarray(L0_ffn_lora_B, dtype=np.float32).reshape(128, H))
    bb_h = np.ascontiguousarray(
        np.broadcast_to(np.asarray(L0_ffn_b_base, dtype=np.float32), (128, H)))
    br_h = np.ascontiguousarray(
        np.broadcast_to(np.asarray(L0_ffn_b_router, dtype=np.float32), (128, 8)))
    tri_h = (np.arange(128)[:, None] <= np.arange(128)[None, :]
             ).astype(ml_dtypes.bfloat16)
    one_h = np.ones((128, 128), dtype=ml_dtypes.bfloat16)
    idn_h = np.eye(128, dtype=ml_dtypes.bfloat16)

    if "nc" not in _NC_CACHE:
        _NC_CACHE["nc"] = build_program()
    nc = _NC_CACHE["nc"]

    tabs = [_tables(0), _tables(1)]
    msks = [_masks(0), _masks(1)]
    in_maps = []
    for c in range(N_CORES):
        b, hh = c // 2, c % 2
        if hh == 0:
            xsl = np.concatenate([np.zeros((HALO, H), np.float32), x[b, 0:1024]],
                                 axis=0)
        else:
            xsl = np.ascontiguousarray(x[b, HALO:2048])
        chi_h, clo_h, shi_h, slo_h = tabs[hh]
        in_maps.append({
            "xs": xsl, "wq": wq_h, "wk": wk_h, "wv": wv_h, "wo": wo_h,
            "wb": wb_h, "wr": wr_h, "la": la_h, "lb": lb_h, "bb": bb_h,
            "br": br_h, "chi": chi_h, "clo": clo_h, "shi": shi_h, "slo": slo_h,
            "mlo": msks[hh], "tri": tri_h, "one": one_h, "idn": idn_h,
        })
    res = run_bass_kernel_spmd(nc, in_maps, core_ids=list(range(N_CORES)))
    out = np.empty((B, S, H), dtype=np.float32)
    for c in range(N_CORES):
        b, hh = c // 2, c % 2
        out[b, hh * 1024:(hh + 1) * 1024] = res.results[c]["os_"]
    return out


# version 11
# speedup vs baseline: 1.4542x; 1.0157x over previous
import contextlib
import math

import numpy as np
import ml_dtypes

import concourse.bass as bass
import concourse.mybir as mybir
from concourse.bass_utils import run_bass_kernel_spmd

F32 = mybir.dt.float32
BF16 = mybir.dt.bfloat16
AF = mybir.ActivationFunctionType
ALU = mybir.AluOpType
AX = mybir.AxisListType

H = 4096
S = 2048
B = 4
NT = 12
NU = 8
HALO = 512
NQH = 32
NKV = 8
NK = 32
SCALE = 1.0 / math.sqrt(128.0)
ESHIFT = -24.0
N_CORES = 8

_NC_CACHE = {}


def _clip(r):
    if r < 4:
        return 0, 128 * (r + 1)
    return 128 * (r - 4), 512


def build_program():
    nc = bass.Bass("TRN2", target_bir_lowering=False)

    xs = nc.dram_tensor("xs", [NT * 128, H], F32, kind="ExternalInput").ap()
    wq = nc.dram_tensor("wq", [8, 128, NK, 512], F32, kind="ExternalInput").ap()
    wk = nc.dram_tensor("wk", [2, 128, NK, 512], F32, kind="ExternalInput").ap()
    wv = nc.dram_tensor("wv", [2, 128, NK, 512], F32, kind="ExternalInput").ap()
    wo = nc.dram_tensor("wo", [8, 128, NK, 512], F32, kind="ExternalInput").ap()
    wb = nc.dram_tensor("wb", [8, 128, NK, 512], F32, kind="ExternalInput").ap()
    wr = nc.dram_tensor("wr", [128, NK, 8], F32, kind="ExternalInput").ap()
    la = nc.dram_tensor("la", [128, NK, 128], F32, kind="ExternalInput").ap()
    lb = nc.dram_tensor("lb", [128, H], F32, kind="ExternalInput").ap()
    bb = nc.dram_tensor("bb", [128, H], F32, kind="ExternalInput").ap()
    br = nc.dram_tensor("br", [128, 8], F32, kind="ExternalInput").ap()
    chi = nc.dram_tensor("chi", [128, NT * 128], F32, kind="ExternalInput").ap()
    clo = nc.dram_tensor("clo", [128, NT * 128], F32, kind="ExternalInput").ap()
    shi = nc.dram_tensor("shi", [128, NT * 128], F32, kind="ExternalInput").ap()
    slo = nc.dram_tensor("slo", [128, NT * 128], F32, kind="ExternalInput").ap()
    mlo = nc.dram_tensor("mlo", [128, 8, 512], BF16, kind="ExternalInput").ap()
    tri = nc.dram_tensor("tri", [128, 128], BF16, kind="ExternalInput").ap()
    one = nc.dram_tensor("one", [128, 128], BF16, kind="ExternalInput").ap()
    idn = nc.dram_tensor("idn", [128, 128], BF16, kind="ExternalInput").ap()
    os_ = nc.dram_tensor("os_", [NU * 128, H], F32, kind="ExternalOutput").ap()
    kts = nc.dram_tensor("kts", [NKV, NT, 128, 128], BF16, kind="Internal").ap()
    vts = nc.dram_tensor("vts", [NKV, NT, 128, 128], BF16, kind="Internal").ap()
    qts = nc.dram_tensor("qts", [NQH, NU, 128, 128], BF16, kind="Internal").ap()
    avs = nc.dram_tensor("avs", [NQH, 128, 1024], BF16, kind="Internal").ap()
    xns = nc.dram_tensor("xns", [128, NK, 1024], BF16, kind="Internal").ap()
    ss = nc.dram_tensor("ss", [NU, 128, H], F32, kind="Internal").ap()

    st = contextlib.ExitStack()
    with st:
        sb = lambda *a, **k: st.enter_context(nc.sbuf_tensor(*a, **k))
        xnT = sb("xnT", [128, NK, NT * 128], BF16, is_transpose_dest=True)
        wbuf = sb("wbuf", [128, NK, 512], BF16)
        xbuf0 = sb("xbuf0", [128, H // 2], F32)
        xbuf1 = sb("xbuf1", [128, H // 2], F32)
        xtm0 = sb("xtm0", [128, H], BF16)
        xtm1 = sb("xtm1", [128, H], BF16)
        sqsc = sb("sqsc", [128, NT * 128], BF16)
        cosv = sb("cosv", [128, NT * 128], F32)
        sinv = sb("sinv", [128, NT * 128], F32)
        stats = sb("stats", [128, NT, 8], F32)
        spx = sb("spx", [128, 128], F32)
        spd = sb("spd", [128, 16], F32)
        hwT = sb("hwT", [128, NU * 128], BF16, is_transpose_dest=True)
        hwtm = sb("hwtm", [128, 128], BF16)
        wrsb = sb("wrsb", [128, NK, 8], BF16)
        brsb = sb("brsb", [128, 8], F32)
        trisb = sb("trisb", [128, 128], BF16)
        onesb = sb("onesb", [128, 128], BF16)
        nb24 = sb("nb24", [128, 1], F32)
        idnsb = sb("idnsb", [128, 128], BF16)
        kvstg0 = sb("kvstg0", [128, 512], BF16)
        kvstg1 = sb("kvstg1", [128, 512], BF16)
        trstg0 = sb("trstg0", [128, 512], BF16, is_transpose_dest=True)
        trstg1 = sb("trstg1", [128, 512], BF16, is_transpose_dest=True)
        sstg0 = sb("sstg0", [128, 512], F32)
        sstg1 = sb("sstg1", [128, 512], F32)
        gstg0 = sb("gstg0", [128, 512], F32)
        gstg1 = sb("gstg1", [128, 512], F32)
        ostg0 = sb("ostg0", [128, 512], F32)
        ostg1 = sb("ostg1", [128, 512], F32)
        xck0 = sb("xck0", [128, 512], F32)
        xck1 = sb("xck1", [128, 512], F32)
        bbs = sb("bbs", [128, 512], F32)
        lbs0 = sb("lbs0", [128, 512], BF16)

        exp_sb = xtm0.rearrange("p (r q) -> p r q", r=8)
        qh = [xtm1[:, 0:1024], xtm1[:, 1024:2048]]
        kh = xtm1[:, 2048:2048 + NT * 128]
        vh = sqsc
        avh = [kvstg0, kvstg1]
        den_r = gstg0
        rtA, rtB = sstg0, sstg1

        pb = [st.enter_context(nc.psum_tensor(f"pb{i}", [128, 512], F32))
              for i in range(6)]
        pt = [st.enter_context(nc.psum_tensor(f"pt{i}", [128, 128], BF16))
              for i in range(2)]

        semnames = [
            "s_ph", "s_xld", "s_xld1", "s_stat", "s_rec", "s_rsq", "s_apl",
            "s_trA0", "s_trA1", "s_sml", "s_sm2", "s_tbl", "s_pmm", "s_spx",
            "s_ksum", "s_hw", "s_hwT", "s_wld", "s_wld1", "s_kvmm", "s_kstg",
            "s_kvtr0", "s_kvtr1", "s_kvsp", "s_xnsp", "s_qld0", "s_qld1",
            "s_kvld", "s_sc", "s_exp", "s_msk", "s_av", "s_avn", "s_avsp",
            "s_avld", "s_smm", "s_scp", "s_ssp", "s_sld0", "s_sld1",
            "s_xcl0", "s_xcl1", "s_fmm", "s_fb", "s_fo", "s_fg", "s_ost",
            "s_lbl", "s_bbl", "s_msl", "s_dvf", "s_ttr", "s_tcp", "s_avB",
        ]
        sm = {n: st.enter_context(nc.semaphore(n)) for n in semnames}
        block = st.enter_context(nc.Block())

        xtm = [xtm0, xtm1]
        xbuf = [xbuf0, xbuf1]
        sxl = [sm["s_xld"], sm["s_xld1"]]
        strA = [sm["s_trA0"], sm["s_trA1"]]
        kvstg = [kvstg0, kvstg1]
        trstg = [trstg0, trstg1]
        skvtr = [sm["s_kvtr0"], sm["s_kvtr1"]]
        sqld = [sm["s_qld0"], sm["s_qld1"]]
        sstg = [sstg0, sstg1]
        gstg = [gstg0, gstg1]
        ostg = [ostg0, ostg1]
        xck = [xck0, xck1]
        lbs = [lbs0, lbs0]
        ssld = [sm["s_sld0"], sm["s_sld1"]]
        sxcl = [sm["s_xcl0"], sm["s_xcl1"]]

        dvf = [0]
        wq2 = lambda k: xnT[:, k, 0:512]
        wz2 = lambda k: xnT[:, k, 1024:1536]
        wq2_full = xnT[:, :, 0:512]
        wz2_full = xnT[:, :, 1024:1536]

        def fence(v, instop):
            dvf[0] += 1
            instop.then_inc(sm["s_dvf"], 1)
            v.wait_ge(sm["s_dvf"], dvf[0])

        def ropeviews(ps, tt, outbuf):
            psv = ps.rearrange("p (h u j) -> p h u j", h=4, u=2)
            pss = bass.AP(ps, 64, [[512, 128], [128, 4], [-64, 2], [1, 64]])
            cv = bass.AP(cosv, tt * 128,
                         [[NT * 128, 128], [0, 4], [64, 2], [1, 64]])
            sv = bass.AP(sinv, tt * 128,
                         [[NT * 128, 128], [0, 4], [64, 2], [1, 64]])
            ov = outbuf.rearrange("p (h u j) -> p h u j", h=4, u=2)
            rA = rtA.rearrange("p (h u j) -> p h u j", h=4, u=2)
            rB = rtB.rearrange("p (h u j) -> p h u j", h=4, u=2)
            return psv, pss, cv, sv, ov, rA, rB

        @block.sync
        def _(sy):
            sy.dma_start(out=brsb[:, :], in_=br).then_inc(sm["s_sml"], 16)
            sy.dma_start(out=trisb[:, :], in_=tri).then_inc(sm["s_sml"], 16)
            sy.dma_start(out=onesb[:, :], in_=one).then_inc(sm["s_sml"], 16)
            sy.dma_start(out=idnsb[:, :], in_=idn).then_inc(sm["s_sml"], 16)
            sy.dma_start(out=cosv[:, :], in_=chi).then_inc(sm["s_tbl"], 16)
            sy.dma_start(out=sinv[:, :], in_=shi).then_inc(sm["s_tbl"], 16)
            sy.wait_ge(sm["s_sml"], 64)
            sy.nop().then_inc(sm["s_ph"], 1)

        @block.gpsimd
        def _(g):
            g.dma_start(out=wrsb[:, :, :], in_=wr).then_inc(sm["s_sm2"], 16)
            g.wait_ge(sm["s_tbl"], 32)
            g.dma_start(out=cosv[:, :], in_=clo, accum_op=ALU.add
                        ).then_inc(sm["s_sm2"], 16)
            g.dma_start(out=sinv[:, :], in_=slo, accum_op=ALU.add
                        ).then_inc(sm["s_sm2"], 16)
            g.wait_ge(sm["s_sm2"], 48)
            g.nop().then_inc(sm["s_ph"], 1)

        @block.vector
        def _(v):
            v.memset(nb24[:, :], ESHIFT)
            v.nop().then_inc(sm["s_ph"], 1)

        @block.scalar
        def _(s):
            s.nop().then_inc(sm["s_ph"], 1)

        @block.tensor
        def _(t):
            t.nop().then_inc(sm["s_ph"], 1)

        @block.sync
        def _(sy):
            sy.wait_ge(sm["s_ph"], 5)
            HH = H // 2
            for t in range(NT):
                gates = [("apl", 2 * t - 1), ("apl", 2 * t),
                         ("stat", 2 * t + 1), ("stat", 2 * t + 2)]
                for j, (nm, val) in enumerate(gates):
                    if val > 0:
                        sy.wait_ge(sm["s_apl" if nm == "apl" else "s_stat"], val)
                    half = j % 2
                    sy.dma_start(out=xbuf[j % 2][:, :],
                                 in_=xs[t * 128:(t + 1) * 128,
                                        half * HH:(half + 1) * HH]
                                 ).then_inc(sxl[j % 2], 16)
            sy.nop().then_inc(sm["s_ph"], 1)

        @block.scalar
        def _(s):
            s.wait_ge(sm["s_ph"], 5)
            for t in range(NT):
                pa = t % 2
                for half in range(2):
                    s.wait_ge(sxl[half], 16 * (2 * t + 1))
                    s.activation(xbuf[half][:, :], xbuf[half][:, :], AF.Copy,
                                 accum_out=stats[:, t, 0 + half:1 + half])
                    s.activation(xbuf[half][:, :], xbuf[half][:, :], AF.Square,
                                 accum_out=stats[:, t, 2 + half:3 + half]
                                 ).then_inc(sm["s_stat"], 1)
                s.wait_ge(sm["s_rec"], t + 1)
                s.activation(stats[:, t, 6:7], stats[:, t, 5:6], AF.Sqrt
                             ).then_inc(sm["s_rsq"], 1)
            s.nop().then_inc(sm["s_ph"], 1)

        @block.vector
        def _(v):
            v.wait_ge(sm["s_ph"], 5)
            HH = H // 2
            for t in range(NT):
                pa = t % 2
                v.wait_ge(sm["s_stat"], 2 * t + 2)
                fence(v, v.tensor_tensor(stats[:, t, 4:5], stats[:, t, 0:1],
                                          stats[:, t, 1:2], op=ALU.add))
                fence(v, v.tensor_tensor(stats[:, t, 5:6], stats[:, t, 2:3],
                                         stats[:, t, 3:4], op=ALU.add))
                fence(v, v.tensor_scalar(stats[:, t, 4:5], stats[:, t, 4:5],
                                         1.0 / H, None, op0=ALU.mult))
                fence(v, v.tensor_scalar(stats[:, t, 5:6], stats[:, t, 5:6],
                                         1.0 / H, None, op0=ALU.mult))
                fence(v, v.tensor_tensor(stats[:, t, 7:8], stats[:, t, 4:5],
                                         stats[:, t, 4:5], op=ALU.mult))
                fence(v, v.tensor_tensor(stats[:, t, 5:6], stats[:, t, 5:6],
                                         stats[:, t, 7:8], op=ALU.subtract))
                fence(v, v.tensor_scalar(stats[:, t, 5:6], stats[:, t, 5:6],
                                         1e-5, None, op0=ALU.add))
                v.reciprocal(stats[:, t, 5:6], stats[:, t, 5:6]
                             ).then_inc(sm["s_rec"], 1)
                v.wait_ge(sm["s_rsq"], t + 1)
                if t >= 2:
                    v.wait_ge(sm["s_ttr"], NK * (t - 1))
                for half in range(2):
                    v.wait_ge(sxl[half], 16 * (2 * t + 2))
                    v.tensor_scalar(xtm[pa][:, half * HH:(half + 1) * HH],
                                    xbuf[half][:, :], stats[:, t, 4:5],
                                    stats[:, t, 6:7], op0=ALU.subtract,
                                    op1=ALU.mult).then_inc(sm["s_apl"], 1)
                for k in range(NK):
                    gidx = t * NK + k
                    v.wait_ge(sm["s_ttr"], gidx + 1)
                    v.tensor_copy(xnT[:, k, t * 128:(t + 1) * 128],
                                  pt[k % 2][:, :]).then_inc(sm["s_tcp"], 1)
            v.nop().then_inc(sm["s_ph"], 1)

        @block.tensor
        def _(t):
            t.wait_ge(sm["s_ph"], 5)
            t.wait_ge(sm["s_sml"], 64)
            for tt in range(NT):
                pa = tt % 2
                t.wait_ge(sm["s_apl"], 2 * tt + 2)
                for k in range(NK):
                    gidx = tt * NK + k
                    if gidx >= 2:
                        t.wait_ge(sm["s_tcp"], gidx - 1)
                    t.transpose(pt[k % 2][:, :],
                                xtm[pa][:, k * 128:(k + 1) * 128], idnsb[:, :]
                                ).then_inc(sm["s_ttr"], 1)
            t.nop().then_inc(sm["s_ph"], 1)

        @block.gpsimd
        def _(g):
            g.wait_ge(sm["s_ph"], 5)
            g.dma_start(out=wbuf[:, :, 0:128], in_=la).then_inc(sm["s_sm2"], 16)
            g.nop().then_inc(sm["s_ph"], 1)

        CMP = spx[:, 0:64].rearrange("p (i j) -> p i j", i=8)
        LG = spx[:, 64:72]
        KI = spx[:, 72:80]
        SI = spx[:, 80:88]
        KZ = spx[:, 88:96]
        CO = spx[:, 96:104]
        ZC = spx[:, 104:112]
        KS = spx[:, 112:113]
        TS = spx[:, 113:114]
        TM = spx[:, 114:115]
        RK = spx[:, 115:116]
        TAU = spx[:, 116:117]
        WW = spx[:, 120:128]

        @block.tensor
        def _(t):
            t.wait_ge(sm["s_ph"], 10)
            t.wait_ge(sm["s_sm2"], 64)
            for u in range(NU):
                tt = u + 4
                if u >= 1:
                    t.wait_ge(sm["s_spx"], u)
                    t.wait_ge(sm["s_hw"], u)
                for k in range(NK):
                    t.matmul(pb[0][:, 0:8], xnT[:, k, tt * 128:(tt + 1) * 128],
                             wrsb[:, k, :], start=(k == 0), stop=(k == NK - 1)
                             ).then_inc(sm["s_pmm"], 1) if k == NK - 1 else \
                        t.matmul(pb[0][:, 0:8], xnT[:, k, tt * 128:(tt + 1) * 128],
                                 wrsb[:, k, :], start=(k == 0), stop=False)
                for k in range(NK):
                    t.matmul(pb[1][:, 0:128], xnT[:, k, tt * 128:(tt + 1) * 128],
                             wbuf[:, k, 0:128], start=(k == 0), stop=(k == NK - 1)
                             ).then_inc(sm["s_pmm"], 1) if k == NK - 1 else \
                        t.matmul(pb[1][:, 0:128], xnT[:, k, tt * 128:(tt + 1) * 128],
                                 wbuf[:, k, 0:128], start=(k == 0), stop=False)
            t.nop().then_inc(sm["s_ph"], 1)

        @block.vector
        def _(v):
            v.wait_ge(sm["s_ph"], 10)
            for u in range(NU):
                v.wait_ge(sm["s_pmm"], 2 * u + 1)
                fence(v, v.tensor_tensor(LG, pb[0][:, 0:8], brsb[:, :], op=ALU.add))
                for i in range(8):
                    v.tensor_scalar(CMP[:, i, :], LG, LG[:, i:i + 1], None,
                                    op0=ALU.is_ge)
                fence(v, v.tensor_scalar(CMP[:, 7, :], LG, LG[:, 7:8], None,
                                         op0=ALU.is_ge))
                fence(v, v.tensor_reduce(KI, CMP, axis=AX.X, op=ALU.add))
                fence(v, v.tensor_tensor(CMP, CMP,
                                         bass.AP(spx, 64, [[128, 128], [0, 8], [1, 8]]),
                                         op=ALU.mult))
                fence(v, v.tensor_reduce(SI, CMP, axis=AX.X, op=ALU.add))
                fence(v, v.tensor_tensor(KZ, KI, LG, op=ALU.mult))
                fence(v, v.tensor_scalar(KZ, KZ, 1.0, None, op0=ALU.add))
                fence(v, v.tensor_tensor(CO, KZ, SI, op=ALU.is_gt))
                fence(v, v.tensor_tensor(ZC, LG, CO, op=ALU.mult))
                v.nop().then_inc(sm["s_spx"], 1)
                v.wait_ge(sm["s_ksum"], u + 1)
                fence(v, v.tensor_scalar(TM, TS, 1.0, None, op0=ALU.subtract))
                fence(v, v.reciprocal(RK, KS))
                fence(v, v.tensor_tensor(TAU, TM, RK, op=ALU.mult))
                fence(v, v.tensor_scalar(WW, LG, TAU, 0.0, op0=ALU.subtract,
                                         op1=ALU.max))
                v.wait_ge(sm["s_pmm"], 2 * u + 2)
                if u >= 1:
                    v.wait_ge(sm["s_hwT"], 16 * u)
                for e in range(8):
                    op = v.tensor_scalar(hwtm[:, e * 16:(e + 1) * 16],
                                         pb[1][:, e * 16:(e + 1) * 16],
                                         WW[:, e:e + 1], None, op0=ALU.mult)
                    if e == 7:
                        op.then_inc(sm["s_hw"], 1)
            v.nop().then_inc(sm["s_ph"], 1)

        @block.scalar
        def _(s):
            s.wait_ge(sm["s_ph"], 10)
            for u in range(NU):
                s.wait_ge(sm["s_spx"], u + 1)
                s.activation(spd[:, 0:8], CO, AF.Copy, accum_out=KS)
                s.activation(spd[:, 8:16], ZC, AF.Copy, accum_out=TS
                             ).then_inc(sm["s_ksum"], 1)
            s.nop().then_inc(sm["s_ph"], 1)

        @block.sync
        def _(sy):
            sy.wait_ge(sm["s_ph"], 10)
            for u in range(NU):
                sy.wait_ge(sm["s_hw"], u + 1)
                sy.dma_start_transpose(hwT[:, u * 128:(u + 1) * 128],
                                       hwtm[:, 0:128]).then_inc(sm["s_hwT"], 16)
            sy.wait_ge(sm["s_hwT"], 16 * NU)
            sy.nop().then_inc(sm["s_ph"], 1)

        @block.gpsimd
        def _(g):
            g.wait_ge(sm["s_ph"], 10)
            g.nop().then_inc(sm["s_ph"], 1)

        @block.gpsimd
        def _(g):
            g.wait_ge(sm["s_ph"], 15)
            for c in range(4):
                src = [wk, wk, wv, wv][c]
                oc = c % 2
                if c >= 1:
                    g.wait_ge(sm["s_kvmm"], c * NT)
                g.dma_start(out=wbuf[:, :, :], in_=src[oc]).then_inc(sm["s_wld"], 16)
            g.nop().then_inc(sm["s_ph"], 1)

        @block.tensor
        def _(t):
            t.wait_ge(sm["s_ph"], 15)
            for c in range(4):
                t.wait_ge(sm["s_wld"], 16 * (c + 1))
                for tt in range(NT):
                    it = c * NT + tt
                    if it >= 2:
                        t.wait_ge(sm["s_kstg"], it - 1)
                    for k in range(NK):
                        mmop = t.matmul(pb[it % 2][:, :],
                                        xnT[:, k, tt * 128:(tt + 1) * 128],
                                        wbuf[:, k, :], start=(k == 0),
                                        stop=(k == NK - 1))
                        if k == NK - 1:
                            mmop.then_inc(sm["s_kvmm"], 1)
            t.nop().then_inc(sm["s_ph"], 1)

        @block.vector
        def _(v):
            v.wait_ge(sm["s_ph"], 15)
            for c in range(4):
                isk = c < 2
                for tt in range(NT):
                    it = c * NT + tt
                    v.wait_ge(sm["s_kvmm"], it + 1)
                    if it >= 2:
                        if it - 2 < 24:
                            v.wait_ge(skvtr[it % 2], 64 * ((it - 2) // 2 + 1))
                        else:
                            v.wait_ge(sm["s_kvsp"], 16 * (it - 1))
                    if isk:
                        psv, pss, cv, sv, ov, rA, rB = ropeviews(
                            pb[it % 2], tt, kvstg[it % 2])
                        v.tensor_tensor(rA, psv, cv, op=ALU.mult)
                        fence(v, v.tensor_tensor(rB, pss, sv, op=ALU.mult))
                        v.tensor_tensor(ov, rA, rB, op=ALU.add
                                        ).then_inc(sm["s_kstg"], 1)
                    else:
                        v.tensor_copy(kvstg[it % 2][:, :], pb[it % 2][:, :]
                                      ).then_inc(sm["s_kstg"], 1)
            v.nop().then_inc(sm["s_ph"], 1)

        @block.scalar
        def _(s):
            s.wait_ge(sm["s_ph"], 15)
            for c in range(2):
                for tt in range(NT):
                    it = c * NT + tt
                    s.wait_ge(sm["s_kstg"], it + 1)
                    if it >= 2:
                        s.wait_ge(sm["s_kvsp"], 16 * (it - 1))
                    for j in range(4):
                        s.dma_start_transpose(trstg[it % 2][:, j * 128:(j + 1) * 128],
                                              kvstg[it % 2][:, j * 128:(j + 1) * 128]
                                              ).then_inc(skvtr[it % 2], 16)
            s.nop().then_inc(sm["s_ph"], 1)

        @block.sync
        def _(sy):
            sy.wait_ge(sm["s_ph"], 15)
            for c in range(4):
                isk = c < 2
                oc = c % 2
                for tt in range(NT):
                    it = c * NT + tt
                    if isk:
                        sy.wait_ge(skvtr[it % 2], 64 * (it // 2 + 1))
                        sy.dma_start(
                            out=kts[4 * oc:4 * oc + 4, tt, :, :].transpose([1, 0, 2]),
                            in_=trstg[it % 2].rearrange("p (j q) -> p j q", j=4)
                        ).then_inc(sm["s_kvsp"], 16)
                    else:
                        sy.wait_ge(sm["s_kstg"], it + 1)
                        sy.dma_start(
                            out=vts[4 * oc:4 * oc + 4, tt, :, :].transpose([1, 0, 2]),
                            in_=kvstg[it % 2].rearrange("p (j q) -> p j q", j=4)
                        ).then_inc(sm["s_kvsp"], 16)
            sy.wait_ge(sm["s_kvsp"], 16 * 48)
            sy.nop().then_inc(sm["s_ph"], 1)

        @block.gpsimd
        def _(g):
            g.wait_ge(sm["s_ph"], 20)
            for oc in range(8):
                if oc >= 2:
                    g.wait_ge(sm["s_kvmm"], 48 + (oc - 1) * NU)
                dst = wbuf[:, :, :] if oc % 2 == 0 else wq2_full
                g.dma_start(out=dst, in_=wq[oc]).then_inc(
                    sm["s_wld"] if oc % 2 == 0 else sm["s_wld1"], 16)
            g.nop().then_inc(sm["s_ph"], 1)

        @block.tensor
        def _(t):
            t.wait_ge(sm["s_ph"], 20)
            for oc in range(8):
                if oc % 2 == 0:
                    t.wait_ge(sm["s_wld"], 16 * (4 + oc // 2 + 1))
                else:
                    t.wait_ge(sm["s_wld1"], 16 * (oc // 2 + 1))
                for u in range(NU):
                    tt = u + 4
                    it = oc * NU + u
                    if it >= 2:
                        t.wait_ge(sm["s_kstg"], 48 + it - 1)
                    for k in range(NK):
                        rhsw = wbuf[:, k, :] if oc % 2 == 0 else wq2(k)
                        mmop = t.matmul(pb[it % 2][:, :],
                                        xnT[:, k, tt * 128:(tt + 1) * 128],
                                        rhsw, start=(k == 0),
                                        stop=(k == NK - 1))
                        if k == NK - 1:
                            mmop.then_inc(sm["s_kvmm"], 1)
            t.nop().then_inc(sm["s_ph"], 1)

        @block.vector
        def _(v):
            v.wait_ge(sm["s_ph"], 20)
            for oc in range(8):
                for u in range(NU):
                    tt = u + 4
                    it = oc * NU + u
                    v.wait_ge(sm["s_kvmm"], 48 + it + 1)
                    if it >= 2:
                        v.wait_ge(skvtr[it % 2], 768 + 64 * ((it - 2) // 2 + 1))
                    psv, pss, cv, sv, ov, rA, rB = ropeviews(
                        pb[it % 2], tt, kvstg[it % 2])
                    v.tensor_tensor(rA, psv, cv, op=ALU.mult)
                    fence(v, v.tensor_tensor(rB, pss, sv, op=ALU.mult))
                    v.tensor_tensor(ov, rA, rB, op=ALU.add).then_inc(sm["s_kstg"], 1)
            v.nop().then_inc(sm["s_ph"], 1)

        @block.scalar
        def _(s):
            s.wait_ge(sm["s_ph"], 20)
            for oc in range(8):
                for u in range(NU):
                    it = oc * NU + u
                    s.wait_ge(sm["s_kstg"], 48 + it + 1)
                    if it >= 2:
                        s.wait_ge(sm["s_kvsp"], 16 * (48 + it - 1))
                    for j in range(4):
                        s.dma_start_transpose(trstg[it % 2][:, j * 128:(j + 1) * 128],
                                              kvstg[it % 2][:, j * 128:(j + 1) * 128]
                                              ).then_inc(skvtr[it % 2], 16)
            s.nop().then_inc(sm["s_ph"], 1)

        @block.sync
        def _(sy):
            sy.wait_ge(sm["s_ph"], 20)
            for oc in range(8):
                for u in range(NU):
                    it = oc * NU + u
                    sy.wait_ge(skvtr[it % 2], 768 + 64 * (it // 2 + 1))
                    sy.dma_start(
                        out=qts[4 * oc:4 * oc + 4, u, :, :].transpose([1, 0, 2]),
                        in_=trstg[it % 2].rearrange("p (j q) -> p j q", j=4)
                    ).then_inc(sm["s_kvsp"], 16)
            sy.wait_ge(sm["s_kvsp"], 16 * (48 + 64))
            for g4 in range(4):
                sy.dma_start(out=xns[:, g4 * 8:(g4 + 1) * 8, :],
                             in_=xnT[:, g4 * 8:(g4 + 1) * 8, 512:NT * 128]
                             ).then_inc(sm["s_xnsp"], 16)
            sy.wait_ge(sm["s_xnsp"], 64)
            sy.nop().then_inc(sm["s_ph"], 1)

        @block.gpsimd
        def _(g):
            g.wait_ge(sm["s_ph"], 25)
            g.dma_start(out=wbuf[:, 0:8, :], in_=mlo).then_inc(sm["s_msl"], 16)
            g.nop().then_inc(sm["s_ph"], 1)

        @block.sync
        def _(sy):
            sy.wait_ge(sm["s_ph"], 25)
            for h in range(NQH):
                if h >= 2:
                    sy.wait_ge(sm["s_sc"], ((h - 2) * 2 + 2) * 8)
                sy.dma_start(out=qh[h % 2].rearrange("p (u q) -> p u q", u=8),
                             in_=qts[h].transpose([1, 0, 2])
                             ).then_inc(sqld[h % 2], 16)
                if h % 4 == 0:
                    if h >= 4:
                        sy.wait_ge(sm["s_av"], 2 * h)
                    sy.dma_start(out=kh.rearrange("p (tt q) -> p tt q", tt=NT),
                                 in_=kts[h // 4].transpose([1, 0, 2])
                                 ).then_inc(sm["s_kvld"], 16)
                    sy.dma_start(out=vh.rearrange("p (tt q) -> p tt q", tt=NT),
                                 in_=vts[h // 4].transpose([1, 0, 2])
                                 ).then_inc(sm["s_kvld"], 16)
                if h >= 1:
                    sy.wait_ge(sm["s_avn"], 2 * h)
                    sy.dma_start(out=avs[h - 1][:, 0:512], in_=kvstg0[:, :]
                                 ).then_inc(sm["s_avsp"], 16)
                    sy.dma_start(out=avs[h - 1][:, 512:1024], in_=kvstg1[:, :]
                                 ).then_inc(sm["s_avsp"], 16)
            sy.wait_ge(sm["s_avn"], 2 * NQH)
            sy.dma_start(out=avs[NQH - 1][:, 0:512], in_=kvstg0[:, :]
                         ).then_inc(sm["s_avsp"], 16)
            sy.dma_start(out=avs[NQH - 1][:, 512:1024], in_=kvstg1[:, :]
                         ).then_inc(sm["s_avsp"], 16)
            sy.wait_ge(sm["s_avsp"], 32 * NQH)
            sy.nop().then_inc(sm["s_ph"], 1)

        @block.tensor
        def _(t):
            t.wait_ge(sm["s_ph"], 25)
            for h in range(NQH):
                t.wait_ge(sqld[h % 2], 16 * (h // 2 + 1))
                t.wait_ge(sm["s_kvld"], 32 * (h // 4 + 1))
                for qc in range(2):
                    it = h * 2 + qc
                    RORD = [4, 5, 6, 7, 0, 1, 2, 3]
                    for idx, r in enumerate(RORD):
                        kb = 4 * qc + r
                        lo, hi = _clip(r)
                        if it * 8 + idx >= 2:
                            t.wait_ge(sm["s_exp"], it * 8 + idx - 1)
                        t.matmul(pb[idx % 2][:, lo:hi],
                                 kh[:, kb * 128:(kb + 1) * 128],
                                 qh[h % 2][:, qc * 512 + lo:qc * 512 + hi],
                                 start=True, stop=True).then_inc(sm["s_sc"], 1)
                    pba = pb[2 + 2 * (it % 2)]
                    pbd = pb[3 + 2 * (it % 2)]
                    if it >= 2:
                        t.wait_ge(sm["s_avn"], it - 1)
                    t.wait_ge(sm["s_msk"], it * 2 + 1)
                    for idx, r in enumerate([4, 5, 6, 7]):
                        kb = 4 * qc + r
                        lo, hi = _clip(r)
                        t.matmul(pba[:, lo:hi], vh[:, kb * 128:(kb + 1) * 128],
                                 exp_sb[:, r, lo:hi], start=(idx == 0), stop=False)
                    for idx, r in enumerate([4, 5, 6, 7]):
                        lo, hi = _clip(r)
                        mmop = t.matmul(pbd[:, lo:hi], onesb[:, :],
                                        exp_sb[:, r, lo:hi], start=(idx == 0),
                                        stop=False)
                        if idx == 3:
                            mmop.then_inc(sm["s_avB"], 1)
                    t.wait_ge(sm["s_msk"], it * 2 + 2)
                    for r in [0, 1, 2, 3]:
                        kb = 4 * qc + r
                        lo, hi = _clip(r)
                        t.matmul(pba[:, lo:hi], vh[:, kb * 128:(kb + 1) * 128],
                                 exp_sb[:, r, lo:hi], start=False, stop=(r == 3))
                    for r in [0, 1, 2, 3]:
                        lo, hi = _clip(r)
                        mmop = t.matmul(pbd[:, lo:hi], onesb[:, :],
                                        exp_sb[:, r, lo:hi], start=False,
                                        stop=(r == 3))
                        if r == 3:
                            mmop.then_inc(sm["s_av"], 1)
            t.nop().then_inc(sm["s_ph"], 1)

        @block.scalar
        def _(s):
            s.wait_ge(sm["s_ph"], 25)
            for h in range(NQH):
                for qc in range(2):
                    it = h * 2 + qc
                    RORD = [4, 5, 6, 7, 0, 1, 2, 3]
                    for idx, r in enumerate(RORD):
                        lo, hi = _clip(r)
                        s.wait_ge(sm["s_sc"], it * 8 + idx + 1)
                        if it >= 1 and idx == 0:
                            s.wait_ge(sm["s_avB"], it)
                        if it >= 1 and idx == 4:
                            s.wait_ge(sm["s_av"], it)
                        s.activation(exp_sb[:, r, lo:hi], pb[idx % 2][:, lo:hi],
                                     AF.Exp, bias=nb24[:, 0:1], scale=SCALE
                                     ).then_inc(sm["s_exp"], 1)
            s.nop().then_inc(sm["s_ph"], 1)

        @block.vector
        def _(v):
            v.wait_ge(sm["s_ph"], 25)
            v.wait_ge(sm["s_msl"], 16)
            for h in range(NQH):
                for qc in range(2):
                    it = h * 2 + qc
                    v.wait_ge(sm["s_exp"], it * 8 + 4)
                    triv = bass.AP(xtm0, 4 * 512, [[4096, 128], [640, 4], [1, 128]])
                    trib = bass.AP(trisb, 0, [[128, 128], [0, 4], [1, 128]])
                    v.tensor_tensor(triv, triv, trib, op=ALU.mult
                                    ).then_inc(sm["s_msk"], 1)
                    v.wait_ge(sm["s_exp"], it * 8 + 8)
                    v.tensor_tensor(exp_sb[:, 0:4, :], exp_sb[:, 0:4, :],
                                    wbuf[:, 4 * qc:4 * qc + 4, :], op=ALU.mult
                                    ).then_inc(sm["s_msk"], 1)
                    v.wait_ge(sm["s_av"], it + 1)
                    if qc == 0 and h >= 1:
                        v.wait_ge(sm["s_avsp"], 32 * h)
                    fence(v, v.reciprocal(den_r[:, :], pb[3 + 2 * (it % 2)][:, :]))
                    v.tensor_tensor(avh[qc][:, :], pb[2 + 2 * (it % 2)][:, :],
                                    den_r[:, :],
                                    op=ALU.mult).then_inc(sm["s_avn"], 1)
            v.nop().then_inc(sm["s_ph"], 1)

        @block.gpsimd
        def _(g):
            g.wait_ge(sm["s_ph"], 30)
            for oc in range(8):
                if oc >= 2:
                    g.wait_ge(sm["s_smm"], (oc - 1) * NU)
                dst = wbuf[:, :, :] if oc % 2 == 0 else wz2_full
                g.dma_start(out=dst, in_=wo[oc]).then_inc(
                    sm["s_wld"] if oc % 2 == 0 else sm["s_wld1"], 16)
            g.nop().then_inc(sm["s_ph"], 1)

        @block.sync
        def _(sy):
            sy.wait_ge(sm["s_ph"], 30)
            for h in range(NQH):
                sy.dma_start(out=xnT[:, h, 0:1024], in_=avs[h]
                             ).then_inc(sm["s_avld"], 16)
            sy.wait_ge(sm["s_ssp"], 16 * 64)
            sy.nop().then_inc(sm["s_ph"], 1)

        @block.tensor
        def _(t):
            t.wait_ge(sm["s_ph"], 30)
            t.wait_ge(sm["s_avld"], 16 * NQH)
            for oc in range(8):
                if oc % 2 == 0:
                    t.wait_ge(sm["s_wld"], 16 * (8 + oc // 2 + 1))
                else:
                    t.wait_ge(sm["s_wld1"], 16 * (4 + oc // 2 + 1))
                for u in range(NU):
                    it = oc * NU + u
                    if it >= 4:
                        t.wait_ge(sm["s_scp"], it - 3)
                    for k in range(NK):
                        rhsw = wbuf[:, k, :] if oc % 2 == 0 else wz2(k)
                        mmop = t.matmul(pb[it % 4][:, :], xnT[:, k, u * 128:(u + 1) * 128],
                                        rhsw, start=(k == 0),
                                        stop=(k == NK - 1))
                        if k == NK - 1:
                            mmop.then_inc(sm["s_smm"], 1)
            t.nop().then_inc(sm["s_ph"], 1)

        @block.scalar
        def _(s):
            s.wait_ge(sm["s_ph"], 30)
            for oc in range(8):
                for u in range(NU):
                    it = oc * NU + u
                    s.wait_ge(sm["s_smm"], it + 1)
                    if it >= 2:
                        s.wait_ge(sm["s_ssp"], 16 * (it - 1))
                    s.activation(sstg[it % 2][:, :], pb[it % 4][:, :], AF.Copy
                                 ).then_inc(sm["s_scp"], 1)
                    s.wait_ge(sm["s_scp"], it + 1)
                    s.dma_start(out=ss[u][:, oc * 512:(oc + 1) * 512],
                                in_=sstg[it % 2][:, :]).then_inc(sm["s_ssp"], 16)
            s.nop().then_inc(sm["s_ph"], 1)

        @block.vector
        def _(v):
            v.wait_ge(sm["s_ph"], 30)
            v.nop().then_inc(sm["s_ph"], 1)

        @block.gpsimd
        def _(g):
            g.wait_ge(sm["s_ph"], 35)
            for oc in range(8):
                if oc >= 2:
                    g.wait_ge(sm["s_fmm"], (oc - 1) * NU)
                dst = wbuf[:, :, :] if oc % 2 == 0 else wz2_full
                g.dma_start(out=dst, in_=wb[oc]).then_inc(
                    sm["s_wld"] if oc % 2 == 0 else sm["s_wld1"], 16)
                if oc >= 1:
                    g.wait_ge(sm["s_fmm"], oc * NU)
                g.dma_start(out=lbs[oc % 2][:, :], in_=lb[:, oc * 512:(oc + 1) * 512]
                            ).then_inc(sm["s_lbl"], 16)
            g.nop().then_inc(sm["s_ph"], 1)

        @block.sync
        def _(sy):
            sy.wait_ge(sm["s_ph"], 35)
            for g4 in range(4):
                sy.dma_start(out=xnT[:, g4 * 8:(g4 + 1) * 8, 0:1024],
                             in_=xns[:, g4 * 8:(g4 + 1) * 8, :]
                             ).then_inc(sm["s_avld"], 16)
            for oc in range(8):
                if oc >= 1:
                    sy.wait_ge(sm["s_fb"], oc * NU)
                sy.dma_start(out=bbs[:, :], in_=bb[:, oc * 512:(oc + 1) * 512]
                             ).then_inc(sm["s_bbl"], 16)
                for u in range(NU):
                    it = oc * NU + u
                    sy.wait_ge(sm["s_fo"], it + 1)
                    sy.dma_start(out=os_[u * 128:(u + 1) * 128,
                                         oc * 512:(oc + 1) * 512],
                                 in_=ostg[it % 2][:, :]).then_inc(sm["s_ost"], 16)
            sy.wait_ge(sm["s_ost"], 16 * 64)

        @block.tensor
        def _(t):
            t.wait_ge(sm["s_ph"], 35)
            t.wait_ge(sm["s_avld"], 16 * (NQH + 4))
            for oc in range(8):
                if oc % 2 == 0:
                    t.wait_ge(sm["s_wld"], 16 * (12 + oc // 2 + 1))
                else:
                    t.wait_ge(sm["s_wld1"], 16 * (8 + oc // 2 + 1))
                t.wait_ge(sm["s_lbl"], 16 * (oc + 1))
                for u in range(NU):
                    it = oc * NU + u
                    if it >= 4:
                        t.wait_ge(sm["s_scp"], 64 + it - 3)
                    for k in range(NK):
                        rhsw = wbuf[:, k, :] if oc % 2 == 0 else wz2(k)
                        t.matmul(pb[it % 4][:, :], xnT[:, k, u * 128:(u + 1) * 128],
                                 rhsw, start=(k == 0), stop=False)
                    t.matmul(pb[it % 4][:, :], hwT[:, u * 128:(u + 1) * 128],
                             lbs[oc % 2][:, :], start=False, stop=True
                             ).then_inc(sm["s_fmm"], 1)
            t.nop().then_inc(sm["s_ph"], 1)

        @block.vector
        def _(v):
            v.wait_ge(sm["s_ph"], 35)
            for oc in range(8):
                for u in range(NU):
                    it = oc * NU + u
                    v.wait_ge(sm["s_fmm"], it + 1)
                    v.wait_ge(sm["s_bbl"], 16 * (oc + 1))
                    v.tensor_tensor(pb[it % 4][:, :], pb[it % 4][:, :], bbs[:, :],
                                    op=ALU.add).then_inc(sm["s_fb"], 1)
                    v.wait_ge(sm["s_scp"], 64 + it + 1)
                    v.wait_ge(ssld[it % 2], 16 * (it // 2 + 1))
                    v.wait_ge(sxcl[it % 2], 16 * (it // 2 + 1))
                    if it >= 2:
                        v.wait_ge(sm["s_ost"], 16 * (it - 1))
                    fence(v, v.tensor_tensor(ostg[it % 2][:, :], sstg[it % 2][:, :],
                                              gstg[it % 2][:, :], op=ALU.mult))
                    v.tensor_tensor(ostg[it % 2][:, :], ostg[it % 2][:, :],
                                    xck[it % 2][:, :], op=ALU.add
                                    ).then_inc(sm["s_fo"], 1)
                    v.nop().then_inc(sm["s_fg"], 1)

        @block.scalar
        def _(s):
            s.wait_ge(sm["s_ph"], 35)
            for oc in range(8):
                for u in range(NU):
                    it = oc * NU + u
                    if it >= 2:
                        s.wait_ge(sm["s_fg"], it - 1)
                    s.dma_start(out=sstg[it % 2][:, :],
                                in_=ss[u][:, oc * 512:(oc + 1) * 512]
                                ).then_inc(ssld[it % 2], 16)
                    s.dma_start(out=xck[it % 2][:, :],
                                in_=xs[(u + 4) * 128:(u + 5) * 128,
                                       oc * 512:(oc + 1) * 512]
                                ).then_inc(sxcl[it % 2], 16)
                    s.wait_ge(sm["s_fb"], it + 1)
                    s.activation(gstg[it % 2][:, :], pb[it % 4][:, :], AF.Silu
                                 ).then_inc(sm["s_scp"], 1)

    return nc


def _tables(h_half):
    theta = 10000.0
    inv = 1.0 / theta ** (np.arange(0, 128, 2, dtype=np.float64) / 128.0)
    loc = np.arange(NT * 128)
    if h_half == 0:
        pos = np.where(loc < HALO, 0, loc - HALO).astype(np.float64)
    else:
        pos = (loc + HALO).astype(np.float64)
    ang = pos[:, None] * inv[None, :]
    cos64 = np.cos(ang)
    sin64 = np.sin(ang)
    cosT = np.concatenate([cos64, cos64], axis=1)
    sinA = np.concatenate([-sin64, sin64], axis=1)
    out = []
    for tab in (cosT, sinA):
        hi = tab.astype(np.float32)
        lo = (tab - hi.astype(np.float64)).astype(np.float32)
        out.append(np.ascontiguousarray(
            hi.reshape(NT, 128, 128).transpose(1, 0, 2).reshape(128, NT * 128)))
        out.append(np.ascontiguousarray(
            lo.reshape(NT, 128, 128).transpose(1, 0, 2).reshape(128, NT * 128)))
    return out


def _masks(h_half):
    m = np.zeros((128, 8, 512), dtype=np.float32)
    kc = np.arange(128)[:, None]
    qc = np.arange(512)[None, :]
    for qch in range(2):
        for r in range(4):
            if h_half == 0 and qch == 0:
                continue
            lo, hi = _clip(r)
            allow = (qc <= r * 128 + kc) & (qc < hi)
            m[:, qch * 4 + r, :] = allow.astype(np.float32)
    return m.astype(ml_dtypes.bfloat16)


def _w_tiles(w, noc):
    return np.ascontiguousarray(
        w.reshape(NK, 128, noc, 512).transpose(2, 1, 0, 3))


def kernel(L0_attn_hidden_states, L0_attn_Wq, L0_attn_Wk, L0_attn_Wv, L0_attn_Wo,
           L0_ffn_W_base, L0_ffn_b_base, L0_ffn_W_router, L0_ffn_b_router,
           L0_ffn_lora_A, L0_ffn_lora_B):
    x = np.ascontiguousarray(L0_attn_hidden_states, dtype=np.float32)
    wq_h = _w_tiles(np.ascontiguousarray(L0_attn_Wq, dtype=np.float32), 8)
    wk_h = _w_tiles(np.ascontiguousarray(L0_attn_Wk, dtype=np.float32), 2)
    wv_h = _w_tiles(np.ascontiguousarray(L0_attn_Wv, dtype=np.float32), 2)
    wo_h = _w_tiles(np.ascontiguousarray(L0_attn_Wo, dtype=np.float32), 8)
    wb_h = _w_tiles(np.ascontiguousarray(L0_ffn_W_base.T, dtype=np.float32), 8)
    wr_h = np.ascontiguousarray(
        np.asarray(L0_ffn_W_router, dtype=np.float32).T.reshape(NK, 128, 8)
        .transpose(1, 0, 2))
    la_flat = np.asarray(L0_ffn_lora_A, dtype=np.float32).transpose(1, 0, 2).reshape(H, 128)
    la_h = np.ascontiguousarray(la_flat.reshape(NK, 128, 128).transpose(1, 0, 2))
    lb_h = np.ascontiguousarray(np.asarray(L0_ffn_lora_B, dtype=np.float32).reshape(128, H))
    bb_h = np.ascontiguousarray(
        np.broadcast_to(np.asarray(L0_ffn_b_base, dtype=np.float32), (128, H)))
    br_h = np.ascontiguousarray(
        np.broadcast_to(np.asarray(L0_ffn_b_router, dtype=np.float32), (128, 8)))
    tri_h = (np.arange(128)[:, None] <= np.arange(128)[None, :]
             ).astype(ml_dtypes.bfloat16)
    one_h = np.ones((128, 128), dtype=ml_dtypes.bfloat16)
    idn_h = np.eye(128, dtype=ml_dtypes.bfloat16)

    if "nc" not in _NC_CACHE:
        _NC_CACHE["nc"] = build_program()
    nc = _NC_CACHE["nc"]

    tabs = [_tables(0), _tables(1)]
    msks = [_masks(0), _masks(1)]
    in_maps = []
    for c in range(N_CORES):
        b, hh = c // 2, c % 2
        if hh == 0:
            xsl = np.concatenate([np.zeros((HALO, H), np.float32), x[b, 0:1024]],
                                 axis=0)
        else:
            xsl = np.ascontiguousarray(x[b, HALO:2048])
        chi_h, clo_h, shi_h, slo_h = tabs[hh]
        in_maps.append({
            "xs": xsl, "wq": wq_h, "wk": wk_h, "wv": wv_h, "wo": wo_h,
            "wb": wb_h, "wr": wr_h, "la": la_h, "lb": lb_h, "bb": bb_h,
            "br": br_h, "chi": chi_h, "clo": clo_h, "shi": shi_h, "slo": slo_h,
            "mlo": msks[hh], "tri": tri_h, "one": one_h, "idn": idn_h,
        })
    res = run_bass_kernel_spmd(nc, in_maps, core_ids=list(range(N_CORES)))
    out = np.empty((B, S, H), dtype=np.float32)
    for c in range(N_CORES):
        b, hh = c // 2, c % 2
        out[b, hh * 1024:(hh + 1) * 1024] = res.results[c]["os_"]
    return out
